# version 07
# baseline (speedup 1.0000x reference)
import math
import numpy as np
import ml_dtypes
from contextlib import ExitStack

import concourse.bass as bass
import concourse.mybir as mybir
from concourse.bass_utils import run_bass_kernel_spmd

F32 = mybir.dt.float32
BF16 = mybir.dt.bfloat16
AF = mybir.ActivationFunctionType
ALU = mybir.AluOpType
AX = mybir.AxisListType.X

CORES = 8
T, C = 8192, 2048
HD = 128
B0 = 14.0
B1 = 14.0
SCALE = 1.0 / math.sqrt(HD)
CLIP = 128.0 - 1e-6
BF = ml_dtypes.bfloat16
RG = [list(range(CORES))]


def tiles_of_core(c):
    return [8 * j + 7 - c for j in range(8)]



def _rope_tables(positions):
    inv = 1.0 / 10000.0 ** (np.arange(0, HD, 2, dtype=np.float64) / HD)
    fr = positions.astype(np.float64)[None, :] * np.concatenate([inv, inv])[:, None]
    cos = np.cos(fr)
    sin = np.sin(fr)
    sin[:64] = -sin[:64]
    return cos.astype(np.float32), sin.astype(np.float32)


def make_core_inputs(c, z):
    x = z["L0_attn_x"][0]
    cols = np.concatenate([np.arange(128 * g, 128 * g + 128)
                           for g in tiles_of_core(c)])
    d = {}
    d["x_in"] = np.ascontiguousarray(np.asarray(x, np.float32)[cols, :].T)
    cos, sin = _rope_tables(cols)
    d["rope_cosq"] = np.ascontiguousarray(cos * SCALE)
    d["rope_sinq"] = np.ascontiguousarray(sin * SCALE)
    d["rope_cosk"] = cos
    d["rope_sink"] = sin
    m = np.zeros((128, 8, 8, 128), np.float32)
    tri = (np.arange(128)[:, None] <= np.arange(128)[None, :]).astype(np.float32)
    for j in range(8):
        Dj = 8 * (j + 1)
        dj = Dj - c
        for idx in range(8):
            pp = Dj - 8 + idx
            if pp < dj - 1:
                m[:, j, idx, :] = 1.0
            elif pp == dj - 1:
                m[:, j, idx, :] = tri
    d["mask0"] = np.ascontiguousarray(m.reshape(128, 8 * 8 * 128)).astype(BF)
    psw = np.zeros((128, 128), np.float32)
    psw[(np.arange(128) + 64) % 128, np.arange(128)] = 1.0
    d["pswap"] = psw.astype(BF)
    cc = np.zeros((128, 8), np.float32)
    cc[:, 0] = -B0
    cc[:, 1] = -B1
    cc[:, 2] = 1e-5
    cc[:, 3] = 1e-11
    cc[:, 4] = 1e-6
    d["cconst"] = cc
    d["ones_bf"] = np.ones((128, 1), np.float32).astype(BF)
    d["ones_row"] = np.ones((1, 128), np.float32)
    r0, r1 = 256 * c, 256 * (c + 1)
    d["wq_s"] = np.ascontiguousarray(z["L0_attn_Wq"][r0:r1])
    d["wk_s"] = np.ascontiguousarray(z["L0_attn_Wk"][r0:r1])
    d["wv_s"] = np.ascontiguousarray(z["L0_attn_Wv"][r0:r1])
    d["wproj_s"] = np.ascontiguousarray(z["L0_attn_Wproj"][r0:r1])
    d["gup_s"] = np.ascontiguousarray(
        z["L0_ffn_gate_up_proj"][:, r0:r1].reshape(4 * 256, 4096))
    d["down_s"] = np.ascontiguousarray(
        z["L0_ffn_down_proj"][:, r0:r1].reshape(4 * 256, 2048))
    d["wqkv_s"] = np.ascontiguousarray(z["L1_attn_w_qkv"][:, r0:r1])
    d["wout_s"] = np.ascontiguousarray(z["L1_attn_w_out"][:, r0:r1])
    d["wfc_s"] = np.ascontiguousarray(z["L1_ffn_w_fc"][:, r0:r1])
    d["wproj2_s"] = np.ascontiguousarray(
        z["L1_ffn_w_proj"][:, 1024 * c:1024 * (c + 1)])
    return {k: np.asarray(v) for k, v in d.items()}


def assemble_output(xs):
    out = np.empty((1, T, C), np.float32)
    for c in range(CORES):
        for j, g in enumerate(tiles_of_core(c)):
            out[0, 128 * g:128 * g + 128, :] = xs[c][:, 128 * j:128 * j + 128].T
    return out



class P:
    def __init__(self, nc, es, mode):
        self.nc, self.es, self.mode = nc, es, mode
        self.cnt, self.sem, self.sb, self.dr, self.ccmark = {}, {}, {}, {}, {}

    def mksem(self, *names):
        for n in names:
            self.sem[n] = self.es.enter_context(self.nc.semaphore(n))
            self.cnt[n] = 0

    def inc(self, instr, name, n=1):
        instr.then_inc(self.sem[name], n)
        self.cnt[name] += n
        return self.cnt[name]

    def wait(self, eng, name, val=None):
        v = self.cnt[name] if val is None else val
        if v > 0:
            eng.wait_ge(self.sem[name], v)

    def sbuf(self, name, shape, dtype):
        h = self.es.enter_context(self.nc.sbuf_tensor("sb_" + name, shape, dtype))
        self.sb[name] = h
        return h

    def psum(self, name, shape, dtype=F32):
        h = self.es.enter_context(self.nc.psum_tensor(name, shape, dtype))
        self.sb[name] = h
        return h

    def dram(self, name, shape, dtype, kind=None):
        h = (self.nc.dram_tensor(name, shape, dtype, kind=kind) if kind
             else self.nc.dram_tensor(name, shape, dtype))
        self.dr[name] = h
        return h

    def joinall(self, eng,
                names=("dS", "dG", "dT", "tMM", "tS", "aE", "aX", "vM", "vX")):
        for n in names:
            self.wait(eng, n)


ALLSEMS = ["dS", "dSa", "dSb", "dG", "dT", "dTa", "dTb", "cc", "tMM", "tS", "aE", "aX", "vM", "vX", "st"]


def build_program(mode, taps=False):
    nc = bass.Bass("TRN2", target_bir_lowering=False)
    es = ExitStack()
    p = P(nc, es, mode)
    p.taps = taps
    with es:
        _declare_io(p)
        _declare_sbuf(p)
        p.mksem(*ALLSEMS)
        p.psum("PS_A", [128, 1024])
        p.psum("PS_B", [128, 1024])
        p.psum("PS_C", [128, 512])
        p.psum("PS_D", [128, 512])
        p.psum("PS_E", [128, 512])
        p.psum("PS_F", [128, 512])
        p.blk = es.enter_context(nc.Block())

        _emit_loads(p)
        _emit_prep1(p)
        if mode in ("l0a", "full"):
            _emit_l0_qkv(p)
            _emit_l0_attn(p)
            _emit_proj(p, layer=0)
        _emit_prep2a(p)
        if mode in ("l0f", "full"):
            _emit_l0_ffn(p)
        _emit_prep2b(p)
        if mode in ("l1a", "full"):
            _emit_l1_qkv(p)
        else:
            _emit_prep2b_tr(p)
        _emit_prep2c(p)
        if mode in ("l1a", "full"):
            _emit_l1_attn(p)
            _emit_proj(p, layer=1)
        if mode in ("l1f", "full"):
            _emit_l1_ffn(p)
        _emit_store(p)
    return nc


def _declare_io(p):
    p.dram("x_in", [C, 1024], F32, kind="ExternalInput")
    p.dram("x_out", [C, 1024], F32, kind="ExternalOutput")
    for nm, sh in [("rope_cosq", [128, 1024]), ("rope_sinq", [128, 1024]),
                   ("rope_cosk", [128, 1024]), ("rope_sink", [128, 1024]),
                   ("cconst", [128, 8]), ("ones_row", [1, 128])]:
        p.dram(nm, sh, F32, kind="ExternalInput")
    for nm, sh in [("mask0", [128, 8 * 8 * 128]), ("pswap", [128, 128]),
                   ("ones_bf", [128, 1])]:
        p.dram(nm, sh, BF16, kind="ExternalInput")
    for nm, sh in [("wq_s", [256, 2048]), ("wk_s", [256, 512]), ("wv_s", [256, 512]),
                   ("wproj_s", [256, 2048]), ("gup_s", [1024, 4096]),
                   ("down_s", [1024, 2048]), ("wqkv_s", [6144, 256]),
                   ("wout_s", [2048, 256]), ("wfc_s", [8192, 256]),
                   ("wproj2_s", [2048, 1024])]:
        p.dram(nm, sh, F32, kind="ExternalInput")
    for nm, csh, gsh in [
        ("wq", [256, 2048], [2048, 2048]), ("wk", [256, 512], [2048, 512]),
        ("wv", [256, 512], [2048, 512]), ("wproj", [256, 2048], [2048, 2048]),
        ("gup", [1024, 4096], [8192, 4096]), ("down", [1024, 2048], [8192, 2048]),
        ("wqkvT", [256, 6144], [2048, 6144]), ("woutT", [256, 2048], [2048, 2048]),
        ("wfcT", [256, 8192], [2048, 8192]), ("wprojT", [1024, 2048], [8192, 2048]),
        ("k0", [512, 1024], [4096, 1024]), ("v0", [1024, 512], [8192, 512]),
    ]:
        p.dram(nm + "_c", csh, BF16)
        p.dram(nm + "_g", gsh, BF16)
    for hg in range(4):
        p.dram(f"k1_c{hg}", [512, 1024], BF16)
        p.dram(f"k1_g{hg}", [4096, 1024], BF16)
        p.dram(f"v1_c{hg}", [1024, 512], BF16)
        p.dram(f"v1_g{hg}", [8192, 512], BF16)
    p.dram("wqkv_bf", [6144, 256], BF16)
    p.dram("wout_bf", [2048, 256], BF16)
    p.dram("wfc_bf", [8192, 256], BF16)
    p.dram("wproj2_bf", [2048, 1024], BF16)
    p.dram("gam_c", [128, 2], F32)
    p.dram("gam_g", [128, 2], F32)
    if p.taps == 2:
        p.dram("gam_dbg", [128, 2], F32, kind="ExternalOutput")
        p.dram("wfcT_dbg", [2048, 8192], BF16, kind="ExternalOutput")
        p.dram("wprojT_dbg", [8192, 2048], BF16, kind="ExternalOutput")
        p.dram("xhat_dbg", [1024, 2048], BF16, kind="ExternalOutput")
        p.dram("h0_dbg", [128, 8192], BF16, kind="ExternalOutput")
        p.dram("xh2_dbg", [256, 8192], BF16, kind="ExternalOutput")
        p.dram("xbf_dbg", [2048, 1024], BF16, kind="ExternalOutput")
        p.dram("sm_dbg", [128, 16], F32, kind="ExternalOutput")
        p.dram("xt0_dbg", [128, 2048], BF16, kind="ExternalOutput")
        p.dram("x2fm_dbg", [128, 16384], BF16, kind="ExternalOutput")
        p.dram("wth_dbg", [128, 4096], BF16, kind="ExternalOutput")
    if p.taps == 1:
        p.dram("xln_dbg", [128, 16384], BF16, kind="ExternalOutput")
        p.dram("q_dbg", [128, 16384], BF16, kind="ExternalOutput")
        p.dram("k0g_dbg", [4096, 1024], BF16, kind="ExternalOutput")
        p.dram("v0g_dbg", [8192, 512], BF16, kind="ExternalOutput")
        p.dram("ob_dbg", [128, 16384], BF16, kind="ExternalOutput")
    p.dram("xbf_d", [2048, 1024], BF16)
    p.dram("xhat_d", [1024, 2048], BF16)
    p.dram("xh2_d", [256, 8192], BF16)


def _declare_sbuf(p):
    p.sbuf("x", [128, 16, 1024], F32)
    p.sbuf("QB", [128, 16, 1024], BF16)
    p.sbuf("OB", [128, 16, 1024], BF16)
    p.sbuf("U1", [128, 16, 1024], BF16)
    p.sbuf("AR1", [128, 16384], BF16)
    p.sbuf("WT", [128, 2, 16, 128], BF16)
    p.sbuf("TMP", [128, 1024], F32)
    p.sbuf("statm", [128, 16], F32)
    p.sbuf("cconst", [128, 8], F32)
    p.sbuf("pswap", [128, 128], BF16)
    p.sbuf("ones_bf", [128, 1], BF16)
    p.sbuf("ones_row", [1, 128], F32)
    p.sbuf("grow", [1, 16], F32)



def V_ROPE(p):
    return p.sb["AR1"][:, 0:8192].bitcast(F32).rearrange("p (a n) -> p a n", n=1024)

def V_RMAT(p):
    return p.sb["AR1"][:, 8192:10240].bitcast(F32)

def V_RTMP(p):
    return p.sb["AR1"][:, 10240:12288].bitcast(F32)

def V_KS(p):
    return p.sb["AR1"][:, 0:8192].rearrange("p (t n) -> p t n", n=128)

def V_VS(p):
    return p.sb["AR1"][:, 8192:16384].rearrange("p (t n) -> p t n", n=128)

def V_MSK(p):
    return p.sb["U1"][:, 0:8, :].rearrange("p a n -> p (a n)").rearrange(
        "p (t n) -> p t n", n=128)

def V_PT(p, i):
    return p.sb["U1"][:, 8 + i, :]

def V_XLN(p):
    return p.sb["U1"]

def V_KC(p):
    return p.sb["OB"][:, 0:4, :]

def V_VC(p):
    return p.sb["OB"][:, 4:8, :].rearrange("p a n -> p (a n)").rearrange(
        "p (a n) -> p a n", n=512)

def V_QL0(p):
    return p.sb["QB"][:, :, :].rearrange("p a n -> p (a n)").rearrange(
        "p (k s h n) -> p k s h n", k=4, s=8, h=4)



def _emit_loads(p):
    blk = p.blk

    @blk.sync
    def _(sy):
        p.inc(sy.dma_start(
            out=p.sb["x"][:, :, :],
            in_=p.dr["x_in"].ap().rearrange("(a p) n -> p a n", p=128)), "dS", 16)
        for nm in ("cconst", "ones_row", "pswap", "ones_bf"):
            p.inc(sy.dma_start(out=p.sb[nm][:, :], in_=p.dr[nm].ap()), "dS", 16)
        rope = V_ROPE(p)
        if p.mode in ("l0a", "full"):
            for i, nm in enumerate(["rope_cosq", "rope_sinq",
                                    "rope_cosk", "rope_sink"]):
                p.inc(sy.dma_start(out=rope[:, i, :], in_=p.dr[nm].ap()), "dS", 16)
    p.c_xload = p.cnt["dS"]


def _emit_store(p):
    blk = p.blk

    @blk.sync
    def _(sy):
        p.joinall(sy)
        p.inc(sy.dma_start(
            out=p.dr["x_out"].ap().rearrange("(a p) n -> p a n", p=128),
            in_=p.sb["x"][:, :, :]), "st", 16)
        p.wait(sy, "st")



def _emit_prep1(p):
    blk, m = p.blk, p.mode
    names = []
    if m in ("l0a", "full"):
        names += ["wk", "wv", "wq", "wproj"]
    if m in ("l0f", "full"):
        names += ["gup", "down"]
    need_l1a = m in ("l1a", "full")

    early = [nm for nm in names if nm in ("wk", "wv", "wq")] \
        if p.mode == "full" else names
    p.deferred_ags = [nm for nm in names if nm not in early]

    @blk.gpsimd
    def _(g):
        for nm in names:
            i = p.inc(g.dma_start(out=p.dr[nm + "_c"].ap(),
                                  in_=p.dr[nm + "_s"].ap()), "dG", 16)
        g.wait_ge(p.sem["dG"], p.cnt["dG"])
        for nm in early:
            p.inc(g.collective_compute(
                "AllGather", ALU.bypass, replica_groups=RG,
                ins=[p.dr[nm + "_c"].ap()], outs=[p.dr[nm + "_g"].ap()]), "cc", 1)
            p.ccmark[nm] = p.cnt["cc"]
        if need_l1a:
            for src, bnc in [("wqkv_s", "wqkv_bf"), ("wout_s", "wout_bf")]:
                p.inc(g.dma_start(out=p.dr[bnc].ap(), in_=p.dr[src].ap()), "dG", 16)
    c_cast = p.cnt["dG"]

    if need_l1a:
        qbf = p.sb["QB"][:, :, :].rearrange("p a n -> p (a n)")

        @blk.scalar
        def _(s):
            s.wait_ge(p.sem["dG"], c_cast)
            for bnc, contrib, nrows in [("wqkv_bf", "wqkvT_c", 6144),
                                        ("wout_bf", "woutT_c", 2048)]:
                for ci in range(2):
                    p.inc(s.dma_start(out=qbf[:, 0:nrows],
                                      in_=p.dr[bnc].ap()[:, ci * 128:(ci + 1) * 128],
                                      transpose=True), "dT", 16)
                    s.wait_ge(p.sem["dT"], p.cnt["dT"])
                    p.inc(s.dma_start(
                        out=p.dr[contrib].ap()[ci * 128:(ci + 1) * 128, :],
                        in_=qbf[:, 0:nrows]), "dT", 16)
                    s.wait_ge(p.sem["dT"], p.cnt["dT"])

        if p.mode == "full":
            p.deferred_ags += ["wqkvT", "woutT"]
            p.deferred_dt = p.cnt["dT"]
        else:
            @blk.gpsimd
            def _(g):
                g.wait_ge(p.sem["dT"], p.cnt["dT"])
                for nm in ("wqkvT", "woutT"):
                    p.inc(g.collective_compute(
                        "AllGather", ALU.bypass, replica_groups=RG,
                        ins=[p.dr[nm + "_c"].ap()], outs=[p.dr[nm + "_g"].ap()]),
                        "cc", 1)
                    p.ccmark[nm] = p.cnt["cc"]
    p.c_prep1_dT = p.cnt["dT"]


def _emit_prep2a(p):
    if p.mode not in ("l1f", "full"):
        return
    blk = p.blk
    obf = p.sb["OB"][:, :, :].rearrange("p a n -> p (a n)")
    obq = obf[:, 0:8192].bitcast(F32)
    sm = p.sb["statm"]
    quarters = [("wfc_s", q, 2048, 256, 0) for q in range(4)] + \
               [("wproj2_s", q, 512, 1024, 1) for q in range(4)]
    for src_, q, rows, ncol, mi in quarters:
        fcols = rows * ncol // 128

        @blk.sync
        def _(sy, src_=src_, q=q, rows=rows, ncol=ncol, fcols=fcols, mi=mi):
            if mi == 0 and q == 0:
                p.joinall(sy, ("vX", "aE", "aX", "tS"))
            else:
                p.wait(sy, "vX")
            src_ap = p.dr[src_].ap()[q * rows:(q + 1) * rows, :]
            u3 = obq[:, 0:fcols].rearrange("p (a n) -> p a n", n=ncol)
            p.inc(sy.dma_start(out=u3,
                               in_=src_ap.rearrange("(a p) n -> p a n", p=128)),
                  "dS", 16)
        cL = p.cnt["dS"]

        @blk.vector
        def _(v, cL=cL, fcols=fcols, q=q, mi=mi):
            v.wait_ge(p.sem["dS"], cL)
            p.inc(v.reduce_sum(sm[:, q:q + 1], obq[:, 0:fcols], AX,
                               apply_absolute_value=True), "vX", 1)
            if q == 3:
                v.wait_ge(p.sem["vX"], p.cnt["vX"])
                p.inc(v.tensor_tensor(sm[:, 6:7], sm[:, 0:1], sm[:, 1:2],
                                      ALU.add), "vX", 1)
                v.wait_ge(p.sem["vX"], p.cnt["vX"])
                p.inc(v.tensor_tensor(sm[:, 7:8], sm[:, 2:3], sm[:, 3:4],
                                      ALU.add), "vX", 1)
                v.wait_ge(p.sem["vX"], p.cnt["vX"])
                p.inc(v.tensor_tensor(sm[:, 4 + mi:5 + mi], sm[:, 6:7],
                                      sm[:, 7:8], ALU.add), "vX", 1)
    @blk.vector
    def _(v):
        p.inc(v.memset(sm[:, 15:16], 1.0), "vX", 1)
    c_comb = p.cnt["vX"]

    @blk.gpsimd
    def _(g):
        g.wait_ge(p.sem["vX"], c_comb)
        i = p.inc(g.dma_start(out=p.dr["gam_c"].ap(), in_=sm[:, 4:6]),
                  "dG", 16)
        g.wait_ge(p.sem["dG"], i)
        p.inc(g.collective_compute(
            "AllReduce", ALU.add, replica_groups=RG,
            ins=[p.dr["gam_c"].ap()], outs=[p.dr["gam_g"].ap()]), "cc", 1)
        p.ccmark["gam"] = p.cnt["cc"]
    p.c_p2a = p.cnt["vX"]


def _emit_prep2b(p):
    if p.mode not in ("l1f", "full"):
        return
    blk = p.blk
    obf = p.sb["OB"][:, :, :].rearrange("p a n -> p (a n)")
    obq = obf[:, 0:8192].bitcast(F32)
    obo = obf[:, 8192:12288]
    sm = p.sb["statm"]

    @blk.sync
    def _(sy):
        p.wait(sy, "cc", p.ccmark["gam"])
        p.inc(sy.dma_start(out=sm[:, 12:14], in_=p.dr["gam_g"].ap()),
              "dG", 16)
    c_gl = p.cnt["dG"]
    c_vtail = p.cnt["vX"]

    @blk.tensor
    def _(t):
        t.wait_ge(p.sem["dG"], c_gl)
        t.wait_ge(p.sem["vX"], c_vtail)
        p.inc(t.matmul(p.sb["PS_F"][0:1, 0:2], sm[:, 15:16], sm[:, 12:14],
                       start=True, stop=True), "tMM", 1)
    c_gm = p.cnt["tMM"]

    @blk.vector
    def _(v):
        v.wait_ge(p.sem["tMM"], c_gm)
        p.inc(v.tensor_copy(p.sb["grow"][:, 4:6], p.sb["PS_F"][0:1, 0:2]),
              "vX", 1)
        v.wait_ge(p.sem["vX"], p.cnt["vX"])
        p.inc(v.tensor_scalar(p.sb["grow"][:, 8:10], p.sb["grow"][:, 4:6],
                              1.0 / (8192.0 * 2048.0), 1e-6, ALU.mult, ALU.add),
              "vX", 1)
        v.wait_ge(p.sem["vX"], p.cnt["vX"])
        v.tensor_scalar_mul(p.sb["grow"][:, 12:14], p.sb["grow"][:, 4:6],
                            1.0 / (8192.0 * 2048.0))
        p.inc(v.reciprocal(p.sb["grow"][:, 10:12], p.sb["grow"][:, 8:10]), "vX", 1)
    c_gv = p.cnt["vX"]

    @blk.tensor
    def _(t):
        t.wait_ge(p.sem["vX"], c_gv)
        p.inc(t.matmul(p.sb["PS_F"][:, 0:4], p.sb["ones_row"][:, :],
                       p.sb["grow"][0:1, 10:14], start=True, stop=True), "tMM", 1)
    c_gb = p.cnt["tMM"]

    @blk.vector
    def _(v):
        v.wait_ge(p.sem["tMM"], c_gb)
        p.inc(v.tensor_copy(sm[:, 8:12], p.sb["PS_F"][:, 0:4]), "vX", 1)
    p.c_gam = p.cnt["vX"]

    for src_, bnc, q, rows, ncol, icol, gcol in [
        ("wfc_s", "wfc_bf", 0, 2048, 256, 8, 10),
        ("wfc_s", "wfc_bf", 1, 2048, 256, 8, 10),
        ("wfc_s", "wfc_bf", 2, 2048, 256, 8, 10),
        ("wfc_s", "wfc_bf", 3, 2048, 256, 8, 10),
        ("wproj2_s", "wproj2_bf", 0, 512, 1024, 9, 11),
        ("wproj2_s", "wproj2_bf", 1, 512, 1024, 9, 11),
        ("wproj2_s", "wproj2_bf", 2, 512, 1024, 9, 11),
        ("wproj2_s", "wproj2_bf", 3, 512, 1024, 9, 11),
    ]:
        fcols = rows * ncol // 128

        @blk.sync
        def _(sy, src_=src_, q=q, rows=rows, ncol=ncol, fcols=fcols, mi=icol):
            if mi == 8 and q == 0:
                p.wait(sy, "vX", p.c_p2a)
            else:
                p.wait(sy, "vX")
            src_ap = p.dr[src_].ap()[q * rows:(q + 1) * rows, :]
            u3 = obq[:, 0:fcols].rearrange("p (a n) -> p a n", n=ncol)
            p.inc(sy.dma_start(out=u3,
                               in_=src_ap.rearrange("(a p) n -> p a n", p=128)),
                  "dS", 16)
        cL = p.cnt["dS"]

        @blk.vector
        def _(v, cL=cL, fcols=fcols, icol=icol, gcol=gcol):
            v.wait_ge(p.sem["dS"], cL)
            p.wait(v, "vX", p.c_gam)
            tmp = p.sb["TMP"][:, 0:512]
            rm = p.sb["TMP"][:, 512:1024]
            i = None
            for off in range(0, fcols, 512):
                w = min(512, fcols - off)
                sl = obq[:, off:off + w]
                v.tensor_scalar_mul(tmp[:, :w], sl, sm[:, icol:icol + 1])
                v.tensor_scalar(rm[:, :w], tmp[:, :w], 0.5, None, ALU.is_gt)
                v.tensor_scalar(tmp[:, :w], tmp[:, :w], -0.5, None, ALU.is_lt)
                v.tensor_tensor(rm[:, :w], rm[:, :w], tmp[:, :w], ALU.subtract)
                i = v.tensor_scalar_mul(obo[:, off:off + w], rm[:, :w],
                                        sm[:, gcol:gcol + 1])
            p.inc(i, "vX", 1)
        cT = p.cnt["vX"]

        @blk.sync
        def _(sy, bnc=bnc, q=q, rows=rows, ncol=ncol, fcols=fcols, cT=cT):
            sy.wait_ge(p.sem["vX"], cT)
            dst = p.dr[bnc].ap()[q * rows:(q + 1) * rows, :]
            ob3 = obo[:, 0:fcols].rearrange("p (a n) -> p a n", n=ncol)
            p.inc(sy.dma_start(out=dst.rearrange("(a p) n -> p a n", p=128),
                               in_=ob3), "dS", 16)
    c_bf = p.cnt["dS"]

    p.c_prep2_bf = c_bf


def _emit_prep2b_tr(p):
    if p.mode not in ("l1f", "full"):
        return
    blk = p.blk
    obf = p.sb["OB"][:, :, :].rearrange("p a n -> p (a n)")
    c_bf = p.c_prep2_bf

    @blk.scalar
    def _(s):
        s.wait_ge(p.sem["dS"], c_bf)
        for bnc, contrib, nrows, ncols in [("wfc_bf", "wfcT_c", 8192, 256),
                                           ("wproj2_bf", "wprojT_c", 2048, 1024)]:
            stg = obf[:, 8192:8192 + nrows]
            for ci in range(ncols // 128):
                p.inc(s.dma_start(out=stg,
                                  in_=p.dr[bnc].ap()[:, ci * 128:(ci + 1) * 128],
                                  transpose=True), "dT", 16)
                s.wait_ge(p.sem["dT"], p.cnt["dT"])
                p.inc(s.dma_start(
                    out=p.dr[contrib].ap()[ci * 128:(ci + 1) * 128, :],
                    in_=stg), "dT", 16)
                s.wait_ge(p.sem["dT"], p.cnt["dT"])
    p.c_prep2_dT = p.cnt["dT"]


def _emit_prep2c(p):
    if p.mode not in ("l1f", "full"):
        return
    blk = p.blk

    @blk.gpsimd
    def _(g):
        g.wait_ge(p.sem["dT"], p.c_prep2_dT)
        for nm in ("wfcT", "wprojT"):
            p.inc(g.collective_compute(
                "AllGather", ALU.bypass, replica_groups=RG,
                ins=[p.dr[nm + "_c"].ap()], outs=[p.dr[nm + "_g"].ap()]), "cc", 1)
            p.ccmark[nm] = p.cnt["cc"]



def _emit_ln_stream(p, tag):
    blk = p.blk
    x = p.sb["x"]
    XLN = V_XLN(p)
    wtf = p.sb["WT"][:, :, :, :].rearrange("p b a n -> p (b a n)")
    wsets = [(wtf[:, 0:1024], wtf[:, 1024:2048]),
             (wtf[:, 2048:3072], wtf[:, 3072:4096])]
    PS_A, PS_B = p.sb["PS_A"], p.sb["PS_B"]
    TMP = p.sb["TMP"]
    RM = p.sb["U1"][:, :, :].rearrange("p a n -> p (a n)").bitcast(F32)[:, 0:1024]

    mm_cnt = [0, 0]
    for ic in range(16):
        wta, wtb = wsets[ic % 2]

        @blk.vector
        def _(v, ic=ic, wta=wta, wtb=wtb):
            if ic == 0:
                p.joinall(v)
            elif ic == 1:
                pass
            else:
                p.wait(v, "tS", mm_cnt[ic % 2])
            v.tensor_copy(wta[:, :], x[:, ic, :])
            p.inc(v.tensor_tensor(wtb[:, :], x[:, ic, :], x[:, ic, :], ALU.mult),
                  "vX", 1)
        cV = p.cnt["vX"]

        @blk.tensor
        def _(t, ic=ic, cV=cV, wta=wta, wtb=wtb):
            t.wait_ge(p.sem["vX"], cV)
            i = None
            for tc in range(2):
                t.matmul(PS_A[0:1, tc * 512:(tc + 1) * 512], p.sb["ones_bf"][:, :],
                         wta[:, tc * 512:(tc + 1) * 512],
                         start=(ic == 0), stop=(ic == 15))
                i = t.matmul(PS_B[0:1, tc * 512:(tc + 1) * 512],
                             p.sb["ones_bf"][:, :],
                             wtb[:, tc * 512:(tc + 1) * 512],
                             start=(ic == 0), stop=(ic == 15))
            p.inc(i, "tS", 1)
        mm_cnt[ic % 2] = p.cnt["tS"]
    c_mm = p.cnt["tS"]

    @blk.vector
    def _(v):
        v.wait_ge(p.sem["tS"], c_mm)
        p.inc(v.tensor_scalar_mul(TMP[0:1, :], PS_A[0:1, :], 1.0 / 2048.0),
              "vX", 1)
        v.wait_ge(p.sem["vX"], p.cnt["vX"])
        p.inc(v.tensor_tensor(RM[0:1, :], TMP[0:1, :], TMP[0:1, :], ALU.mult),
              "vX", 1)
        v.wait_ge(p.sem["vX"], p.cnt["vX"])
        p.inc(v.scalar_tensor_tensor(RM[0:1, :], PS_B[0:1, :], 1.0 / 2048.0,
                                     RM[0:1, :], ALU.mult, ALU.subtract), "vX", 1)
    c_v1 = p.cnt["vX"]

    @blk.scalar
    def _(s):
        s.wait_ge(p.sem["vX"], c_v1)
        p.inc(s.activation(RM[0:1, :], RM[0:1, :], AF.Sqrt,
                           bias=p.sb["cconst"][0:1, 2:3]), "aX", 1)
    c_sq = p.cnt["aX"]

    @blk.vector
    def _(v):
        v.wait_ge(p.sem["aX"], c_sq)
        p.inc(v.reciprocal(RM[0:1, :], RM[0:1, :]), "vX", 1)
        p.inc(v.tensor_scalar_mul(TMP[0:1, :], TMP[0:1, :], -1.0), "vX", 1)
    c_v2 = p.cnt["vX"]

    @blk.tensor
    def _(t):
        t.wait_ge(p.sem["vX"], c_v2)
        i = None
        for tc in range(2):
            t.matmul(PS_A[:, tc * 512:(tc + 1) * 512], p.sb["ones_row"][:, :],
                     TMP[0:1, tc * 512:(tc + 1) * 512], start=True, stop=True)
            i = t.matmul(PS_B[:, tc * 512:(tc + 1) * 512], p.sb["ones_row"][:, :],
                         RM[0:1, tc * 512:(tc + 1) * 512], start=True, stop=True)
        p.inc(i, "tS", 1)
    c_r1 = p.cnt["tS"]

    ln_cnts = []

    @blk.vector
    def _(v, ln_cnts=ln_cnts):
        v.wait_ge(p.sem["tS"], c_r1)
        for ic in range(16):
            v.tensor_tensor(TMP[:, :], x[:, ic, :], PS_A[:, :], ALU.add)
            p.inc(v.tensor_tensor(XLN[:, ic, :], TMP[:, :], PS_B[:, :],
                                  ALU.mult), "vX", 1)
            ln_cnts.append(p.cnt["vX"])
    setattr(p, f"c_ln_{tag}", p.cnt["vX"])
    setattr(p, f"ln_cnts_{tag}", ln_cnts)



def _wtile_dma(p, sy, wg_name, oc, buf, colw=128, sem=None):
    sem = sem or ("dSa" if buf == 0 else "dSb")
    src = p.dr[wg_name].ap()[:, oc * colw:(oc + 1) * colw]
    src = src.rearrange("(a p) n -> p a n", p=128)
    p.inc(sy.dma_start(out=p.sb["WT"][:, buf, 0:8, 0:colw], in_=src[:, 0:8, :]),
          sem, 16)
    p.inc(sy.dma_start(out=p.sb["WT"][:, buf, 8:16, 0:colw], in_=src[:, 8:16, :]),
          sem, 16)
    return p.cnt[sem]


def _emit_l0_qkv(p):
    blk = p.blk
    _emit_ln_stream(p, "l0a")
    c_ln = p.c_ln_l0a
    XLN = V_XLN(p)
    rope = V_ROPE(p)
    PS_A, PS_B = p.sb["PS_A"], p.sb["PS_B"]
    QL0 = V_QL0(p)
    rope_cnt = []

    chunks = [("wq_g", oc, "q") for oc in range(16)] + \
             [("wk_g", oc, "k") for oc in range(4)]
    WTP = p.sb["WT"][:, :, :, :].rearrange("p b a n -> p (b a n)").rearrange(
        "p (a n) -> p a n", n=256)
    pair_cons = [p.cnt["tS"]]
    for idx, (wg, oc, kind) in enumerate(chunks):
        half = idx % 2
        qraw = p.sb["OB"][:, 8 + half, :]

        if half == 0:
            @blk.sync
            def _(sy, wg=wg, oc=oc):
                p.wait(sy, "cc", p.ccmark["wq" if wg == "wq_g" else "wk"])
                p.wait(sy, "tS", pair_cons[0])
                srcp = p.dr[wg].ap()[:, oc * 128:oc * 128 + 256]
                srcp = srcp.rearrange("(a p) n -> p a n", p=128)
                p.inc(sy.dma_start(out=WTP[:, :, :], in_=srcp), "dSa", 16)
            cW = p.cnt["dSa"]
            p._cW_pair = cW
        else:
            cW = p._cW_pair

        @blk.tensor
        def _(t, half=half, cW=cW, idx=idx):
            t.wait_ge(p.sem["dSa"], cW)
            t.wait_ge(p.sem["vX"], c_ln)
            if idx >= 1:
                p.wait(t, "aE")
            i = None
            for tc in range(2):
                for ic in range(16):
                    i = t.matmul(PS_A[:, tc * 512:(tc + 1) * 512],
                                 WTP[:, ic, half * 128:half * 128 + 128],
                                 XLN[:, ic, tc * 512:(tc + 1) * 512],
                                 start=(ic == 0), stop=(ic == 15))
            p.inc(i, "tS", 1)
        if half == 1:
            pair_cons[0] = p.cnt["tS"]
        cQ = p.cnt["tS"]

        @blk.scalar
        def _(s, cQ=cQ, qraw=qraw, idx=idx):
            s.wait_ge(p.sem["tS"], cQ)
            if idx >= 2:
                p.wait(s, "vX", rope_cnt[idx - 2])
            p.inc(s.copy(qraw[:, :], PS_A[:, :]), "aE", 1)
        cE = p.cnt["aE"]

        @blk.tensor
        def _(t, cE=cE, qraw=qraw, idx=idx):
            t.wait_ge(p.sem["aE"], cE)
            if idx >= 1:
                p.wait(t, "vX", rope_cnt[idx - 1])
            i = None
            for tc in range(2):
                i = t.matmul(PS_B[:, tc * 512:(tc + 1) * 512], p.sb["pswap"][:, :],
                             qraw[:, tc * 512:(tc + 1) * 512],
                             start=True, stop=True)
            p.inc(i, "tS", 1)
        cS = p.cnt["tS"]

        @blk.vector
        def _(v, oc=oc, kind=kind, cS=cS, qraw=qraw):
            v.wait_ge(p.sem["tS"], cS)
            ct = 0 if kind == "q" else 2
            st = 1 if kind == "q" else 3
            TMP, RT = p.sb["TMP"], V_RTMP(p)
            v.tensor_tensor(TMP[:, :], qraw[:, :], rope[:, ct, :], ALU.mult)
            v.tensor_tensor(RT[:, :], PS_B[:, :], rope[:, st, :], ALU.mult)
            if kind == "q":
                out = QL0[:, oc // 4, :, oc % 4, :]
                t3 = TMP[:, :].rearrange("p (s n) -> p s n", n=128)
                r3 = RT[:, :].rearrange("p (s n) -> p s n", n=128)
                p.inc(v.tensor_tensor(out, t3, r3, ALU.add), "vX", 1)
            else:
                p.inc(v.tensor_tensor(V_KC(p)[:, oc, :], TMP[:, :], RT[:, :],
                                      ALU.add), "vX", 1)
        rope_cnt.append(p.cnt["vX"])
    c_qk = p.cnt["vX"]

    v_cons = [p.cnt["tS"]]
    for vh in range(4):
        half = vh % 2

        if half == 0:
            @blk.sync
            def _(sy, vh=vh):
                p.wait(sy, "cc", p.ccmark["wv"])
                p.wait(sy, "tS", v_cons[0])
                srcp = p.dr["wv_g"].ap()[:, vh * 128:vh * 128 + 256]
                srcp = srcp.rearrange("(a p) n -> p a n", p=128)
                p.inc(sy.dma_start(out=WTP[:, :, :], in_=srcp), "dSa", 16)
            cW = p.cnt["dSa"]
            p._cW_pair = cW
        else:
            cW = p._cW_pair
        aE_base = p.cnt["aE"]

        @blk.tensor
        def _(t, vh=vh, half=half, cW=cW, aE_base=aE_base):
            t.wait_ge(p.sem["dSa"], cW)
            t.wait_ge(p.sem["vX"], c_ln)
            for tt in range(8):
                ps = p.sb["PS_C"] if tt % 2 == 0 else p.sb["PS_D"]
                if vh > 0 or tt >= 2:
                    t.wait_ge(p.sem["aE"], aE_base + tt - 2 + 1
                              if tt >= 2 else aE_base)
                i = None
                for ic in range(16):
                    i = t.matmul(ps[:, 0:128], XLN[:, ic, tt * 128:(tt + 1) * 128],
                                 WTP[:, ic, half * 128:half * 128 + 128],
                                 start=(ic == 0), stop=(ic == 15))
                p.inc(i, "tS", 1)
        if half == 1:
            v_cons[0] = p.cnt["tS"]
        cV8 = p.cnt["tS"]

        @blk.scalar
        def _(s, vh=vh, cV8=cV8):
            for tt in range(8):
                ps = p.sb["PS_C"] if tt % 2 == 0 else p.sb["PS_D"]
                s.wait_ge(p.sem["tS"], cV8 - 7 + tt)
                p.inc(s.copy(V_VC(p)[:, tt, vh * 128:(vh + 1) * 128],
                             ps[:, 0:128]), "aE", 1)
    c_v = p.cnt["aE"]

    @blk.sync
    def _(sy):
        p.wait(sy, "vX", c_qk)
        p.inc(sy.dma_start(
            out=p.dr["k0_c"].ap().rearrange("(a p) n -> p a n", p=128),
            in_=V_KC(p)[:, :, :]), "dS", 16)
        p.wait(sy, "aE", c_v)
        p.inc(sy.dma_start(
            out=p.dr["v0_c"].ap().rearrange("(a p) n -> p a n", p=128),
            in_=V_VC(p)[:, :, :]), "dS", 16)
    c_ctr = p.cnt["dS"]

    @blk.gpsimd
    def _(g):
        g.wait_ge(p.sem["dS"], c_ctr)
        for nm in ("k0", "v0"):
            p.inc(g.collective_compute(
                "AllGather", ALU.bypass, replica_groups=RG,
                ins=[p.dr[nm + "_c"].ap()], outs=[p.dr[nm + "_g"].ap()]), "cc", 1)
            p.ccmark[nm] = p.cnt["cc"]
        for nm in getattr(p, "deferred_ags", []):
            if nm in ("wqkvT", "woutT"):
                g.wait_ge(p.sem["dT"], p.deferred_dt)
            p.inc(g.collective_compute(
                "AllGather", ALU.bypass, replica_groups=RG,
                ins=[p.dr[nm + "_c"].ap()], outs=[p.dr[nm + "_g"].ap()]), "cc", 1)
            p.ccmark[nm] = p.cnt["cc"]
        p.deferred_ags = []



def _emit_kv_load(p, kg_name, vg_name, krow_off, vcol_off):
    blk = p.blk
    KS, VS = V_KS(p), V_VS(p)

    @blk.sync
    def _(sy):
        kg = p.dr[kg_name].ap()
        vg = p.dr[vg_name].ap()
        for r in range(8):
            src = kg[512 * r + krow_off:512 * r + krow_off + 128, :]
            src = src.rearrange("p (j n) -> p j n", n=128)
            p.inc(sy.dma_start(out=KS[:, 7 - r:64:8, :], in_=src), "dS", 16)
            vsrc = vg[1024 * r:1024 * (r + 1), vcol_off:vcol_off + 128]
            vsrc = vsrc.rearrange("(j p) n -> p j n", p=128)
            p.inc(sy.dma_start(out=VS[:, 7 - r:64:8, :], in_=vsrc), "dS", 16)
    return p.cnt["dS"]


def _emit_attn_group(p, rhs_q, out_ap, Dj, mask_slot, bias_col, scale):
    blk = p.blk
    KS, VS = V_KS(p), V_VS(p)
    PS_A, PS_B = p.sb["PS_A"], p.sb["PS_B"]
    psO, psD, psR = p.sb["PS_C"], p.sb["PS_E"], p.sb["PS_F"]
    npair = Dj // 2
    tS0, aE0, vM0, vX0 = (p.cnt["tS"], p.cnt["aE"], p.cnt["vM"], p.cnt["vX"])

    def pt(pr):
        return V_PT(p, pr % 2)

    @blk.tensor
    def _(t):

        def emit_S(pp):
            pr = pp // 2
            reg = PS_A if pr % 2 == 0 else PS_B
            t.wait_ge(p.sem["aE"], aE0 + pr - 1 if pr >= 2 else aE0)
            t.matmul(reg[:, (pp % 2) * 512:(pp % 2) * 512 + 512], KS[:, pp, :],
                     rhs_q, start=True, stop=True).then_inc(p.sem["tS"], 1)

        for pp0 in range(min(4, Dj)):
            emit_S(pp0)
        i = None
        for pp in range(Dj):
            pr = pp // 2
            t.wait_ge(p.sem["aE"], aE0 + pr + 1)
            if mask_slot is not None and pp >= Dj - 8:
                mp = (pp - (Dj - 8)) // 2
                t.wait_ge(p.sem["vM"], vM0 + mp + 1)
            if pp == 0:
                p.wait(t, "vX", vX0)
            src = pt(pr)[:, (pp % 2) * 512:(pp % 2) * 512 + 512]
            t.matmul(psO[:, :], VS[:, pp, :], src,
                     start=(pp == 0), stop=(pp == Dj - 1))
            i = t.matmul(psD[0:1, :], p.sb["ones_bf"][:, :], src,
                         start=(pp == 0), stop=(pp == Dj - 1))
            if pp % 2 == 1 and pp + 4 < Dj:
                emit_S(pp + 3)
                emit_S(pp + 4)
        i.then_inc(p.sem["tS"], 1)
        t.wait_ge(p.sem["vX"], vX0 + 1)
        t.matmul(psR[:, :], p.sb["ones_row"][:, :], p.sb["TMP"][0:1, 512:1024],
                 start=True, stop=True).then_inc(p.sem["tMM"], 1)
    p.cnt["tS"] = tS0 + Dj + 1
    p.cnt["tMM"] += 1
    c_av = p.cnt["tS"]
    c_r1 = p.cnt["tMM"]

    @blk.scalar
    def _(s):
        for pr in range(npair):
            s.wait_ge(p.sem["tS"], tS0 + 2 * pr + 2)
            reg = PS_A if pr % 2 == 0 else PS_B
            s.activation(pt(pr)[:, :], reg[:, :], AF.Exp,
                         bias=p.sb["cconst"][:, bias_col:bias_col + 1],
                         scale=scale).then_inc(p.sem["aE"], 1)
    p.cnt["aE"] = aE0 + npair

    @blk.vector
    def _(v):
        if mask_slot is not None:
            MSK = V_MSK(p)
            for mp in range(4):
                pr = npair - 4 + mp
                v.wait_ge(p.sem["aE"], aE0 + pr + 1)
                buf = pt(pr).rearrange("p (a h n) -> p a h n", a=2, h=4)
                mblk = MSK[:, mask_slot * 8 + 2 * mp:mask_slot * 8 + 2 * mp + 2, :]
                mblk = mblk.unsqueeze(2).broadcast_to([128, 2, 4, 128])
                v.tensor_tensor(buf, buf, mblk, ALU.mult).then_inc(p.sem["vM"], 1)
            p.cnt["vM"] = vM0 + 4
        v.wait_ge(p.sem["tS"], c_av)
        p.inc(v.reciprocal(p.sb["TMP"][0:1, 512:1024], psD[0:1, :]), "vX", 1)
        v.wait_ge(p.sem["tMM"], c_r1)
        v.tensor_copy(p.sb["TMP"][:, 0:512], psR[:, :])
        if out_ap.ndim == 3:
            p.inc(v.tensor_tensor(
                out_ap, psO[:, :].rearrange("p (h n) -> p h n", n=128),
                p.sb["TMP"][:, 0:512].rearrange("p (h n) -> p h n", n=128),
                ALU.mult), "vX", 1)
        else:
            p.inc(v.tensor_tensor(out_ap, psO[:, :], p.sb["TMP"][:, 0:512],
                                  ALU.mult), "vX", 1)


def _emit_l0_attn(p):
    blk = p.blk
    QL0 = V_QL0(p)

    @blk.sync
    def _(sy):
        p.joinall(sy)
        if p.taps:
            p.inc(sy.dma_start(out=p.dr["xln_dbg"].ap(),
                               in_=p.sb["U1"][:, :, :].rearrange("p a n -> p (a n)")),
                  "dS", 16)
            p.inc(sy.dma_start(out=p.dr["q_dbg"].ap(),
                               in_=p.sb["QB"][:, :, :].rearrange("p a n -> p (a n)")),
                  "dS", 16)
            p.wait(sy, "cc", p.ccmark["v0"])
            p.inc(sy.dma_start(out=p.dr["k0g_dbg"].ap(), in_=p.dr["k0_g"].ap()),
                  "dS", 16)
            p.inc(sy.dma_start(out=p.dr["v0g_dbg"].ap(), in_=p.dr["v0_g"].ap()),
                  "dS", 16)
            p.wait(sy, "dS")
        p.inc(sy.dma_start(
            out=p.sb["U1"][:, 0:8, :],
            in_=p.dr["mask0"].ap().rearrange("p (a n) -> p a n", n=1024)), "dS", 16)
    c_msk = p.cnt["dS"]

    for kv in range(4):
        @blk.sync
        def _(sy, kv=kv):
            p.wait(sy, "cc", p.ccmark["v0"])
            sy.wait_ge(p.sem["dS"], c_msk)
            if kv > 0:
                p.wait(sy, "vX")
        c_kv = _emit_kv_load(p, "k0_g", "v0_g", 128 * kv, 128 * kv)

        for j in range(8):
            @blk.tensor
            def _(t, c_kv=c_kv, j=j):
                if j == 0:
                    t.wait_ge(p.sem["dS"], c_kv)
            rhs = QL0[:, kv, j, :, :].rearrange("p h n -> p (h n)")
            out = p.sb["OB"][:, 4 * kv:4 * kv + 4, 128 * j:128 * j + 128]
            _emit_attn_group(p, rhs, out, 8 * (j + 1), j, 0, 1.0)


def _emit_proj(p, layer):
    blk = p.blk
    wg = "wproj_g" if layer == 0 else "woutT_g"
    mark = "wproj" if layer == 0 else "woutT"
    OB, x = p.sb["OB"], p.sb["x"]

    if p.taps and layer == 0:
        @blk.sync
        def _(sy):
            p.joinall(sy, ("vX",))
            p.inc(sy.dma_start(out=p.dr["ob_dbg"].ap(),
                               in_=OB[:, :, :].rearrange("p a n -> p (a n)")),
                  "dS", 16)

    WTP = p.sb["WT"][:, :, :, :].rearrange("p b a n -> p (b a n)").rearrange(
        "p (a n) -> p a n", n=256)
    wt_cons = [p.cnt["tS"]]
    ev_cnt = {0: p.cnt["vX"], 1: p.cnt["vX"]}
    for oc in range(16):
        half = oc % 2

        if half == 0:
            @blk.sync
            def _(sy, oc=oc):
                p.wait(sy, "cc", p.ccmark[mark])
                if oc == 0:
                    p.joinall(sy, ("vX", "aE", "tS"))
                p.wait(sy, "tS", wt_cons[0])
                srcp = p.dr[wg].ap()[:, oc * 128:oc * 128 + 256]
                srcp = srcp.rearrange("(a p) n -> p a n", p=128)
                p.inc(sy.dma_start(out=WTP[:, :, :], in_=srcp), "dSa", 16)
            cW = p.cnt["dSa"]
            p._cW_pair = cW
        else:
            cW = p._cW_pair

        @blk.tensor
        def _(t, oc=oc, half=half, cW=cW, e0=ev_cnt[0], e1=ev_cnt[1]):
            t.wait_ge(p.sem["dSa"], cW)
            for tc in range(2):
                ps = p.sb["PS_C"] if tc == 0 else p.sb["PS_D"]
                p.wait(t, "vX", e0 if tc == 0 else e1)
                i = None
                for ic in range(16):
                    i = t.matmul(ps[:, :], WTP[:, ic, half * 128:half * 128 + 128],
                                 OB[:, ic, tc * 512:(tc + 1) * 512],
                                 start=(ic == 0), stop=(ic == 15))
                p.inc(i, "tS", 1)
        if half == 1:
            wt_cons[0] = p.cnt["tS"]
        cM = p.cnt["tS"]

        @blk.vector
        def _(v, oc=oc, cM=cM):
            for tc in range(2):
                v.wait_ge(p.sem["tS"], cM - 1 + tc)
                ps = p.sb["PS_C"] if tc == 0 else p.sb["PS_D"]
                p.inc(v.tensor_tensor(x[:, oc, tc * 512:(tc + 1) * 512], ps[:, :],
                                      x[:, oc, tc * 512:(tc + 1) * 512], ALU.add),
                      "vX", 1)
            ev_cnt[0] = p.cnt["vX"] - 1
            ev_cnt[1] = p.cnt["vX"]



def _emit_l0_ffn(p):
    blk = p.blk
    c_bf0 = p.cnt["vX"]
    _emit_ln_stream(p, "l0f")
    c_ln = p.c_ln_l0f
    XLN = V_XLN(p)
    x = p.sb["x"]
    inter = p.sb["AR1"][:, 0:8192].rearrange("p (a n) -> p a n", n=512)
    sg = p.sb["TMP"][:, :].bitcast(BF16)
    banks = [p.sb["PS_C"], p.sb["PS_D"], p.sb["PS_E"], p.sb["PS_F"]]
    sg_free = [p.cnt["vX"], p.cnt["vX"]]

    for e in range(4):
        rhs_all = XLN[:, :, :].rearrange("p a (n4 f) -> p a n4 f", f=4)
        jobs = []
        for o in range(16):
            jobs.append((o, 0))
            jobs.append((o, 1))
        wt_cons = [p.cnt["tS"], p.cnt["tS"]]
        bank_free = [c_bf0 if e == 0 else p.cnt["vX"]] * 4
        gate_mm = {}
        silu_cnt = {}
        mult_cnt = {}
        for ji, (o, kind) in enumerate(jobs):
            buf = ji % 2
            bank = ji % 4

            @blk.sync
            def _(sy, e=e, o=o, kind=kind, buf=buf):
                p.wait(sy, "cc", p.ccmark["gup"])
                p.wait(sy, "tS", wt_cons[buf])
                srcw = p.dr["gup_g"].ap().rearrange(
                    "(r ee i pp) n -> ee pp r i n", ee=4, i=2, pp=128)[e]
                co = kind * 2048 + o * 128
                sem = "dSa" if buf == 0 else "dSb"
                for k in range(2):
                    p.inc(sy.dma_start(
                        out=p.sb["WT"][:, buf, k:16:2, :],
                        in_=srcw[:, :, k, co:co + 128]), sem, 16)
            cW = p.cnt["dSa" if buf == 0 else "dSb"]

            @blk.tensor
            def _(t, e=e, ji=ji, o=o, kind=kind, buf=buf, bank=bank, cW=cW,
                  bf=bank_free[bank]):
                t.wait_ge(p.sem["dSa" if buf == 0 else "dSb"], cW)
                p.wait(t, "vX", bf)
                if kind == 1:
                    p.wait(t, "aX", silu_cnt.get(o, 0))
                i = None
                for ic in range(16):
                    if e == 0 and ji == 0:
                        t.wait_ge(p.sem["vX"], p.ln_cnts_l0f[ic])
                    i = t.matmul(banks[bank][:, 0:256], p.sb["WT"][:, buf, ic, :],
                                 rhs_all[:, ic, :, e], start=(ic == 0),
                                 stop=(ic == 15))
                p.inc(i, "tS", 1)
            wt_cons[buf] = p.cnt["tS"]
            cM = p.cnt["tS"]
            if kind == 0:
                gate_mm[o] = (cM, bank)
            else:
                gcM, gbank = gate_mm[o]

                sl = sg[:, (o % 2) * 256:(o % 2) * 256 + 256]

                @blk.scalar
                def _(s, o=o, gcM=gcM, gbank=gbank, sl=sl, sf=sg_free[o % 2]):
                    s.wait_ge(p.sem["tS"], gcM)
                    s.wait_ge(p.sem["vX"], mult_cnt[o - 2] if o >= 2 else sf)
                    p.inc(s.activation(sl, banks[gbank][:, 0:256], AF.Silu),
                          "aX", 1)
                silu_cnt[o] = p.cnt["aX"]
                cS = p.cnt["aX"]

                @blk.vector
                def _(v, o=o, cS=cS, cM=cM, bank=bank, sl=sl):
                    v.wait_ge(p.sem["aX"], cS)
                    v.wait_ge(p.sem["tS"], cM)
                    p.inc(v.tensor_tensor(inter[:, o, 0:256], sl,
                                          banks[bank][:, 0:256], ALU.mult),
                          "vX", 1)
                mult_cnt[o] = p.cnt["vX"]
                sg_free[o % 2] = p.cnt["vX"]
                gb_ev = p.cnt["vX"]
                bank_free[gbank] = gb_ev
                bank_free[bank] = gb_ev
        c_int = p.cnt["vX"]

        wt_cons = [p.cnt["tS"], p.cnt["tS"]]
        bank_free = [p.cnt["vX"]] * 2
        for oc in range(16):
            buf = oc % 2
            bank = oc % 2

            @blk.sync
            def _(sy, e=e, oc=oc, buf=buf):
                p.wait(sy, "cc", p.ccmark["down"])
                p.wait(sy, "tS", wt_cons[buf])
                srcw = p.dr["down_g"].ap().rearrange(
                    "(r ee i pp) n -> ee pp r i n", ee=4, i=2, pp=128)[e]
                sem = "dSa" if buf == 0 else "dSb"
                for k in range(2):
                    p.inc(sy.dma_start(
                        out=p.sb["WT"][:, buf, k:16:2, :],
                        in_=srcw[:, :, k, oc * 128:(oc + 1) * 128]), sem, 16)
            cW = p.cnt["dSa" if buf == 0 else "dSb"]

            @blk.tensor
            def _(t, oc=oc, buf=buf, bank=bank, cW=cW, c_int=c_int,
                  bf=bank_free[bank]):
                t.wait_ge(p.sem["dSa" if buf == 0 else "dSb"], cW)
                t.wait_ge(p.sem["vX"], c_int)
                p.wait(t, "vX", bf)
                i = None
                for ic in range(16):
                    i = t.matmul(banks[bank][:, 0:256], p.sb["WT"][:, buf, ic, :],
                                 inter[:, ic, 0:256], start=(ic == 0),
                                 stop=(ic == 15))
                p.inc(i, "tS", 1)
            wt_cons[buf] = p.cnt["tS"]
            cM = p.cnt["tS"]

            @blk.vector
            def _(v, e=e, oc=oc, cM=cM, bank=bank):
                v.wait_ge(p.sem["tS"], cM)
                xo = x[:, :, :].rearrange("p a (n4 f) -> p a n4 f", f=4)[:, oc, :, e]
                p.inc(v.tensor_tensor(xo, banks[bank][:, 0:256], xo, ALU.add),
                      "vX", 1)
            bank_free[bank] = p.cnt["vX"]



def _emit_l1_qkv(p):
    blk = p.blk
    _emit_ln_stream(p, "l1a")
    c_ln = p.c_ln_l1a
    XLN = V_XLN(p)
    QB = p.sb["QB"]

    banks = [p.sb["PS_C"], p.sb["PS_D"], p.sb["PS_E"], p.sb["PS_F"]]
    WTP = p.sb["WT"][:, :, :, :].rearrange("p b a n -> p (b a n)").rearrange(
        "p (a n) -> p a n", n=256)
    wt_cons = [p.cnt["tS"]]
    pair_free = [p.cnt["aX"], p.cnt["aX"]]

    def qkv_chunk(oc, dest_fn, idx):
        half = idx % 2
        pr = idx % 2

        if half == 0:
            @blk.sync
            def _(sy, oc=oc):
                p.wait(sy, "cc", p.ccmark["wqkvT"])
                p.wait(sy, "tS", wt_cons[0])
                srcp = p.dr["wqkvT_g"].ap()[:, oc * 128:oc * 128 + 256]
                srcp = srcp.rearrange("(a p) n -> p a n", p=128)
                p.inc(sy.dma_start(out=WTP[:, :, :], in_=srcp), "dSa", 16)
            p._cW_pair = p.cnt["dSa"]
        cW = p._cW_pair

        @blk.tensor
        def _(t, half=half, cW=cW, pr=pr, pf=pair_free[pr]):
            t.wait_ge(p.sem["dSa"], cW)
            p.wait(t, "aX", pf)
            for tc in range(2):
                ps = banks[pr * 2 + tc]
                i = None
                for ic in range(16):
                    if tc == 0:
                        t.wait_ge(p.sem["vX"], p.ln_cnts_l1a[ic])
                    i = t.matmul(ps[:, :], WTP[:, ic, half * 128:half * 128 + 128],
                                 XLN[:, ic, tc * 512:(tc + 1) * 512],
                                 start=(ic == 0), stop=(ic == 15))
                p.inc(i, "tS", 1)
        if half == 1:
            wt_cons[0] = p.cnt["tS"]
        cM = p.cnt["tS"]

        @blk.scalar
        def _(s, oc=oc, cM=cM, pr=pr):
            for tc in range(2):
                s.wait_ge(p.sem["tS"], cM - 1 + tc)
                ps = banks[pr * 2 + tc]
                p.inc(s.copy(dest_fn(oc, tc), ps[:, :]), "aX", 1)
        pair_free[pr] = p.cnt["aX"]

    idx = 0
    for oc in range(16):
        qkv_chunk(oc, lambda oc, tc: QB[:, oc, tc * 512:(tc + 1) * 512], idx)
        idx += 1
        if idx == 2:
            @blk.vector
            def _(v):
                pass

    for hg in range(4):
        for kk in range(4):
            qkv_chunk(16 + hg * 4 + kk,
                      lambda oc, tc, kk=kk: V_KC(p)[:, kk, tc * 512:(tc + 1) * 512],
                      idx)
            idx += 1
        c_k = p.cnt["aX"]

        @blk.sync
        def _(sy, hg=hg, c_k=c_k):
            p.wait(sy, "aX", c_k)
            p.wait(sy, "tS")
            src = p.dr["wqkvT_g"].ap()[:, 4096 + hg * 512:4096 + (hg + 1) * 512]
            src = src.rearrange("(a p) n -> p a n", p=128)
            for hic in range(2):
                p.inc(sy.dma_start(
                    out=p.sb["AR1"][:, hic * 8192:(hic + 1) * 8192].rearrange(
                        "p (a n) -> p a n", n=512)[:, 0:8, :],
                    in_=src[:, hic * 8:(hic + 1) * 8, :]), "dS", 16)
        cVW = p.cnt["dS"]
        aXb = p.cnt["aX"]

        @blk.tensor
        def _(t, cVW=cVW, aXb=aXb, pf0=pair_free[0], pf1=pair_free[1]):
            t.wait_ge(p.sem["dS"], cVW)
            for tt in range(8):
                ps = p.sb["PS_C"] if tt % 2 == 0 else p.sb["PS_D"]
                if tt >= 2:
                    t.wait_ge(p.sem["aX"], aXb + tt - 1)
                else:
                    t.wait_ge(p.sem["aX"], aXb)
                i = None
                for ic in range(16):
                    wv = p.sb["AR1"][:, (ic // 8) * 8192:(ic // 8 + 1) * 8192]
                    wv = wv.rearrange("p (a n) -> p a n", n=512)
                    i = t.matmul(ps[:, :], XLN[:, ic, tt * 128:(tt + 1) * 128],
                                 wv[:, ic % 8, :], start=(ic == 0), stop=(ic == 15))
                p.inc(i, "tS", 1)
        cVM = p.cnt["tS"]

        @blk.scalar
        def _(s, cVM=cVM):
            for tt in range(8):
                ps = p.sb["PS_C"] if tt % 2 == 0 else p.sb["PS_D"]
                s.wait_ge(p.sem["tS"], cVM - 7 + tt)
                p.inc(s.copy(V_VC(p)[:, tt, :], ps[:, :]), "aX", 1)
        c_vg = p.cnt["aX"]

        @blk.sync
        def _(sy, hg=hg, c_vg=c_vg):
            p.wait(sy, "aX", c_vg)
            p.inc(sy.dma_start(
                out=p.dr[f"k1_c{hg}"].ap().rearrange("(a p) n -> p a n", p=128),
                in_=V_KC(p)[:, :, :]), "dS", 16)
            p.inc(sy.dma_start(
                out=p.dr[f"v1_c{hg}"].ap().rearrange("(a p) n -> p a n", p=128),
                in_=V_VC(p)[:, :, :]), "dS", 16)
        c_ctr = p.cnt["dS"]

        @blk.gpsimd
        def _(g, hg=hg, c_ctr=c_ctr):
            g.wait_ge(p.sem["dS"], c_ctr)
            for nm in (f"k1_c{hg}", f"v1_c{hg}"):
                p.inc(g.collective_compute(
                    "AllGather", ALU.bypass, replica_groups=RG,
                    ins=[p.dr[nm].ap()], outs=[p.dr[nm.replace('_c', '_g')].ap()]),
                    "cc", 1)
            p.ccmark[f"kv1_{hg}"] = p.cnt["cc"]
        if hg == 1:
            _emit_prep2b_tr(p)


def _emit_l1_attn(p):
    blk = p.blk
    QB = p.sb["QB"]
    first = True
    for hg in range(4):
        for hh in range(4):
            h = hg * 4 + hh

            @blk.sync
            def _(sy, hg=hg, first=first):
                p.wait(sy, "cc", p.ccmark[f"kv1_{hg}"])
                if first:
                    p.joinall(sy)
                else:
                    p.wait(sy, "vX")
            c_kv = _emit_kv_load(p, f"k1_g{hg}", f"v1_g{hg}", 128 * hh, 128 * hh)

            for qc in range(2):
                @blk.tensor
                def _(t, c_kv=c_kv, qc=qc):
                    if qc == 0:
                        t.wait_ge(p.sem["dS"], c_kv)
                _emit_attn_group(p, QB[:, h, qc * 512:(qc + 1) * 512],
                                 p.sb["OB"][:, h, qc * 512:(qc + 1) * 512],
                                 64, None, 1, SCALE)
            first = False




def _vchain(p, v, ops):
    for f in ops:
        p.inc(f(), "vX", 1)
        v.wait_ge(p.sem["vX"], p.cnt["vX"])


def _emit_l1_ffn(p):
    blk = p.blk
    x = p.sb["x"]
    QB, OB, U1 = p.sb["QB"], p.sb["OB"], p.sb["U1"]
    sm = p.sb["statm"]
    xT = [OB[:, 0:2, :].rearrange("p a n -> p (a n)"),
          OB[:, 2:4, :].rearrange("p a n -> p (a n)")]
    xQ = [OB[:, 4:6, :].rearrange("p a n -> p (a n)"),
          OB[:, 6:8, :].rearrange("p a n -> p (a n)")]
    scr = OB[:, 8:10, :].rearrange("p a n -> p (a n)")
    TMPf = p.sb["TMP"]

    @blk.vector
    def _(v):
        p.joinall(v)
        for ic in range(16):
            p.inc(v.tensor_copy(QB[:, ic, :], x[:, ic, :]), "vX", 1)
    c_xbf = p.cnt["vX"]

    @blk.sync
    def _(sy):
        p.wait(sy, "vX", c_xbf)
        p.inc(sy.dma_start(
            out=p.dr["xbf_d"].ap().rearrange("(a p) n -> p a n", p=128),
            in_=QB[:, :, :]), "dS", 16)
    c_xd = p.cnt["dS"]

    quant_cnt = []
    ship_cnt = []
    for tt in range(8):
        b = tt % 2

        @blk.scalar
        def _(s, tt=tt, b=b):
            s.wait_ge(p.sem["dS"], c_xd)
            if tt >= 2:
                p.wait(s, "vX", quant_cnt[tt - 2])
            p.inc(s.dma_start(out=xT[b][:, :],
                              in_=p.dr["xbf_d"].ap()[:, tt * 128:(tt + 1) * 128],
                              transpose=True), "dTa" if b == 0 else "dTb", 16)
        cT = p.cnt["dTa" if b == 0 else "dTb"]

        @blk.vector
        def _(v, tt=tt, b=b, cT=cT):
            v.wait_ge(p.sem["dTa" if b == 0 else "dTb"], cT)
            v.reduce_sum(sm[:, 0:1], xT[b][:, :], AX)
            v.reduce_max(sm[:, 2:3], xT[b][:, :], AX)
            v.tensor_reduce(sm[:, 3:4], xT[b][:, :], AX, ALU.min)
            p.inc(v.scalar_tensor_tensor(scr[:, 0:2048], xT[b][:, :], 1.0,
                                         xT[b][:, :], ALU.mult, ALU.mult,
                                         accum_out=sm[:, 1:2]), "vX", 1)
            v.wait_ge(p.sem["vX"], p.cnt["vX"])
            _vchain(p, v, [
                lambda: v.tensor_scalar_mul(sm[:, 4:5], sm[:, 0:1], 1.0 / 2048.0),
                lambda: v.tensor_tensor(sm[:, 5:6], sm[:, 4:5], sm[:, 4:5],
                                        ALU.mult),
                lambda: v.tensor_scalar_mul(sm[:, 9:10], sm[:, 1:2], 1.0 / 2048.0),
                lambda: v.tensor_tensor(sm[:, 5:6], sm[:, 9:10], sm[:, 5:6],
                                        ALU.subtract),
            ])
        cV = p.cnt["vX"]

        @blk.scalar
        def _(s, cV=cV):
            s.wait_ge(p.sem["vX"], cV)
            p.inc(s.activation(sm[:, 6:7], sm[:, 5:6], AF.Sqrt,
                               bias=p.sb["cconst"][:, 3:4], scale=1.0 + 1e-6),
                  "aX", 1)
        cS = p.cnt["aX"]

        @blk.vector
        def _(v, tt=tt, b=b, cS=cS):
            v.wait_ge(p.sem["aX"], cS)
            if tt >= 2:
                v.wait_ge(p.sem["dS"], ship_cnt[tt - 2])
            _vchain(p, v, [
                lambda: v.reciprocal(sm[:, 6:7], sm[:, 6:7]),
                lambda: v.tensor_tensor(sm[:, 2:3], sm[:, 2:3], sm[:, 4:5],
                                        ALU.subtract),
                lambda: v.tensor_tensor(sm[:, 3:4], sm[:, 4:5], sm[:, 3:4],
                                        ALU.subtract),
                lambda: v.tensor_tensor(sm[:, 7:8], sm[:, 2:3], sm[:, 3:4],
                                        ALU.max),
                lambda: v.reciprocal(sm[:, 12:13], sm[:, 7:8]),
                lambda: v.tensor_scalar_mul(sm[:, 12:13], sm[:, 12:13], 128.0),
                lambda: v.tensor_tensor(sm[:, 13:14], sm[:, 4:5], sm[:, 12:13],
                                        ALU.mult),
                lambda: v.tensor_scalar_mul(sm[:, 13:14], sm[:, 13:14], -1.0),
                lambda: v.tensor_tensor(sm[:, 14:15], sm[:, 6:7], sm[:, 7:8],
                                        ALU.mult),
                lambda: v.tensor_scalar_mul(sm[:, 14:15], sm[:, 14:15],
                                            1.0 / 128.0),
            ])
            v.tensor_scalar(xQ[b][:, :], xT[b][:, :], sm[:, 12:13], sm[:, 13:14],
                            ALU.mult, ALU.add)
            v.tensor_scalar(xQ[b][:, :], xQ[b][:, :], CLIP, -CLIP, ALU.min, ALU.max)
            p.inc(v.tensor_scalar_mul(xQ[b][:, :], xQ[b][:, :], sm[:, 14:15]),
                  "vX", 1)
        quant_cnt.append(p.cnt["vX"])

        @blk.sync
        def _(sy, tt=tt, b=b):
            p.wait(sy, "vX", quant_cnt[tt])
            if p.taps == 2 and tt == 0:
                p.inc(sy.dma_start(out=p.dr["sm_dbg"].ap(), in_=sm[:, :]), "dS", 16)
                p.inc(sy.dma_start(out=p.dr["xt0_dbg"].ap(), in_=xT[b][:, :]),
                      "dS", 16)
                p.inc(sy.dma_start(
                    out=p.dr["xbf_dbg"].ap().rearrange("(a p) n -> p a n", p=128),
                    in_=QB[:, :, :].rearrange("p a n -> p (a n)").rearrange(
                        "p (a n) -> p a n", n=1024)), "dS", 16)
            p.inc(sy.dma_start(out=p.dr["xhat_d"].ap()[tt * 128:(tt + 1) * 128, :],
                               in_=xQ[b][:, :]), "dS", 16)
        ship_cnt.append(p.cnt["dS"])
    c_xq = p.cnt["dS"]

    @blk.scalar
    def _(s):
        s.wait_ge(p.sem["dS"], c_xq)
        for ic in range(16):
            p.inc(s.dma_start(out=U1[:, ic, :],
                              in_=p.dr["xhat_d"].ap()[:, ic * 128:(ic + 1) * 128],
                              transpose=True), "dT", 16)
    c_xh = p.cnt["dT"]

    if p.taps == 2:
        @blk.sync
        def _(sy):
            p.wait(sy, "cc", p.ccmark["wprojT"])
            p.wait(sy, "dT", c_xh)
            p.inc(sy.dma_start(out=p.dr["gam_dbg"].ap(), in_=p.dr["gam_g"].ap()),
                  "dS", 16)
            p.inc(sy.dma_start(out=p.dr["wfcT_dbg"].ap(), in_=p.dr["wfcT_g"].ap()),
                  "dS", 16)
            p.inc(sy.dma_start(out=p.dr["wprojT_dbg"].ap(),
                               in_=p.dr["wprojT_g"].ap()), "dS", 16)
            p.inc(sy.dma_start(out=p.dr["xhat_dbg"].ap(), in_=p.dr["xhat_d"].ap()),
                  "dS", 16)
            p.wait(sy, "dS")

    hbuf = [p.sb["AR1"][:, 0:8192], p.sb["AR1"][:, 8192:16384]]
    x2fm = QB[:, :, :].rearrange("p a n -> p (a n)").rearrange(
        "p (a n) -> p a n", n=256)
    wst = p.sb["WT"][:, :, :, :].rearrange("p b a n -> p (b a n)").rearrange(
        "p (a n) -> p a n", n=512)
    wtile32 = p.sb["WT"][:, :, :, :].rearrange("p b a n -> p (b a) n")

    for w in range(4):
        wave_v0 = p.cnt["vX"]
        wst2 = [p.sb["WT"][:, 0, :, :].rearrange("p a n -> p (a n)").rearrange(
                    "p (a n) -> p a n", n=512),
                p.sb["WT"][:, 1, :, :].rearrange("p a n -> p (a n)").rearrange(
                    "p (a n) -> p a n", n=512)]
        wt_cons2 = [p.cnt["tS"], p.cnt["tS"]]
        pairfree = {0: p.cnt["aX"], 1: p.cnt["aX"]}
        for fcg in range(16):
            for ich in range(4):
                buf = ich % 2

                @blk.sync
                def _(sy, w=w, fcg=fcg, ich=ich, buf=buf):
                    p.wait(sy, "cc", p.ccmark["wfcT"])
                    p.wait(sy, "tS", wt_cons2[buf])
                    src = p.dr["wfcT_g"].ap()[512 * ich:512 * (ich + 1),
                                              fcg * 512:(fcg + 1) * 512]
                    src = src.rearrange("(a p) n -> p a n", p=128)
                    p.inc(sy.dma_start(out=wst2[buf][:, :, :], in_=src),
                          "dSa" if buf == 0 else "dSb", 16)
                cW = p.cnt["dSa" if buf == 0 else "dSb"]

                @blk.tensor
                def _(t, w=w, fcg=fcg, ich=ich, buf=buf, cW=cW,
                      pf=pairfree[fcg % 2]):
                    t.wait_ge(p.sem["dSa" if buf == 0 else "dSb"], cW)
                    t.wait_ge(p.sem["dT"], c_xh)
                    if ich == 0:
                        p.wait(t, "aX", pf)
                        if fcg < 2:
                            p.wait(t, "vX", wave_v0)
                    i = None
                    for lt in range(2):
                        tt = 2 * w + lt
                        ps = (p.sb["PS_C"], p.sb["PS_D"], p.sb["PS_E"],
                              p.sb["PS_F"])[(fcg % 2) * 2 + lt]
                        for ic4 in range(4):
                            ic = ich * 4 + ic4
                            i = t.matmul(ps[:, :],
                                         U1[:, ic, tt * 128:(tt + 1) * 128],
                                         wst2[buf][:, ic4, :],
                                         start=(ic == 0), stop=(ic == 15))
                    p.inc(i, "tS", 1)
                wt_cons2[buf] = p.cnt["tS"]
            cM = p.cnt["tS"]

            @blk.scalar
            def _(s, w=w, fcg=fcg, cM=cM):
                s.wait_ge(p.sem["tS"], cM)
                for lt in range(2):
                    ps = (p.sb["PS_C"], p.sb["PS_D"], p.sb["PS_E"],
                          p.sb["PS_F"])[(fcg % 2) * 2 + lt]
                    p.inc(s.activation(hbuf[lt][:, fcg * 512:(fcg + 1) * 512],
                                       ps[:, :], AF.Gelu), "aX", 1)
            pairfree[fcg % 2] = p.cnt["aX"]
            cG = p.cnt["aX"]

            @blk.vector
            def _(v, fcg=fcg, cG=cG):
                v.wait_ge(p.sem["aX"], cG)
                for lt in range(2):
                    base = lt * 64
                    hsl = hbuf[lt][:, fcg * 512:(fcg + 1) * 512]
                    v.reduce_sum(TMPf[:, base + fcg:base + fcg + 1], hsl, AX)
                    v.reduce_max(TMPf[:, base + 32 + fcg:base + 33 + fcg],
                                 hsl, AX)
                    v.tensor_reduce(TMPf[:, base + 48 + fcg:base + 49 + fcg],
                                    hsl, AX, ALU.min)
                    p.inc(v.scalar_tensor_tensor(
                        scr[:, 0:512], hsl, 1.0, hsl, ALU.mult, ALU.mult,
                        accum_out=TMPf[:, base + 16 + fcg:base + 17 + fcg]),
                        "vX", 1)
        c_fc = p.cnt["aX"]
        c_part = p.cnt["vX"]

        if p.taps == 2 and w == 0:
            @blk.sync
            def _(sy, c_fc=c_fc):
                sy.wait_ge(p.sem["aX"], c_fc)
                p.inc(sy.dma_start(out=p.dr["h0_dbg"].ap(), in_=hbuf[0][:, :]),
                      "dS", 16)
                p.wait(sy, "dS")

        for lt in range(2):
            hb = hbuf[lt]
            base = lt * 64

            @blk.vector
            def _(v, hb=hb, base=base, c_part=c_part):
                v.wait_ge(p.sem["vX"], c_part)
                _vchain(p, v, [
                    lambda: v.reduce_sum(sm[:, 0:1], TMPf[:, base:base + 16],
                                         AX),
                    lambda: v.reduce_sum(sm[:, 1:2],
                                         TMPf[:, base + 16:base + 32], AX),
                    lambda: v.reduce_max(sm[:, 2:3],
                                         TMPf[:, base + 32:base + 48], AX),
                    lambda: v.tensor_reduce(sm[:, 3:4],
                                            TMPf[:, base + 48:base + 64], AX,
                                            ALU.min),
                    lambda: v.tensor_scalar_mul(sm[:, 4:5], sm[:, 0:1],
                                                1.0 / 8192.0),
                    lambda: v.tensor_tensor(sm[:, 5:6], sm[:, 4:5], sm[:, 4:5],
                                            ALU.mult),
                    lambda: v.tensor_scalar_mul(sm[:, 9:10], sm[:, 1:2],
                                                1.0 / 8192.0),
                    lambda: v.tensor_tensor(sm[:, 5:6], sm[:, 9:10], sm[:, 5:6],
                                            ALU.subtract),
                ])
            cV = p.cnt["vX"]

            @blk.scalar
            def _(s, cV=cV):
                s.wait_ge(p.sem["vX"], cV)
                p.inc(s.activation(sm[:, 6:7], sm[:, 5:6], AF.Sqrt,
                                   bias=p.sb["cconst"][:, 4:5]), "aX", 1)
            cS = p.cnt["aX"]

            @blk.vector
            def _(v, hb=hb, cS=cS):
                v.wait_ge(p.sem["aX"], cS)
                _vchain(p, v, [
                    lambda: v.reciprocal(sm[:, 6:7], sm[:, 6:7]),
                    lambda: v.tensor_tensor(sm[:, 2:3], sm[:, 2:3], sm[:, 4:5],
                                            ALU.subtract),
                    lambda: v.tensor_tensor(sm[:, 3:4], sm[:, 4:5], sm[:, 3:4],
                                            ALU.subtract),
                    lambda: v.tensor_tensor(sm[:, 7:8], sm[:, 2:3], sm[:, 3:4],
                                            ALU.max),
                    lambda: v.reciprocal(sm[:, 12:13], sm[:, 7:8]),
                    lambda: v.tensor_scalar_mul(sm[:, 12:13], sm[:, 12:13], 128.0),
                    lambda: v.tensor_tensor(sm[:, 13:14], sm[:, 4:5],
                                            sm[:, 12:13], ALU.mult),
                    lambda: v.tensor_scalar_mul(sm[:, 13:14], sm[:, 13:14], -1.0),
                    lambda: v.tensor_tensor(sm[:, 14:15], sm[:, 6:7], sm[:, 7:8],
                                            ALU.mult),
                    lambda: v.tensor_scalar_mul(sm[:, 14:15], sm[:, 14:15],
                                                1.0 / 128.0),
                ])
                v.tensor_scalar(hb[:, :], hb[:, :], sm[:, 12:13], sm[:, 13:14],
                                ALU.mult, ALU.add)
                v.tensor_scalar(hb[:, :], hb[:, :], CLIP, -CLIP, ALU.min, ALU.max)
                p.inc(v.tensor_scalar_mul(hb[:, :], hb[:, :], sm[:, 14:15]),
                      "vX", 1)
            cQ2 = p.cnt["vX"]

            @blk.sync
            def _(sy, lt=lt, hb=hb, cQ2=cQ2):
                sy.wait_ge(p.sem["vX"], cQ2)
                p.inc(sy.dma_start(
                    out=p.dr["xh2_d"].ap()[lt * 128:(lt + 1) * 128, :],
                    in_=hb[:, :]), "dS", 16)
        c_ship = p.cnt["dS"]

        if p.taps == 2 and w == 0:
            @blk.sync
            def _(sy, c_ship=c_ship):
                sy.wait_ge(p.sem["dS"], c_ship)
                p.inc(sy.dma_start(out=p.dr["xh2_dbg"].ap(), in_=p.dr["xh2_d"].ap()),
                      "dS", 16)
                p.wait(sy, "dS")

        x2g = []

        @blk.scalar
        def _(s, w=w, c_ship=c_ship, x2g=x2g):
            s.wait_ge(p.sem["dS"], c_ship)
            if w > 0:
                p.wait(s, "tS")
            for g8 in range(8):
                for ic8 in range(8):
                    ic = g8 * 8 + ic8
                    p.inc(s.dma_start(
                        out=x2fm[:, ic, :],
                        in_=p.dr["xh2_d"].ap()[:, ic * 128:(ic + 1) * 128],
                        transpose=True), "dT", 16)
                s.wait_ge(p.sem["dT"], p.cnt["dT"])
                x2g.append(p.cnt["dT"])
        c_x2 = p.cnt["dT"]

        WTH = [p.sb["WT"][:, 0, :, :].rearrange("p a n -> p (a n)").rearrange(
                   "p (a n) -> p a n", n=256),
               p.sb["WT"][:, 1, :, :].rearrange("p a n -> p (a n)").rearrange(
                   "p (a n) -> p a n", n=256)]
        wt_cons3 = [p.cnt["tS"], p.cnt["tS"]]
        bankfree = {0: p.cnt["vX"], 1: p.cnt["vX"]}
        for ocp in range(8):
            for ih in range(8):
                buf = ih % 2

                @blk.sync
                def _(sy, ocp=ocp, ih=ih, buf=buf):
                    p.wait(sy, "cc", p.ccmark["wprojT"])
                    p.wait(sy, "tS", wt_cons3[buf])
                    src = p.dr["wprojT_g"].ap()[1024 * ih:1024 * (ih + 1),
                                                ocp * 256:(ocp + 1) * 256]
                    src = src.rearrange("(a p) n -> p a n", p=128)
                    p.inc(sy.dma_start(out=WTH[buf][:, :, :], in_=src),
                          "dSa" if buf == 0 else "dSb", 16)
                cW = p.cnt["dSa" if buf == 0 else "dSb"]

                if p.taps == 2 and w == 0 and ocp == 0 and ih == 1:
                    @blk.sync
                    def _(sy, cW=cW):
                        sy.wait_ge(p.sem["dT"], c_x2)
                        sy.wait_ge(p.sem["dSa"], p.cnt["dSa"])
                        sy.wait_ge(p.sem["dSb"], p.cnt["dSb"])
                        p.inc(sy.dma_start(
                            out=p.dr["x2fm_dbg"].ap(),
                            in_=QB[:, :, :].rearrange("p a n -> p (a n)")),
                            "dS", 16)
                        p.inc(sy.dma_start(
                            out=p.dr["wth_dbg"].ap(),
                            in_=p.sb["WT"][:, :, :, :].rearrange(
                                "p b a n -> p (b a n)")), "dS", 16)
                        p.wait(sy, "dS")

                @blk.tensor
                def _(t, ocp=ocp, ih=ih, buf=buf, cW=cW, c_ih=x2g[ih],
                      bf0=bankfree[0], bf1=bankfree[1]):
                    t.wait_ge(p.sem["dSa" if buf == 0 else "dSb"], cW)
                    t.wait_ge(p.sem["dT"], c_ih)
                    if p.taps == 2 and w == 0 and ocp == 0 and ih <= 1:
                        p.wait(t, "dS")
                    for sub in range(2):
                        if ih == 0:
                            p.wait(t, "vX", bf0 if sub == 0 else bf1)
                        ps = p.sb["PS_C"] if sub == 0 else p.sb["PS_D"]
                        i = None
                        for ic8 in range(8):
                            ic = ih * 8 + ic8
                            i = t.matmul(ps[:, 0:256],
                                         WTH[buf][:, ic8,
                                                  sub * 128:sub * 128 + 128],
                                         x2fm[:, ic, :], start=(ic == 0),
                                         stop=(ic == 63))
                        p.inc(i, "tS", 1)
                wt_cons3[buf] = p.cnt["tS"]
            cM = p.cnt["tS"]

            @blk.vector
            def _(v, w=w, ocp=ocp, cM=cM):
                for sub in range(2):
                    oc = ocp * 2 + sub
                    v.wait_ge(p.sem["tS"], cM - 1 + sub)
                    ps = p.sb["PS_C"] if sub == 0 else p.sb["PS_D"]
                    xo = x[:, oc, 2 * w * 128:(2 * w + 2) * 128]
                    p.inc(v.tensor_tensor(xo, ps[:, 0:256], xo, ALU.add), "vX", 1)
                bankfree[0] = p.cnt["vX"] - 1
                bankfree[1] = p.cnt["vX"]



MODES = ["full"]


def kernel(**inputs):
    per_core = [make_core_inputs(c, inputs) for c in range(CORES)]
    cur = [pc.pop("x_in") for pc in per_core]
    for mode in MODES:
        nc = build_program(mode)
        in_maps = []
        for c in range(CORES):
            m = dict(per_core[c])
            m["x_in"] = cur[c]
            in_maps.append(m)
        res = run_bass_kernel_spmd(nc, in_maps, core_ids=list(range(CORES)))
        cur = [np.asarray(res.results[c]["x_out"]) for c in range(CORES)]
    return assemble_output(cur)


# version 08
# speedup vs baseline: 1.0145x; 1.0145x over previous
import math
import numpy as np
import ml_dtypes
from contextlib import ExitStack

import concourse.bass as bass
import concourse.mybir as mybir
from concourse.bass_utils import run_bass_kernel_spmd

F32 = mybir.dt.float32
BF16 = mybir.dt.bfloat16
AF = mybir.ActivationFunctionType
ALU = mybir.AluOpType
AX = mybir.AxisListType.X

CORES = 8
T, C = 8192, 2048
HD = 128
B0 = 14.0
B1 = 14.0
SCALE = 1.0 / math.sqrt(HD)
CLIP = 128.0 - 1e-6
BF = ml_dtypes.bfloat16
RG = [list(range(CORES))]


def tiles_of_core(c):
    return [8 * j + 7 - c for j in range(8)]



def _rope_tables(positions):
    inv = 1.0 / 10000.0 ** (np.arange(0, HD, 2, dtype=np.float64) / HD)
    fr = positions.astype(np.float64)[None, :] * np.concatenate([inv, inv])[:, None]
    cos = np.cos(fr)
    sin = np.sin(fr)
    sin[:64] = -sin[:64]
    return cos.astype(np.float32), sin.astype(np.float32)


def make_core_inputs(c, z):
    x = z["L0_attn_x"][0]
    cols = np.concatenate([np.arange(128 * g, 128 * g + 128)
                           for g in tiles_of_core(c)])
    d = {}
    d["x_in"] = np.ascontiguousarray(np.asarray(x, np.float32)[cols, :].T)
    cos, sin = _rope_tables(cols)
    d["rope_cosq"] = np.ascontiguousarray(cos * SCALE)
    d["rope_sinq"] = np.ascontiguousarray(sin * SCALE)
    d["rope_cosk"] = cos
    d["rope_sink"] = sin
    m = np.zeros((128, 8, 8, 128), np.float32)
    tri = (np.arange(128)[:, None] <= np.arange(128)[None, :]).astype(np.float32)
    for j in range(8):
        Dj = 8 * (j + 1)
        dj = Dj - c
        for idx in range(8):
            pp = Dj - 8 + idx
            if pp < dj - 1:
                m[:, j, idx, :] = 1.0
            elif pp == dj - 1:
                m[:, j, idx, :] = tri
    d["mask0"] = np.ascontiguousarray(m.reshape(128, 8 * 8 * 128)).astype(BF)
    psw = np.zeros((128, 128), np.float32)
    psw[(np.arange(128) + 64) % 128, np.arange(128)] = 1.0
    d["pswap"] = psw.astype(BF)
    cc = np.zeros((128, 8), np.float32)
    cc[:, 0] = -B0
    cc[:, 1] = -B1
    cc[:, 2] = 1e-5
    cc[:, 3] = 1e-11
    cc[:, 4] = 1e-6
    d["cconst"] = cc
    d["ones_bf"] = np.ones((128, 1), np.float32).astype(BF)
    d["ones_row"] = np.ones((1, 128), np.float32)
    r0, r1 = 256 * c, 256 * (c + 1)
    d["wq_s"] = np.ascontiguousarray(z["L0_attn_Wq"][r0:r1])
    d["wk_s"] = np.ascontiguousarray(z["L0_attn_Wk"][r0:r1])
    d["wv_s"] = np.ascontiguousarray(z["L0_attn_Wv"][r0:r1])
    d["wproj_s"] = np.ascontiguousarray(z["L0_attn_Wproj"][r0:r1])
    d["gup_s"] = np.ascontiguousarray(
        z["L0_ffn_gate_up_proj"][:, r0:r1].reshape(4 * 256, 4096))
    d["down_s"] = np.ascontiguousarray(
        z["L0_ffn_down_proj"][:, r0:r1].reshape(4 * 256, 2048))
    d["wqkv_s"] = np.ascontiguousarray(z["L1_attn_w_qkv"][:, r0:r1])
    d["wout_s"] = np.ascontiguousarray(z["L1_attn_w_out"][:, r0:r1])
    d["wfc_s"] = np.ascontiguousarray(z["L1_ffn_w_fc"][:, r0:r1])
    d["wproj2_s"] = np.ascontiguousarray(
        z["L1_ffn_w_proj"][:, 1024 * c:1024 * (c + 1)])
    return {k: np.asarray(v) for k, v in d.items()}


def assemble_output(xs):
    out = np.empty((1, T, C), np.float32)
    for c in range(CORES):
        for j, g in enumerate(tiles_of_core(c)):
            out[0, 128 * g:128 * g + 128, :] = xs[c][:, 128 * j:128 * j + 128].T
    return out



class P:
    def __init__(self, nc, es, mode):
        self.nc, self.es, self.mode = nc, es, mode
        self.cnt, self.sem, self.sb, self.dr, self.ccmark = {}, {}, {}, {}, {}

    def mksem(self, *names):
        for n in names:
            self.sem[n] = self.es.enter_context(self.nc.semaphore(n))
            self.cnt[n] = 0

    def inc(self, instr, name, n=1):
        instr.then_inc(self.sem[name], n)
        self.cnt[name] += n
        return self.cnt[name]

    def wait(self, eng, name, val=None):
        v = self.cnt[name] if val is None else val
        if v > 0:
            eng.wait_ge(self.sem[name], v)

    def sbuf(self, name, shape, dtype):
        h = self.es.enter_context(self.nc.sbuf_tensor("sb_" + name, shape, dtype))
        self.sb[name] = h
        return h

    def psum(self, name, shape, dtype=F32):
        h = self.es.enter_context(self.nc.psum_tensor(name, shape, dtype))
        self.sb[name] = h
        return h

    def dram(self, name, shape, dtype, kind=None):
        h = (self.nc.dram_tensor(name, shape, dtype, kind=kind) if kind
             else self.nc.dram_tensor(name, shape, dtype))
        self.dr[name] = h
        return h

    def joinall(self, eng,
                names=("dS", "dG", "dT", "tMM", "tS", "aE", "aX", "vM", "vX")):
        for n in names:
            self.wait(eng, n)


ALLSEMS = ["dS", "dSa", "dSb", "dG", "dT", "dTa", "dTb", "cc", "tMM", "tS", "aE", "aX", "vM", "vX", "st"]


def build_program(mode, taps=False):
    nc = bass.Bass("TRN2", target_bir_lowering=False)
    es = ExitStack()
    p = P(nc, es, mode)
    p.taps = taps
    with es:
        _declare_io(p)
        _declare_sbuf(p)
        p.mksem(*ALLSEMS)
        p.psum("PS_A", [128, 1024])
        p.psum("PS_B", [128, 1024])
        p.psum("PS_C", [128, 512])
        p.psum("PS_D", [128, 512])
        p.psum("PS_E", [128, 512])
        p.psum("PS_F", [128, 512])
        p.blk = es.enter_context(nc.Block())

        _emit_loads(p)
        _emit_prep1(p)
        if mode in ("l0a", "full"):
            _emit_l0_qkv(p)
            _emit_l0_attn(p)
            _emit_proj(p, layer=0)
        _emit_prep2a(p)
        if mode in ("l0f", "full"):
            _emit_l0_ffn(p)
        _emit_prep2b(p)
        if mode in ("l1a", "full"):
            _emit_l1_qkv(p)
        _emit_prep2c(p)
        if mode in ("l1a", "full"):
            _emit_l1_attn(p)
            _emit_proj(p, layer=1)
        if mode in ("l1f", "full"):
            _emit_l1_ffn(p)
        _emit_store(p)
    return nc


def _declare_io(p):
    p.dram("x_in", [C, 1024], F32, kind="ExternalInput")
    p.dram("x_out", [C, 1024], F32, kind="ExternalOutput")
    for nm, sh in [("rope_cosq", [128, 1024]), ("rope_sinq", [128, 1024]),
                   ("rope_cosk", [128, 1024]), ("rope_sink", [128, 1024]),
                   ("cconst", [128, 8]), ("ones_row", [1, 128])]:
        p.dram(nm, sh, F32, kind="ExternalInput")
    for nm, sh in [("mask0", [128, 8 * 8 * 128]), ("pswap", [128, 128]),
                   ("ones_bf", [128, 1])]:
        p.dram(nm, sh, BF16, kind="ExternalInput")
    for nm, sh in [("wq_s", [256, 2048]), ("wk_s", [256, 512]), ("wv_s", [256, 512]),
                   ("wproj_s", [256, 2048]), ("gup_s", [1024, 4096]),
                   ("down_s", [1024, 2048]), ("wqkv_s", [6144, 256]),
                   ("wout_s", [2048, 256]), ("wfc_s", [8192, 256]),
                   ("wproj2_s", [2048, 1024])]:
        p.dram(nm, sh, F32, kind="ExternalInput")
    for nm, csh, gsh in [
        ("wq", [256, 2048], [2048, 2048]), ("wk", [256, 512], [2048, 512]),
        ("wv", [256, 512], [2048, 512]), ("wproj", [256, 2048], [2048, 2048]),
        ("gup", [1024, 4096], [8192, 4096]), ("down", [1024, 2048], [8192, 2048]),
        ("wqkvT", [256, 6144], [2048, 6144]), ("woutT", [256, 2048], [2048, 2048]),
        ("wfcT", [256, 8192], [2048, 8192]), ("wprojT", [1024, 2048], [8192, 2048]),
        ("k0", [512, 1024], [4096, 1024]), ("v0", [1024, 512], [8192, 512]),
    ]:
        p.dram(nm + "_c", csh, BF16)
        p.dram(nm + "_g", gsh, BF16)
    for hg in range(4):
        p.dram(f"k1_c{hg}", [512, 1024], BF16)
        p.dram(f"k1_g{hg}", [4096, 1024], BF16)
        p.dram(f"v1_c{hg}", [1024, 512], BF16)
        p.dram(f"v1_g{hg}", [8192, 512], BF16)
    p.dram("wqkv_bf", [6144, 256], BF16)
    p.dram("wout_bf", [2048, 256], BF16)
    p.dram("wfc_bf", [8192, 256], BF16)
    p.dram("wproj2_bf", [2048, 1024], BF16)
    p.dram("gam_c", [128, 2], F32)
    p.dram("gam_g", [128, 2], F32)
    if p.taps == 2:
        p.dram("gam_dbg", [128, 2], F32, kind="ExternalOutput")
        p.dram("wfcT_dbg", [2048, 8192], BF16, kind="ExternalOutput")
        p.dram("wprojT_dbg", [8192, 2048], BF16, kind="ExternalOutput")
        p.dram("xhat_dbg", [1024, 2048], BF16, kind="ExternalOutput")
        p.dram("h0_dbg", [128, 8192], BF16, kind="ExternalOutput")
        p.dram("xh2_dbg", [256, 8192], BF16, kind="ExternalOutput")
        p.dram("xbf_dbg", [2048, 1024], BF16, kind="ExternalOutput")
        p.dram("sm_dbg", [128, 16], F32, kind="ExternalOutput")
        p.dram("xt0_dbg", [128, 2048], BF16, kind="ExternalOutput")
        p.dram("x2fm_dbg", [128, 16384], BF16, kind="ExternalOutput")
        p.dram("wth_dbg", [128, 4096], BF16, kind="ExternalOutput")
    if p.taps == 1:
        p.dram("xln_dbg", [128, 16384], BF16, kind="ExternalOutput")
        p.dram("q_dbg", [128, 16384], BF16, kind="ExternalOutput")
        p.dram("k0g_dbg", [4096, 1024], BF16, kind="ExternalOutput")
        p.dram("v0g_dbg", [8192, 512], BF16, kind="ExternalOutput")
        p.dram("ob_dbg", [128, 16384], BF16, kind="ExternalOutput")
    p.dram("xbf_d", [2048, 1024], BF16)
    p.dram("xhat_d", [1024, 2048], BF16)
    p.dram("xh2_d", [256, 8192], BF16)


def _declare_sbuf(p):
    p.sbuf("x", [128, 16, 1024], F32)
    p.sbuf("QB", [128, 16, 1024], BF16)
    p.sbuf("OB", [128, 16, 1024], BF16)
    p.sbuf("U1", [128, 16, 1024], BF16)
    p.sbuf("AR1", [128, 16384], BF16)
    p.sbuf("WT", [128, 2, 16, 128], BF16)
    p.sbuf("TMP", [128, 1024], F32)
    p.sbuf("statm", [128, 16], F32)
    p.sbuf("cconst", [128, 8], F32)
    p.sbuf("pswap", [128, 128], BF16)
    p.sbuf("ones_bf", [128, 1], BF16)
    p.sbuf("ones_row", [1, 128], F32)
    p.sbuf("grow", [1, 16], F32)



def V_ROPE(p):
    return p.sb["AR1"][:, 0:8192].bitcast(F32).rearrange("p (a n) -> p a n", n=1024)

def V_RMAT(p):
    return p.sb["AR1"][:, 8192:10240].bitcast(F32)

def V_RTMP(p):
    return p.sb["AR1"][:, 10240:12288].bitcast(F32)

def V_KS(p):
    return p.sb["AR1"][:, 0:8192].rearrange("p (t n) -> p t n", n=128)

def V_VS(p):
    return p.sb["AR1"][:, 8192:16384].rearrange("p (t n) -> p t n", n=128)

def V_MSK(p):
    return p.sb["U1"][:, 0:8, :].rearrange("p a n -> p (a n)").rearrange(
        "p (t n) -> p t n", n=128)

def V_PT(p, i):
    return p.sb["U1"][:, 8 + i, :]

def V_XLN(p):
    return p.sb["U1"]

def V_KC(p):
    return p.sb["OB"][:, 0:4, :]

def V_VC(p):
    return p.sb["OB"][:, 4:8, :].rearrange("p a n -> p (a n)").rearrange(
        "p (a n) -> p a n", n=512)

def V_QL0(p):
    return p.sb["QB"][:, :, :].rearrange("p a n -> p (a n)").rearrange(
        "p (k s h n) -> p k s h n", k=4, s=8, h=4)



def _emit_loads(p):
    blk = p.blk

    @blk.sync
    def _(sy):
        p.inc(sy.dma_start(
            out=p.sb["x"][:, :, :],
            in_=p.dr["x_in"].ap().rearrange("(a p) n -> p a n", p=128)), "dS", 16)
        for nm in ("cconst", "ones_row", "pswap", "ones_bf"):
            p.inc(sy.dma_start(out=p.sb[nm][:, :], in_=p.dr[nm].ap()), "dS", 16)
        rope = V_ROPE(p)
        if p.mode in ("l0a", "full"):
            for i, nm in enumerate(["rope_cosq", "rope_sinq",
                                    "rope_cosk", "rope_sink"]):
                p.inc(sy.dma_start(out=rope[:, i, :], in_=p.dr[nm].ap()), "dS", 16)
    p.c_xload = p.cnt["dS"]


def _emit_store(p):
    blk = p.blk

    @blk.sync
    def _(sy):
        p.joinall(sy)
        p.inc(sy.dma_start(
            out=p.dr["x_out"].ap().rearrange("(a p) n -> p a n", p=128),
            in_=p.sb["x"][:, :, :]), "st", 16)
        p.wait(sy, "st")



def _emit_prep1(p):
    blk, m = p.blk, p.mode
    names = []
    if m in ("l0a", "full"):
        names += ["wk", "wv", "wq", "wproj"]
    if m in ("l0f", "full"):
        names += ["gup", "down"]
    need_l1a = m in ("l1a", "full")

    early = [nm for nm in names if nm in ("wk", "wv", "wq")] \
        if p.mode == "full" else names
    p.deferred_ags = [nm for nm in names if nm not in early]

    @blk.gpsimd
    def _(g):
        for nm in names:
            i = p.inc(g.dma_start(out=p.dr[nm + "_c"].ap(),
                                  in_=p.dr[nm + "_s"].ap()), "dG", 16)
        g.wait_ge(p.sem["dG"], p.cnt["dG"])
        for nm in early:
            p.inc(g.collective_compute(
                "AllGather", ALU.bypass, replica_groups=RG,
                ins=[p.dr[nm + "_c"].ap()], outs=[p.dr[nm + "_g"].ap()]), "cc", 1)
            p.ccmark[nm] = p.cnt["cc"]
        if need_l1a:
            for src, bnc in [("wqkv_s", "wqkv_bf"), ("wout_s", "wout_bf")]:
                p.inc(g.dma_start(out=p.dr[bnc].ap(), in_=p.dr[src].ap()), "dG", 16)
    c_cast = p.cnt["dG"]

    if need_l1a:
        qbf = p.sb["QB"][:, :, :].rearrange("p a n -> p (a n)")

        @blk.scalar
        def _(s):
            s.wait_ge(p.sem["dG"], c_cast)
            for bnc, contrib, nrows in [("wqkv_bf", "wqkvT_c", 6144),
                                        ("wout_bf", "woutT_c", 2048)]:
                for ci in range(2):
                    p.inc(s.dma_start(out=qbf[:, 0:nrows],
                                      in_=p.dr[bnc].ap()[:, ci * 128:(ci + 1) * 128],
                                      transpose=True), "dT", 16)
                    s.wait_ge(p.sem["dT"], p.cnt["dT"])
                    p.inc(s.dma_start(
                        out=p.dr[contrib].ap()[ci * 128:(ci + 1) * 128, :],
                        in_=qbf[:, 0:nrows]), "dT", 16)
                    s.wait_ge(p.sem["dT"], p.cnt["dT"])

        if p.mode == "full":
            p.deferred_ags += ["wqkvT", "woutT"]
            p.deferred_dt = p.cnt["dT"]
        else:
            @blk.gpsimd
            def _(g):
                g.wait_ge(p.sem["dT"], p.cnt["dT"])
                for nm in ("wqkvT", "woutT"):
                    p.inc(g.collective_compute(
                        "AllGather", ALU.bypass, replica_groups=RG,
                        ins=[p.dr[nm + "_c"].ap()], outs=[p.dr[nm + "_g"].ap()]),
                        "cc", 1)
                    p.ccmark[nm] = p.cnt["cc"]
    p.c_prep1_dT = p.cnt["dT"]


def _emit_prep2a(p):
    if p.mode not in ("l1f", "full"):
        return
    blk = p.blk
    obf = p.sb["OB"][:, :, :].rearrange("p a n -> p (a n)")
    obq = obf[:, 0:8192].bitcast(F32)
    sm = p.sb["statm"]
    quarters = [("wfc_s", q, 2048, 256, 0) for q in range(4)] + \
               [("wproj2_s", q, 512, 1024, 1) for q in range(4)]
    for src_, q, rows, ncol, mi in quarters:
        fcols = rows * ncol // 128

        @blk.sync
        def _(sy, src_=src_, q=q, rows=rows, ncol=ncol, fcols=fcols, mi=mi):
            if mi == 0 and q == 0:
                p.joinall(sy, ("vX", "aE", "aX", "tS"))
            else:
                p.wait(sy, "vX")
            src_ap = p.dr[src_].ap()[q * rows:(q + 1) * rows, :]
            u3 = obq[:, 0:fcols].rearrange("p (a n) -> p a n", n=ncol)
            p.inc(sy.dma_start(out=u3,
                               in_=src_ap.rearrange("(a p) n -> p a n", p=128)),
                  "dS", 16)
        cL = p.cnt["dS"]

        @blk.vector
        def _(v, cL=cL, fcols=fcols, q=q, mi=mi):
            v.wait_ge(p.sem["dS"], cL)
            p.inc(v.reduce_sum(sm[:, q:q + 1], obq[:, 0:fcols], AX,
                               apply_absolute_value=True), "vX", 1)
            if q == 3:
                v.wait_ge(p.sem["vX"], p.cnt["vX"])
                p.inc(v.tensor_tensor(sm[:, 6:7], sm[:, 0:1], sm[:, 1:2],
                                      ALU.add), "vX", 1)
                v.wait_ge(p.sem["vX"], p.cnt["vX"])
                p.inc(v.tensor_tensor(sm[:, 7:8], sm[:, 2:3], sm[:, 3:4],
                                      ALU.add), "vX", 1)
                v.wait_ge(p.sem["vX"], p.cnt["vX"])
                p.inc(v.tensor_tensor(sm[:, 4 + mi:5 + mi], sm[:, 6:7],
                                      sm[:, 7:8], ALU.add), "vX", 1)
    @blk.vector
    def _(v):
        p.inc(v.memset(sm[:, 15:16], 1.0), "vX", 1)
    c_comb = p.cnt["vX"]

    @blk.gpsimd
    def _(g):
        g.wait_ge(p.sem["vX"], c_comb)
        i = p.inc(g.dma_start(out=p.dr["gam_c"].ap(), in_=sm[:, 4:6]),
                  "dG", 16)
        g.wait_ge(p.sem["dG"], i)
        p.inc(g.collective_compute(
            "AllReduce", ALU.add, replica_groups=RG,
            ins=[p.dr["gam_c"].ap()], outs=[p.dr["gam_g"].ap()]), "cc", 1)
        p.ccmark["gam"] = p.cnt["cc"]
    p.c_p2a = p.cnt["vX"]


def _emit_prep2b(p):
    if p.mode not in ("l1f", "full"):
        return
    blk = p.blk
    obf = p.sb["OB"][:, :, :].rearrange("p a n -> p (a n)")
    obq = obf[:, 0:8192].bitcast(F32)
    obo = obf[:, 8192:12288]
    sm = p.sb["statm"]

    @blk.sync
    def _(sy):
        p.wait(sy, "cc", p.ccmark["gam"])
        p.inc(sy.dma_start(out=sm[:, 12:14], in_=p.dr["gam_g"].ap()),
              "dG", 16)
    c_gl = p.cnt["dG"]
    c_vtail = p.cnt["vX"]

    @blk.tensor
    def _(t):
        t.wait_ge(p.sem["dG"], c_gl)
        t.wait_ge(p.sem["vX"], c_vtail)
        p.inc(t.matmul(p.sb["PS_F"][0:1, 0:2], sm[:, 15:16], sm[:, 12:14],
                       start=True, stop=True), "tMM", 1)
    c_gm = p.cnt["tMM"]

    @blk.vector
    def _(v):
        v.wait_ge(p.sem["tMM"], c_gm)
        p.inc(v.tensor_copy(p.sb["grow"][:, 4:6], p.sb["PS_F"][0:1, 0:2]),
              "vX", 1)
        v.wait_ge(p.sem["vX"], p.cnt["vX"])
        p.inc(v.tensor_scalar(p.sb["grow"][:, 8:10], p.sb["grow"][:, 4:6],
                              1.0 / (8192.0 * 2048.0), 1e-6, ALU.mult, ALU.add),
              "vX", 1)
        v.wait_ge(p.sem["vX"], p.cnt["vX"])
        v.tensor_scalar_mul(p.sb["grow"][:, 12:14], p.sb["grow"][:, 4:6],
                            1.0 / (8192.0 * 2048.0))
        p.inc(v.reciprocal(p.sb["grow"][:, 10:12], p.sb["grow"][:, 8:10]), "vX", 1)
    c_gv = p.cnt["vX"]

    @blk.tensor
    def _(t):
        t.wait_ge(p.sem["vX"], c_gv)
        p.inc(t.matmul(p.sb["PS_F"][:, 0:4], p.sb["ones_row"][:, :],
                       p.sb["grow"][0:1, 10:14], start=True, stop=True), "tMM", 1)
    c_gb = p.cnt["tMM"]

    @blk.vector
    def _(v):
        v.wait_ge(p.sem["tMM"], c_gb)
        p.inc(v.tensor_copy(sm[:, 8:12], p.sb["PS_F"][:, 0:4]), "vX", 1)
    p.c_gam = p.cnt["vX"]

    for src_, bnc, q, rows, ncol, icol, gcol in [
        ("wfc_s", "wfc_bf", 0, 2048, 256, 8, 10),
        ("wfc_s", "wfc_bf", 1, 2048, 256, 8, 10),
        ("wfc_s", "wfc_bf", 2, 2048, 256, 8, 10),
        ("wfc_s", "wfc_bf", 3, 2048, 256, 8, 10),
        ("wproj2_s", "wproj2_bf", 0, 512, 1024, 9, 11),
        ("wproj2_s", "wproj2_bf", 1, 512, 1024, 9, 11),
        ("wproj2_s", "wproj2_bf", 2, 512, 1024, 9, 11),
        ("wproj2_s", "wproj2_bf", 3, 512, 1024, 9, 11),
    ]:
        fcols = rows * ncol // 128

        @blk.sync
        def _(sy, src_=src_, q=q, rows=rows, ncol=ncol, fcols=fcols, mi=icol):
            if mi == 8 and q == 0:
                p.wait(sy, "vX", p.c_p2a)
            else:
                p.wait(sy, "vX")
            src_ap = p.dr[src_].ap()[q * rows:(q + 1) * rows, :]
            u3 = obq[:, 0:fcols].rearrange("p (a n) -> p a n", n=ncol)
            p.inc(sy.dma_start(out=u3,
                               in_=src_ap.rearrange("(a p) n -> p a n", p=128)),
                  "dS", 16)
        cL = p.cnt["dS"]

        @blk.vector
        def _(v, cL=cL, fcols=fcols, icol=icol, gcol=gcol):
            v.wait_ge(p.sem["dS"], cL)
            p.wait(v, "vX", p.c_gam)
            tmp = p.sb["TMP"][:, 0:512]
            rm = p.sb["TMP"][:, 512:1024]
            i = None
            for off in range(0, fcols, 512):
                w = min(512, fcols - off)
                sl = obq[:, off:off + w]
                v.tensor_scalar_mul(tmp[:, :w], sl, sm[:, icol:icol + 1])
                v.tensor_scalar(rm[:, :w], tmp[:, :w], 0.5, None, ALU.is_gt)
                v.tensor_scalar(tmp[:, :w], tmp[:, :w], -0.5, None, ALU.is_lt)
                v.tensor_tensor(rm[:, :w], rm[:, :w], tmp[:, :w], ALU.subtract)
                i = v.tensor_scalar_mul(obo[:, off:off + w], rm[:, :w],
                                        sm[:, gcol:gcol + 1])
            p.inc(i, "vX", 1)
        cT = p.cnt["vX"]

        @blk.sync
        def _(sy, bnc=bnc, q=q, rows=rows, ncol=ncol, fcols=fcols, cT=cT):
            sy.wait_ge(p.sem["vX"], cT)
            dst = p.dr[bnc].ap()[q * rows:(q + 1) * rows, :]
            ob3 = obo[:, 0:fcols].rearrange("p (a n) -> p a n", n=ncol)
            p.inc(sy.dma_start(out=dst.rearrange("(a p) n -> p a n", p=128),
                               in_=ob3), "dS", 16)
    c_bf = p.cnt["dS"]

    @blk.scalar
    def _(s):
        s.wait_ge(p.sem["dS"], c_bf)
        for bnc, contrib, nrows, ncols in [("wfc_bf", "wfcT_c", 8192, 256),
                                           ("wproj2_bf", "wprojT_c", 2048, 1024)]:
            for ci in range(ncols // 128):
                p.inc(s.dma_start(out=obf[:, 0:nrows],
                                  in_=p.dr[bnc].ap()[:, ci * 128:(ci + 1) * 128],
                                  transpose=True), "dT", 16)
                s.wait_ge(p.sem["dT"], p.cnt["dT"])
                p.inc(s.dma_start(
                    out=p.dr[contrib].ap()[ci * 128:(ci + 1) * 128, :],
                    in_=obf[:, 0:nrows]), "dT", 16)
                s.wait_ge(p.sem["dT"], p.cnt["dT"])
    p.c_prep2_dT = p.cnt["dT"]


def _emit_prep2c(p):
    if p.mode not in ("l1f", "full"):
        return
    blk = p.blk

    @blk.gpsimd
    def _(g):
        g.wait_ge(p.sem["dT"], p.c_prep2_dT)
        for nm in ("wfcT", "wprojT"):
            p.inc(g.collective_compute(
                "AllGather", ALU.bypass, replica_groups=RG,
                ins=[p.dr[nm + "_c"].ap()], outs=[p.dr[nm + "_g"].ap()]), "cc", 1)
            p.ccmark[nm] = p.cnt["cc"]



def _emit_ln_stream(p, tag):
    blk = p.blk
    x = p.sb["x"]
    XLN = V_XLN(p)
    wtf = p.sb["WT"][:, :, :, :].rearrange("p b a n -> p (b a n)")
    wsets = [(wtf[:, 0:1024], wtf[:, 1024:2048]),
             (wtf[:, 2048:3072], wtf[:, 3072:4096])]
    PS_A, PS_B = p.sb["PS_A"], p.sb["PS_B"]
    TMP = p.sb["TMP"]
    RM = p.sb["U1"][:, :, :].rearrange("p a n -> p (a n)").bitcast(F32)[:, 0:1024]

    mm_cnt = [0, 0]
    for ic in range(16):
        wta, wtb = wsets[ic % 2]

        @blk.vector
        def _(v, ic=ic, wta=wta, wtb=wtb):
            if ic == 0:
                p.joinall(v)
            elif ic == 1:
                pass
            else:
                p.wait(v, "tS", mm_cnt[ic % 2])
            v.tensor_copy(wta[:, :], x[:, ic, :])
            p.inc(v.tensor_tensor(wtb[:, :], x[:, ic, :], x[:, ic, :], ALU.mult),
                  "vX", 1)
        cV = p.cnt["vX"]

        @blk.tensor
        def _(t, ic=ic, cV=cV, wta=wta, wtb=wtb):
            t.wait_ge(p.sem["vX"], cV)
            i = None
            for tc in range(2):
                t.matmul(PS_A[0:1, tc * 512:(tc + 1) * 512], p.sb["ones_bf"][:, :],
                         wta[:, tc * 512:(tc + 1) * 512],
                         start=(ic == 0), stop=(ic == 15))
                i = t.matmul(PS_B[0:1, tc * 512:(tc + 1) * 512],
                             p.sb["ones_bf"][:, :],
                             wtb[:, tc * 512:(tc + 1) * 512],
                             start=(ic == 0), stop=(ic == 15))
            p.inc(i, "tS", 1)
        mm_cnt[ic % 2] = p.cnt["tS"]
    c_mm = p.cnt["tS"]

    @blk.vector
    def _(v):
        v.wait_ge(p.sem["tS"], c_mm)
        p.inc(v.tensor_scalar_mul(TMP[0:1, :], PS_A[0:1, :], 1.0 / 2048.0),
              "vX", 1)
        v.wait_ge(p.sem["vX"], p.cnt["vX"])
        p.inc(v.tensor_tensor(RM[0:1, :], TMP[0:1, :], TMP[0:1, :], ALU.mult),
              "vX", 1)
        v.wait_ge(p.sem["vX"], p.cnt["vX"])
        p.inc(v.scalar_tensor_tensor(RM[0:1, :], PS_B[0:1, :], 1.0 / 2048.0,
                                     RM[0:1, :], ALU.mult, ALU.subtract), "vX", 1)
    c_v1 = p.cnt["vX"]

    @blk.scalar
    def _(s):
        s.wait_ge(p.sem["vX"], c_v1)
        p.inc(s.activation(RM[0:1, :], RM[0:1, :], AF.Sqrt,
                           bias=p.sb["cconst"][0:1, 2:3]), "aX", 1)
    c_sq = p.cnt["aX"]

    @blk.vector
    def _(v):
        v.wait_ge(p.sem["aX"], c_sq)
        p.inc(v.reciprocal(RM[0:1, :], RM[0:1, :]), "vX", 1)
        p.inc(v.tensor_scalar_mul(TMP[0:1, :], TMP[0:1, :], -1.0), "vX", 1)
    c_v2 = p.cnt["vX"]

    @blk.tensor
    def _(t):
        t.wait_ge(p.sem["vX"], c_v2)
        i = None
        for tc in range(2):
            t.matmul(PS_A[:, tc * 512:(tc + 1) * 512], p.sb["ones_row"][:, :],
                     TMP[0:1, tc * 512:(tc + 1) * 512], start=True, stop=True)
            i = t.matmul(PS_B[:, tc * 512:(tc + 1) * 512], p.sb["ones_row"][:, :],
                         RM[0:1, tc * 512:(tc + 1) * 512], start=True, stop=True)
        p.inc(i, "tS", 1)
    c_r1 = p.cnt["tS"]

    ln_cnts = []

    @blk.vector
    def _(v, ln_cnts=ln_cnts):
        v.wait_ge(p.sem["tS"], c_r1)
        for ic in range(16):
            v.tensor_tensor(TMP[:, :], x[:, ic, :], PS_A[:, :], ALU.add)
            p.inc(v.tensor_tensor(XLN[:, ic, :], TMP[:, :], PS_B[:, :],
                                  ALU.mult), "vX", 1)
            ln_cnts.append(p.cnt["vX"])
    setattr(p, f"c_ln_{tag}", p.cnt["vX"])
    setattr(p, f"ln_cnts_{tag}", ln_cnts)



def _wtile_dma(p, sy, wg_name, oc, buf, colw=128, sem=None):
    sem = sem or ("dSa" if buf == 0 else "dSb")
    src = p.dr[wg_name].ap()[:, oc * colw:(oc + 1) * colw]
    src = src.rearrange("(a p) n -> p a n", p=128)
    p.inc(sy.dma_start(out=p.sb["WT"][:, buf, 0:8, 0:colw], in_=src[:, 0:8, :]),
          sem, 16)
    p.inc(sy.dma_start(out=p.sb["WT"][:, buf, 8:16, 0:colw], in_=src[:, 8:16, :]),
          sem, 16)
    return p.cnt[sem]


def _emit_l0_qkv(p):
    blk = p.blk
    _emit_ln_stream(p, "l0a")
    c_ln = p.c_ln_l0a
    XLN = V_XLN(p)
    rope = V_ROPE(p)
    PS_A, PS_B = p.sb["PS_A"], p.sb["PS_B"]
    QL0 = V_QL0(p)
    rope_cnt = []

    chunks = [("wq_g", oc, "q") for oc in range(16)] + \
             [("wk_g", oc, "k") for oc in range(4)]
    WTP = p.sb["WT"][:, :, :, :].rearrange("p b a n -> p (b a n)").rearrange(
        "p (a n) -> p a n", n=256)
    pair_cons = [p.cnt["tS"]]
    for idx, (wg, oc, kind) in enumerate(chunks):
        half = idx % 2
        qraw = p.sb["OB"][:, 8 + half, :]

        if half == 0:
            @blk.sync
            def _(sy, wg=wg, oc=oc):
                p.wait(sy, "cc", p.ccmark["wq" if wg == "wq_g" else "wk"])
                p.wait(sy, "tS", pair_cons[0])
                srcp = p.dr[wg].ap()[:, oc * 128:oc * 128 + 256]
                srcp = srcp.rearrange("(a p) n -> p a n", p=128)
                p.inc(sy.dma_start(out=WTP[:, :, :], in_=srcp), "dSa", 16)
            cW = p.cnt["dSa"]
            p._cW_pair = cW
        else:
            cW = p._cW_pair

        @blk.tensor
        def _(t, half=half, cW=cW, idx=idx):
            t.wait_ge(p.sem["dSa"], cW)
            t.wait_ge(p.sem["vX"], c_ln)
            if idx >= 1:
                p.wait(t, "aE")
            i = None
            for tc in range(2):
                for ic in range(16):
                    i = t.matmul(PS_A[:, tc * 512:(tc + 1) * 512],
                                 WTP[:, ic, half * 128:half * 128 + 128],
                                 XLN[:, ic, tc * 512:(tc + 1) * 512],
                                 start=(ic == 0), stop=(ic == 15))
            p.inc(i, "tS", 1)
        if half == 1:
            pair_cons[0] = p.cnt["tS"]
        cQ = p.cnt["tS"]

        @blk.scalar
        def _(s, cQ=cQ, qraw=qraw, idx=idx):
            s.wait_ge(p.sem["tS"], cQ)
            if idx >= 2:
                p.wait(s, "vX", rope_cnt[idx - 2])
            p.inc(s.copy(qraw[:, :], PS_A[:, :]), "aE", 1)
        cE = p.cnt["aE"]

        @blk.tensor
        def _(t, cE=cE, qraw=qraw, idx=idx):
            t.wait_ge(p.sem["aE"], cE)
            if idx >= 1:
                p.wait(t, "vX", rope_cnt[idx - 1])
            i = None
            for tc in range(2):
                i = t.matmul(PS_B[:, tc * 512:(tc + 1) * 512], p.sb["pswap"][:, :],
                             qraw[:, tc * 512:(tc + 1) * 512],
                             start=True, stop=True)
            p.inc(i, "tS", 1)
        cS = p.cnt["tS"]

        @blk.vector
        def _(v, oc=oc, kind=kind, cS=cS, qraw=qraw):
            v.wait_ge(p.sem["tS"], cS)
            ct = 0 if kind == "q" else 2
            st = 1 if kind == "q" else 3
            TMP, RT = p.sb["TMP"], V_RTMP(p)
            v.tensor_tensor(TMP[:, :], qraw[:, :], rope[:, ct, :], ALU.mult)
            v.tensor_tensor(RT[:, :], PS_B[:, :], rope[:, st, :], ALU.mult)
            if kind == "q":
                out = QL0[:, oc // 4, :, oc % 4, :]
                t3 = TMP[:, :].rearrange("p (s n) -> p s n", n=128)
                r3 = RT[:, :].rearrange("p (s n) -> p s n", n=128)
                p.inc(v.tensor_tensor(out, t3, r3, ALU.add), "vX", 1)
            else:
                p.inc(v.tensor_tensor(V_KC(p)[:, oc, :], TMP[:, :], RT[:, :],
                                      ALU.add), "vX", 1)
        rope_cnt.append(p.cnt["vX"])
    c_qk = p.cnt["vX"]

    v_cons = [p.cnt["tS"]]
    for vh in range(4):
        half = vh % 2

        if half == 0:
            @blk.sync
            def _(sy, vh=vh):
                p.wait(sy, "cc", p.ccmark["wv"])
                p.wait(sy, "tS", v_cons[0])
                srcp = p.dr["wv_g"].ap()[:, vh * 128:vh * 128 + 256]
                srcp = srcp.rearrange("(a p) n -> p a n", p=128)
                p.inc(sy.dma_start(out=WTP[:, :, :], in_=srcp), "dSa", 16)
            cW = p.cnt["dSa"]
            p._cW_pair = cW
        else:
            cW = p._cW_pair
        aE_base = p.cnt["aE"]

        @blk.tensor
        def _(t, vh=vh, half=half, cW=cW, aE_base=aE_base):
            t.wait_ge(p.sem["dSa"], cW)
            t.wait_ge(p.sem["vX"], c_ln)
            for tt in range(8):
                ps = p.sb["PS_C"] if tt % 2 == 0 else p.sb["PS_D"]
                if vh > 0 or tt >= 2:
                    t.wait_ge(p.sem["aE"], aE_base + tt - 2 + 1
                              if tt >= 2 else aE_base)
                i = None
                for ic in range(16):
                    i = t.matmul(ps[:, 0:128], XLN[:, ic, tt * 128:(tt + 1) * 128],
                                 WTP[:, ic, half * 128:half * 128 + 128],
                                 start=(ic == 0), stop=(ic == 15))
                p.inc(i, "tS", 1)
        if half == 1:
            v_cons[0] = p.cnt["tS"]
        cV8 = p.cnt["tS"]

        @blk.scalar
        def _(s, vh=vh, cV8=cV8):
            for tt in range(8):
                ps = p.sb["PS_C"] if tt % 2 == 0 else p.sb["PS_D"]
                s.wait_ge(p.sem["tS"], cV8 - 7 + tt)
                p.inc(s.copy(V_VC(p)[:, tt, vh * 128:(vh + 1) * 128],
                             ps[:, 0:128]), "aE", 1)
    c_v = p.cnt["aE"]

    @blk.sync
    def _(sy):
        p.wait(sy, "vX", c_qk)
        p.inc(sy.dma_start(
            out=p.dr["k0_c"].ap().rearrange("(a p) n -> p a n", p=128),
            in_=V_KC(p)[:, :, :]), "dS", 16)
        p.wait(sy, "aE", c_v)
        p.inc(sy.dma_start(
            out=p.dr["v0_c"].ap().rearrange("(a p) n -> p a n", p=128),
            in_=V_VC(p)[:, :, :]), "dS", 16)
    c_ctr = p.cnt["dS"]

    @blk.gpsimd
    def _(g):
        g.wait_ge(p.sem["dS"], c_ctr)
        for nm in ("k0", "v0"):
            p.inc(g.collective_compute(
                "AllGather", ALU.bypass, replica_groups=RG,
                ins=[p.dr[nm + "_c"].ap()], outs=[p.dr[nm + "_g"].ap()]), "cc", 1)
            p.ccmark[nm] = p.cnt["cc"]
        for nm in getattr(p, "deferred_ags", []):
            if nm in ("wqkvT", "woutT"):
                g.wait_ge(p.sem["dT"], p.deferred_dt)
            p.inc(g.collective_compute(
                "AllGather", ALU.bypass, replica_groups=RG,
                ins=[p.dr[nm + "_c"].ap()], outs=[p.dr[nm + "_g"].ap()]), "cc", 1)
            p.ccmark[nm] = p.cnt["cc"]
        p.deferred_ags = []



def _emit_kv_load(p, kg_name, vg_name, krow_off, vcol_off):
    blk = p.blk
    KS, VS = V_KS(p), V_VS(p)

    @blk.sync
    def _(sy):
        kg = p.dr[kg_name].ap()
        vg = p.dr[vg_name].ap()
        for r in range(8):
            src = kg[512 * r + krow_off:512 * r + krow_off + 128, :]
            src = src.rearrange("p (j n) -> p j n", n=128)
            p.inc(sy.dma_start(out=KS[:, 7 - r:64:8, :], in_=src), "dS", 16)
            vsrc = vg[1024 * r:1024 * (r + 1), vcol_off:vcol_off + 128]
            vsrc = vsrc.rearrange("(j p) n -> p j n", p=128)
            p.inc(sy.dma_start(out=VS[:, 7 - r:64:8, :], in_=vsrc), "dS", 16)
    return p.cnt["dS"]


def _emit_attn_group(p, rhs_q, out_ap, Dj, mask_slot, bias_col, scale):
    blk = p.blk
    KS, VS = V_KS(p), V_VS(p)
    PS_A, PS_B = p.sb["PS_A"], p.sb["PS_B"]
    psO, psD, psR = p.sb["PS_C"], p.sb["PS_E"], p.sb["PS_F"]
    npair = Dj // 2
    tS0, aE0, vM0, vX0 = (p.cnt["tS"], p.cnt["aE"], p.cnt["vM"], p.cnt["vX"])

    def pt(pr):
        return V_PT(p, pr % 2)

    @blk.tensor
    def _(t):

        def emit_S(pp):
            pr = pp // 2
            reg = PS_A if pr % 2 == 0 else PS_B
            t.wait_ge(p.sem["aE"], aE0 + pr - 1 if pr >= 2 else aE0)
            t.matmul(reg[:, (pp % 2) * 512:(pp % 2) * 512 + 512], KS[:, pp, :],
                     rhs_q, start=True, stop=True).then_inc(p.sem["tS"], 1)

        for pp0 in range(min(4, Dj)):
            emit_S(pp0)
        i = None
        for pp in range(Dj):
            pr = pp // 2
            t.wait_ge(p.sem["aE"], aE0 + pr + 1)
            if mask_slot is not None and pp >= Dj - 8:
                mp = (pp - (Dj - 8)) // 2
                t.wait_ge(p.sem["vM"], vM0 + mp + 1)
            if pp == 0:
                p.wait(t, "vX", vX0)
            src = pt(pr)[:, (pp % 2) * 512:(pp % 2) * 512 + 512]
            t.matmul(psO[:, :], VS[:, pp, :], src,
                     start=(pp == 0), stop=(pp == Dj - 1))
            i = t.matmul(psD[0:1, :], p.sb["ones_bf"][:, :], src,
                         start=(pp == 0), stop=(pp == Dj - 1))
            if pp % 2 == 1 and pp + 4 < Dj:
                emit_S(pp + 3)
                emit_S(pp + 4)
        i.then_inc(p.sem["tS"], 1)
        t.wait_ge(p.sem["vX"], vX0 + 1)
        t.matmul(psR[:, :], p.sb["ones_row"][:, :], p.sb["TMP"][0:1, 512:1024],
                 start=True, stop=True).then_inc(p.sem["tMM"], 1)
    p.cnt["tS"] = tS0 + Dj + 1
    p.cnt["tMM"] += 1
    c_av = p.cnt["tS"]
    c_r1 = p.cnt["tMM"]

    @blk.scalar
    def _(s):
        for pr in range(npair):
            s.wait_ge(p.sem["tS"], tS0 + 2 * pr + 2)
            reg = PS_A if pr % 2 == 0 else PS_B
            s.activation(pt(pr)[:, :], reg[:, :], AF.Exp,
                         bias=p.sb["cconst"][:, bias_col:bias_col + 1],
                         scale=scale).then_inc(p.sem["aE"], 1)
    p.cnt["aE"] = aE0 + npair

    @blk.vector
    def _(v):
        if mask_slot is not None:
            MSK = V_MSK(p)
            for mp in range(4):
                pr = npair - 4 + mp
                v.wait_ge(p.sem["aE"], aE0 + pr + 1)
                buf = pt(pr).rearrange("p (a h n) -> p a h n", a=2, h=4)
                mblk = MSK[:, mask_slot * 8 + 2 * mp:mask_slot * 8 + 2 * mp + 2, :]
                mblk = mblk.unsqueeze(2).broadcast_to([128, 2, 4, 128])
                v.tensor_tensor(buf, buf, mblk, ALU.mult).then_inc(p.sem["vM"], 1)
            p.cnt["vM"] = vM0 + 4
        v.wait_ge(p.sem["tS"], c_av)
        p.inc(v.reciprocal(p.sb["TMP"][0:1, 512:1024], psD[0:1, :]), "vX", 1)
        v.wait_ge(p.sem["tMM"], c_r1)
        v.tensor_copy(p.sb["TMP"][:, 0:512], psR[:, :])
        if out_ap.ndim == 3:
            p.inc(v.tensor_tensor(
                out_ap, psO[:, :].rearrange("p (h n) -> p h n", n=128),
                p.sb["TMP"][:, 0:512].rearrange("p (h n) -> p h n", n=128),
                ALU.mult), "vX", 1)
        else:
            p.inc(v.tensor_tensor(out_ap, psO[:, :], p.sb["TMP"][:, 0:512],
                                  ALU.mult), "vX", 1)


def _emit_l0_attn(p):
    blk = p.blk
    QL0 = V_QL0(p)

    @blk.sync
    def _(sy):
        p.joinall(sy)
        if p.taps:
            p.inc(sy.dma_start(out=p.dr["xln_dbg"].ap(),
                               in_=p.sb["U1"][:, :, :].rearrange("p a n -> p (a n)")),
                  "dS", 16)
            p.inc(sy.dma_start(out=p.dr["q_dbg"].ap(),
                               in_=p.sb["QB"][:, :, :].rearrange("p a n -> p (a n)")),
                  "dS", 16)
            p.wait(sy, "cc", p.ccmark["v0"])
            p.inc(sy.dma_start(out=p.dr["k0g_dbg"].ap(), in_=p.dr["k0_g"].ap()),
                  "dS", 16)
            p.inc(sy.dma_start(out=p.dr["v0g_dbg"].ap(), in_=p.dr["v0_g"].ap()),
                  "dS", 16)
            p.wait(sy, "dS")
        p.inc(sy.dma_start(
            out=p.sb["U1"][:, 0:8, :],
            in_=p.dr["mask0"].ap().rearrange("p (a n) -> p a n", n=1024)), "dS", 16)
    c_msk = p.cnt["dS"]

    for kv in range(4):
        @blk.sync
        def _(sy, kv=kv):
            p.wait(sy, "cc", p.ccmark["v0"])
            sy.wait_ge(p.sem["dS"], c_msk)
            if kv > 0:
                p.wait(sy, "vX")
        c_kv = _emit_kv_load(p, "k0_g", "v0_g", 128 * kv, 128 * kv)

        for j in range(8):
            @blk.tensor
            def _(t, c_kv=c_kv, j=j):
                if j == 0:
                    t.wait_ge(p.sem["dS"], c_kv)
            rhs = QL0[:, kv, j, :, :].rearrange("p h n -> p (h n)")
            out = p.sb["OB"][:, 4 * kv:4 * kv + 4, 128 * j:128 * j + 128]
            _emit_attn_group(p, rhs, out, 8 * (j + 1), j, 0, 1.0)


def _emit_proj(p, layer):
    blk = p.blk
    wg = "wproj_g" if layer == 0 else "woutT_g"
    mark = "wproj" if layer == 0 else "woutT"
    OB, x = p.sb["OB"], p.sb["x"]

    if p.taps and layer == 0:
        @blk.sync
        def _(sy):
            p.joinall(sy, ("vX",))
            p.inc(sy.dma_start(out=p.dr["ob_dbg"].ap(),
                               in_=OB[:, :, :].rearrange("p a n -> p (a n)")),
                  "dS", 16)

    WTP = p.sb["WT"][:, :, :, :].rearrange("p b a n -> p (b a n)").rearrange(
        "p (a n) -> p a n", n=256)
    wt_cons = [p.cnt["tS"]]
    ev_cnt = {0: p.cnt["vX"], 1: p.cnt["vX"]}
    for oc in range(16):
        half = oc % 2

        if half == 0:
            @blk.sync
            def _(sy, oc=oc):
                p.wait(sy, "cc", p.ccmark[mark])
                if oc == 0:
                    p.joinall(sy, ("vX", "aE", "tS"))
                p.wait(sy, "tS", wt_cons[0])
                srcp = p.dr[wg].ap()[:, oc * 128:oc * 128 + 256]
                srcp = srcp.rearrange("(a p) n -> p a n", p=128)
                p.inc(sy.dma_start(out=WTP[:, :, :], in_=srcp), "dSa", 16)
            cW = p.cnt["dSa"]
            p._cW_pair = cW
        else:
            cW = p._cW_pair

        @blk.tensor
        def _(t, oc=oc, half=half, cW=cW, e0=ev_cnt[0], e1=ev_cnt[1]):
            t.wait_ge(p.sem["dSa"], cW)
            for tc in range(2):
                ps = p.sb["PS_C"] if tc == 0 else p.sb["PS_D"]
                p.wait(t, "vX", e0 if tc == 0 else e1)
                i = None
                for ic in range(16):
                    i = t.matmul(ps[:, :], WTP[:, ic, half * 128:half * 128 + 128],
                                 OB[:, ic, tc * 512:(tc + 1) * 512],
                                 start=(ic == 0), stop=(ic == 15))
                p.inc(i, "tS", 1)
        if half == 1:
            wt_cons[0] = p.cnt["tS"]
        cM = p.cnt["tS"]

        @blk.vector
        def _(v, oc=oc, cM=cM):
            for tc in range(2):
                v.wait_ge(p.sem["tS"], cM - 1 + tc)
                ps = p.sb["PS_C"] if tc == 0 else p.sb["PS_D"]
                p.inc(v.tensor_tensor(x[:, oc, tc * 512:(tc + 1) * 512], ps[:, :],
                                      x[:, oc, tc * 512:(tc + 1) * 512], ALU.add),
                      "vX", 1)
            ev_cnt[0] = p.cnt["vX"] - 1
            ev_cnt[1] = p.cnt["vX"]



def _emit_l0_ffn(p):
    blk = p.blk
    c_bf0 = p.cnt["vX"]
    _emit_ln_stream(p, "l0f")
    c_ln = p.c_ln_l0f
    XLN = V_XLN(p)
    x = p.sb["x"]
    inter = p.sb["AR1"][:, 0:8192].rearrange("p (a n) -> p a n", n=512)
    sg = p.sb["TMP"][:, :].bitcast(BF16)
    banks = [p.sb["PS_C"], p.sb["PS_D"], p.sb["PS_E"], p.sb["PS_F"]]
    sg_free = [p.cnt["vX"], p.cnt["vX"]]

    for e in range(4):
        rhs_all = XLN[:, :, :].rearrange("p a (n4 f) -> p a n4 f", f=4)
        jobs = []
        for o in range(16):
            jobs.append((o, 0))
            jobs.append((o, 1))
        wt_cons = [p.cnt["tS"], p.cnt["tS"]]
        bank_free = [c_bf0 if e == 0 else p.cnt["vX"]] * 4
        gate_mm = {}
        silu_cnt = {}
        mult_cnt = {}
        for ji, (o, kind) in enumerate(jobs):
            buf = ji % 2
            bank = ji % 4

            @blk.sync
            def _(sy, e=e, o=o, kind=kind, buf=buf):
                p.wait(sy, "cc", p.ccmark["gup"])
                p.wait(sy, "tS", wt_cons[buf])
                srcw = p.dr["gup_g"].ap().rearrange(
                    "(r ee i pp) n -> ee pp r i n", ee=4, i=2, pp=128)[e]
                co = kind * 2048 + o * 128
                sem = "dSa" if buf == 0 else "dSb"
                for k in range(2):
                    p.inc(sy.dma_start(
                        out=p.sb["WT"][:, buf, k:16:2, :],
                        in_=srcw[:, :, k, co:co + 128]), sem, 16)
            cW = p.cnt["dSa" if buf == 0 else "dSb"]

            @blk.tensor
            def _(t, e=e, ji=ji, o=o, kind=kind, buf=buf, bank=bank, cW=cW,
                  bf=bank_free[bank]):
                t.wait_ge(p.sem["dSa" if buf == 0 else "dSb"], cW)
                p.wait(t, "vX", bf)
                if kind == 1:
                    p.wait(t, "aX", silu_cnt.get(o, 0))
                i = None
                for ic in range(16):
                    if e == 0 and ji == 0:
                        t.wait_ge(p.sem["vX"], p.ln_cnts_l0f[ic])
                    i = t.matmul(banks[bank][:, 0:256], p.sb["WT"][:, buf, ic, :],
                                 rhs_all[:, ic, :, e], start=(ic == 0),
                                 stop=(ic == 15))
                p.inc(i, "tS", 1)
            wt_cons[buf] = p.cnt["tS"]
            cM = p.cnt["tS"]
            if kind == 0:
                gate_mm[o] = (cM, bank)
            else:
                gcM, gbank = gate_mm[o]

                sl = sg[:, (o % 2) * 256:(o % 2) * 256 + 256]

                @blk.scalar
                def _(s, o=o, gcM=gcM, gbank=gbank, sl=sl, sf=sg_free[o % 2]):
                    s.wait_ge(p.sem["tS"], gcM)
                    s.wait_ge(p.sem["vX"], mult_cnt[o - 2] if o >= 2 else sf)
                    p.inc(s.activation(sl, banks[gbank][:, 0:256], AF.Silu),
                          "aX", 1)
                silu_cnt[o] = p.cnt["aX"]
                cS = p.cnt["aX"]

                @blk.vector
                def _(v, o=o, cS=cS, cM=cM, bank=bank, sl=sl):
                    v.wait_ge(p.sem["aX"], cS)
                    v.wait_ge(p.sem["tS"], cM)
                    p.inc(v.tensor_tensor(inter[:, o, 0:256], sl,
                                          banks[bank][:, 0:256], ALU.mult),
                          "vX", 1)
                mult_cnt[o] = p.cnt["vX"]
                sg_free[o % 2] = p.cnt["vX"]
                gb_ev = p.cnt["vX"]
                bank_free[gbank] = gb_ev
                bank_free[bank] = gb_ev
        c_int = p.cnt["vX"]

        wt_cons = [p.cnt["tS"], p.cnt["tS"]]
        bank_free = [p.cnt["vX"]] * 2
        for oc in range(16):
            buf = oc % 2
            bank = oc % 2

            @blk.sync
            def _(sy, e=e, oc=oc, buf=buf):
                p.wait(sy, "cc", p.ccmark["down"])
                p.wait(sy, "tS", wt_cons[buf])
                srcw = p.dr["down_g"].ap().rearrange(
                    "(r ee i pp) n -> ee pp r i n", ee=4, i=2, pp=128)[e]
                sem = "dSa" if buf == 0 else "dSb"
                for k in range(2):
                    p.inc(sy.dma_start(
                        out=p.sb["WT"][:, buf, k:16:2, :],
                        in_=srcw[:, :, k, oc * 128:(oc + 1) * 128]), sem, 16)
            cW = p.cnt["dSa" if buf == 0 else "dSb"]

            @blk.tensor
            def _(t, oc=oc, buf=buf, bank=bank, cW=cW, c_int=c_int,
                  bf=bank_free[bank]):
                t.wait_ge(p.sem["dSa" if buf == 0 else "dSb"], cW)
                t.wait_ge(p.sem["vX"], c_int)
                p.wait(t, "vX", bf)
                i = None
                for ic in range(16):
                    i = t.matmul(banks[bank][:, 0:256], p.sb["WT"][:, buf, ic, :],
                                 inter[:, ic, 0:256], start=(ic == 0),
                                 stop=(ic == 15))
                p.inc(i, "tS", 1)
            wt_cons[buf] = p.cnt["tS"]
            cM = p.cnt["tS"]

            @blk.vector
            def _(v, e=e, oc=oc, cM=cM, bank=bank):
                v.wait_ge(p.sem["tS"], cM)
                xo = x[:, :, :].rearrange("p a (n4 f) -> p a n4 f", f=4)[:, oc, :, e]
                p.inc(v.tensor_tensor(xo, banks[bank][:, 0:256], xo, ALU.add),
                      "vX", 1)
            bank_free[bank] = p.cnt["vX"]



def _emit_l1_qkv(p):
    blk = p.blk
    _emit_ln_stream(p, "l1a")
    c_ln = p.c_ln_l1a
    XLN = V_XLN(p)
    QB = p.sb["QB"]

    banks = [p.sb["PS_C"], p.sb["PS_D"], p.sb["PS_E"], p.sb["PS_F"]]
    WTP = p.sb["WT"][:, :, :, :].rearrange("p b a n -> p (b a n)").rearrange(
        "p (a n) -> p a n", n=256)
    wt_cons = [p.cnt["tS"]]
    pair_free = [p.cnt["aX"], p.cnt["aX"]]

    def qkv_chunk(oc, dest_fn, idx):
        half = idx % 2
        pr = idx % 2

        if half == 0:
            @blk.sync
            def _(sy, oc=oc):
                p.wait(sy, "cc", p.ccmark["wqkvT"])
                p.wait(sy, "tS", wt_cons[0])
                srcp = p.dr["wqkvT_g"].ap()[:, oc * 128:oc * 128 + 256]
                srcp = srcp.rearrange("(a p) n -> p a n", p=128)
                p.inc(sy.dma_start(out=WTP[:, :, :], in_=srcp), "dSa", 16)
            p._cW_pair = p.cnt["dSa"]
        cW = p._cW_pair

        @blk.tensor
        def _(t, half=half, cW=cW, pr=pr, pf=pair_free[pr]):
            t.wait_ge(p.sem["dSa"], cW)
            p.wait(t, "aX", pf)
            for tc in range(2):
                ps = banks[pr * 2 + tc]
                i = None
                for ic in range(16):
                    if tc == 0:
                        t.wait_ge(p.sem["vX"], p.ln_cnts_l1a[ic])
                    i = t.matmul(ps[:, :], WTP[:, ic, half * 128:half * 128 + 128],
                                 XLN[:, ic, tc * 512:(tc + 1) * 512],
                                 start=(ic == 0), stop=(ic == 15))
                p.inc(i, "tS", 1)
        if half == 1:
            wt_cons[0] = p.cnt["tS"]
        cM = p.cnt["tS"]

        @blk.scalar
        def _(s, oc=oc, cM=cM, pr=pr):
            for tc in range(2):
                s.wait_ge(p.sem["tS"], cM - 1 + tc)
                ps = banks[pr * 2 + tc]
                p.inc(s.copy(dest_fn(oc, tc), ps[:, :]), "aX", 1)
        pair_free[pr] = p.cnt["aX"]

    idx = 0
    for oc in range(16):
        qkv_chunk(oc, lambda oc, tc: QB[:, oc, tc * 512:(tc + 1) * 512], idx)
        idx += 1
        if idx == 2:
            @blk.vector
            def _(v):
                pass

    for hg in range(4):
        for kk in range(4):
            qkv_chunk(16 + hg * 4 + kk,
                      lambda oc, tc, kk=kk: V_KC(p)[:, kk, tc * 512:(tc + 1) * 512],
                      idx)
            idx += 1
        c_k = p.cnt["aX"]

        @blk.sync
        def _(sy, hg=hg, c_k=c_k):
            p.wait(sy, "aX", c_k)
            p.wait(sy, "tS")
            src = p.dr["wqkvT_g"].ap()[:, 4096 + hg * 512:4096 + (hg + 1) * 512]
            src = src.rearrange("(a p) n -> p a n", p=128)
            for hic in range(2):
                p.inc(sy.dma_start(
                    out=p.sb["AR1"][:, hic * 8192:(hic + 1) * 8192].rearrange(
                        "p (a n) -> p a n", n=512)[:, 0:8, :],
                    in_=src[:, hic * 8:(hic + 1) * 8, :]), "dS", 16)
        cVW = p.cnt["dS"]
        aXb = p.cnt["aX"]

        @blk.tensor
        def _(t, cVW=cVW, aXb=aXb, pf0=pair_free[0], pf1=pair_free[1]):
            t.wait_ge(p.sem["dS"], cVW)
            for tt in range(8):
                ps = p.sb["PS_C"] if tt % 2 == 0 else p.sb["PS_D"]
                if tt >= 2:
                    t.wait_ge(p.sem["aX"], aXb + tt - 1)
                else:
                    t.wait_ge(p.sem["aX"], aXb)
                i = None
                for ic in range(16):
                    wv = p.sb["AR1"][:, (ic // 8) * 8192:(ic // 8 + 1) * 8192]
                    wv = wv.rearrange("p (a n) -> p a n", n=512)
                    i = t.matmul(ps[:, :], XLN[:, ic, tt * 128:(tt + 1) * 128],
                                 wv[:, ic % 8, :], start=(ic == 0), stop=(ic == 15))
                p.inc(i, "tS", 1)
        cVM = p.cnt["tS"]

        @blk.scalar
        def _(s, cVM=cVM):
            for tt in range(8):
                ps = p.sb["PS_C"] if tt % 2 == 0 else p.sb["PS_D"]
                s.wait_ge(p.sem["tS"], cVM - 7 + tt)
                p.inc(s.copy(V_VC(p)[:, tt, :], ps[:, :]), "aX", 1)
        c_vg = p.cnt["aX"]

        @blk.sync
        def _(sy, hg=hg, c_vg=c_vg):
            p.wait(sy, "aX", c_vg)
            p.inc(sy.dma_start(
                out=p.dr[f"k1_c{hg}"].ap().rearrange("(a p) n -> p a n", p=128),
                in_=V_KC(p)[:, :, :]), "dS", 16)
            p.inc(sy.dma_start(
                out=p.dr[f"v1_c{hg}"].ap().rearrange("(a p) n -> p a n", p=128),
                in_=V_VC(p)[:, :, :]), "dS", 16)
        c_ctr = p.cnt["dS"]

        @blk.gpsimd
        def _(g, hg=hg, c_ctr=c_ctr):
            g.wait_ge(p.sem["dS"], c_ctr)
            for nm in (f"k1_c{hg}", f"v1_c{hg}"):
                p.inc(g.collective_compute(
                    "AllGather", ALU.bypass, replica_groups=RG,
                    ins=[p.dr[nm].ap()], outs=[p.dr[nm.replace('_c', '_g')].ap()]),
                    "cc", 1)
            p.ccmark[f"kv1_{hg}"] = p.cnt["cc"]


def _emit_l1_attn(p):
    blk = p.blk
    QB = p.sb["QB"]
    first = True
    for hg in range(4):
        for hh in range(4):
            h = hg * 4 + hh

            @blk.sync
            def _(sy, hg=hg, first=first):
                p.wait(sy, "cc", p.ccmark[f"kv1_{hg}"])
                if first:
                    p.joinall(sy)
                else:
                    p.wait(sy, "vX")
            c_kv = _emit_kv_load(p, f"k1_g{hg}", f"v1_g{hg}", 128 * hh, 128 * hh)

            for qc in range(2):
                @blk.tensor
                def _(t, c_kv=c_kv, qc=qc):
                    if qc == 0:
                        t.wait_ge(p.sem["dS"], c_kv)
                _emit_attn_group(p, QB[:, h, qc * 512:(qc + 1) * 512],
                                 p.sb["OB"][:, h, qc * 512:(qc + 1) * 512],
                                 64, None, 1, SCALE)
            first = False




def _vchain(p, v, ops):
    for f in ops:
        p.inc(f(), "vX", 1)
        v.wait_ge(p.sem["vX"], p.cnt["vX"])


def _emit_l1_ffn(p):
    blk = p.blk
    x = p.sb["x"]
    QB, OB, U1 = p.sb["QB"], p.sb["OB"], p.sb["U1"]
    sm = p.sb["statm"]
    xT = [OB[:, 0:2, :].rearrange("p a n -> p (a n)"),
          OB[:, 2:4, :].rearrange("p a n -> p (a n)")]
    xQ = [OB[:, 4:6, :].rearrange("p a n -> p (a n)"),
          OB[:, 6:8, :].rearrange("p a n -> p (a n)")]
    scr = OB[:, 8:10, :].rearrange("p a n -> p (a n)")
    TMPf = p.sb["TMP"]

    @blk.vector
    def _(v):
        p.joinall(v)
        for ic in range(16):
            p.inc(v.tensor_copy(QB[:, ic, :], x[:, ic, :]), "vX", 1)
    c_xbf = p.cnt["vX"]

    @blk.sync
    def _(sy):
        p.wait(sy, "vX", c_xbf)
        p.inc(sy.dma_start(
            out=p.dr["xbf_d"].ap().rearrange("(a p) n -> p a n", p=128),
            in_=QB[:, :, :]), "dS", 16)
    c_xd = p.cnt["dS"]

    quant_cnt = []
    ship_cnt = []
    for tt in range(8):
        b = tt % 2

        @blk.scalar
        def _(s, tt=tt, b=b):
            s.wait_ge(p.sem["dS"], c_xd)
            if tt >= 2:
                p.wait(s, "vX", quant_cnt[tt - 2])
            p.inc(s.dma_start(out=xT[b][:, :],
                              in_=p.dr["xbf_d"].ap()[:, tt * 128:(tt + 1) * 128],
                              transpose=True), "dTa" if b == 0 else "dTb", 16)
        cT = p.cnt["dTa" if b == 0 else "dTb"]

        @blk.vector
        def _(v, tt=tt, b=b, cT=cT):
            v.wait_ge(p.sem["dTa" if b == 0 else "dTb"], cT)
            v.reduce_sum(sm[:, 0:1], xT[b][:, :], AX)
            v.reduce_max(sm[:, 2:3], xT[b][:, :], AX)
            v.tensor_reduce(sm[:, 3:4], xT[b][:, :], AX, ALU.min)
            p.inc(v.scalar_tensor_tensor(scr[:, 0:2048], xT[b][:, :], 1.0,
                                         xT[b][:, :], ALU.mult, ALU.mult,
                                         accum_out=sm[:, 1:2]), "vX", 1)
            v.wait_ge(p.sem["vX"], p.cnt["vX"])
            _vchain(p, v, [
                lambda: v.tensor_scalar_mul(sm[:, 4:5], sm[:, 0:1], 1.0 / 2048.0),
                lambda: v.tensor_tensor(sm[:, 5:6], sm[:, 4:5], sm[:, 4:5],
                                        ALU.mult),
                lambda: v.tensor_scalar_mul(sm[:, 9:10], sm[:, 1:2], 1.0 / 2048.0),
                lambda: v.tensor_tensor(sm[:, 5:6], sm[:, 9:10], sm[:, 5:6],
                                        ALU.subtract),
            ])
        cV = p.cnt["vX"]

        @blk.scalar
        def _(s, cV=cV):
            s.wait_ge(p.sem["vX"], cV)
            p.inc(s.activation(sm[:, 6:7], sm[:, 5:6], AF.Sqrt,
                               bias=p.sb["cconst"][:, 3:4], scale=1.0 + 1e-6),
                  "aX", 1)
        cS = p.cnt["aX"]

        @blk.vector
        def _(v, tt=tt, b=b, cS=cS):
            v.wait_ge(p.sem["aX"], cS)
            if tt >= 2:
                v.wait_ge(p.sem["dS"], ship_cnt[tt - 2])
            _vchain(p, v, [
                lambda: v.reciprocal(sm[:, 6:7], sm[:, 6:7]),
                lambda: v.tensor_tensor(sm[:, 2:3], sm[:, 2:3], sm[:, 4:5],
                                        ALU.subtract),
                lambda: v.tensor_tensor(sm[:, 3:4], sm[:, 4:5], sm[:, 3:4],
                                        ALU.subtract),
                lambda: v.tensor_tensor(sm[:, 7:8], sm[:, 2:3], sm[:, 3:4],
                                        ALU.max),
                lambda: v.reciprocal(sm[:, 12:13], sm[:, 7:8]),
                lambda: v.tensor_scalar_mul(sm[:, 12:13], sm[:, 12:13], 128.0),
                lambda: v.tensor_tensor(sm[:, 13:14], sm[:, 4:5], sm[:, 12:13],
                                        ALU.mult),
                lambda: v.tensor_scalar_mul(sm[:, 13:14], sm[:, 13:14], -1.0),
                lambda: v.tensor_tensor(sm[:, 14:15], sm[:, 6:7], sm[:, 7:8],
                                        ALU.mult),
                lambda: v.tensor_scalar_mul(sm[:, 14:15], sm[:, 14:15],
                                            1.0 / 128.0),
            ])
            v.tensor_scalar(xQ[b][:, :], xT[b][:, :], sm[:, 12:13], sm[:, 13:14],
                            ALU.mult, ALU.add)
            v.tensor_scalar(xQ[b][:, :], xQ[b][:, :], CLIP, -CLIP, ALU.min, ALU.max)
            p.inc(v.tensor_scalar_mul(xQ[b][:, :], xQ[b][:, :], sm[:, 14:15]),
                  "vX", 1)
        quant_cnt.append(p.cnt["vX"])

        @blk.sync
        def _(sy, tt=tt, b=b):
            p.wait(sy, "vX", quant_cnt[tt])
            if p.taps == 2 and tt == 0:
                p.inc(sy.dma_start(out=p.dr["sm_dbg"].ap(), in_=sm[:, :]), "dS", 16)
                p.inc(sy.dma_start(out=p.dr["xt0_dbg"].ap(), in_=xT[b][:, :]),
                      "dS", 16)
                p.inc(sy.dma_start(
                    out=p.dr["xbf_dbg"].ap().rearrange("(a p) n -> p a n", p=128),
                    in_=QB[:, :, :].rearrange("p a n -> p (a n)").rearrange(
                        "p (a n) -> p a n", n=1024)), "dS", 16)
            p.inc(sy.dma_start(out=p.dr["xhat_d"].ap()[tt * 128:(tt + 1) * 128, :],
                               in_=xQ[b][:, :]), "dS", 16)
        ship_cnt.append(p.cnt["dS"])
    c_xq = p.cnt["dS"]

    @blk.scalar
    def _(s):
        s.wait_ge(p.sem["dS"], c_xq)
        for ic in range(16):
            p.inc(s.dma_start(out=U1[:, ic, :],
                              in_=p.dr["xhat_d"].ap()[:, ic * 128:(ic + 1) * 128],
                              transpose=True), "dT", 16)
    c_xh = p.cnt["dT"]

    if p.taps == 2:
        @blk.sync
        def _(sy):
            p.wait(sy, "cc", p.ccmark["wprojT"])
            p.wait(sy, "dT", c_xh)
            p.inc(sy.dma_start(out=p.dr["gam_dbg"].ap(), in_=p.dr["gam_g"].ap()),
                  "dS", 16)
            p.inc(sy.dma_start(out=p.dr["wfcT_dbg"].ap(), in_=p.dr["wfcT_g"].ap()),
                  "dS", 16)
            p.inc(sy.dma_start(out=p.dr["wprojT_dbg"].ap(),
                               in_=p.dr["wprojT_g"].ap()), "dS", 16)
            p.inc(sy.dma_start(out=p.dr["xhat_dbg"].ap(), in_=p.dr["xhat_d"].ap()),
                  "dS", 16)
            p.wait(sy, "dS")

    hbuf = [p.sb["AR1"][:, 0:8192], p.sb["AR1"][:, 8192:16384]]
    x2fm = QB[:, :, :].rearrange("p a n -> p (a n)").rearrange(
        "p (a n) -> p a n", n=256)
    wst = p.sb["WT"][:, :, :, :].rearrange("p b a n -> p (b a n)").rearrange(
        "p (a n) -> p a n", n=512)
    wtile32 = p.sb["WT"][:, :, :, :].rearrange("p b a n -> p (b a) n")

    for w in range(4):
        wave_v0 = p.cnt["vX"]
        wst2 = [p.sb["WT"][:, 0, :, :].rearrange("p a n -> p (a n)").rearrange(
                    "p (a n) -> p a n", n=512),
                p.sb["WT"][:, 1, :, :].rearrange("p a n -> p (a n)").rearrange(
                    "p (a n) -> p a n", n=512)]
        wt_cons2 = [p.cnt["tS"], p.cnt["tS"]]
        pairfree = {0: p.cnt["aX"], 1: p.cnt["aX"]}
        for fcg in range(16):
            for ich in range(4):
                buf = ich % 2

                @blk.sync
                def _(sy, w=w, fcg=fcg, ich=ich, buf=buf):
                    p.wait(sy, "cc", p.ccmark["wfcT"])
                    p.wait(sy, "tS", wt_cons2[buf])
                    src = p.dr["wfcT_g"].ap()[512 * ich:512 * (ich + 1),
                                              fcg * 512:(fcg + 1) * 512]
                    src = src.rearrange("(a p) n -> p a n", p=128)
                    p.inc(sy.dma_start(out=wst2[buf][:, :, :], in_=src),
                          "dSa" if buf == 0 else "dSb", 16)
                cW = p.cnt["dSa" if buf == 0 else "dSb"]

                @blk.tensor
                def _(t, w=w, fcg=fcg, ich=ich, buf=buf, cW=cW,
                      pf=pairfree[fcg % 2]):
                    t.wait_ge(p.sem["dSa" if buf == 0 else "dSb"], cW)
                    t.wait_ge(p.sem["dT"], c_xh)
                    if ich == 0:
                        p.wait(t, "aX", pf)
                        if fcg < 2:
                            p.wait(t, "vX", wave_v0)
                    i = None
                    for lt in range(2):
                        tt = 2 * w + lt
                        ps = (p.sb["PS_C"], p.sb["PS_D"], p.sb["PS_E"],
                              p.sb["PS_F"])[(fcg % 2) * 2 + lt]
                        for ic4 in range(4):
                            ic = ich * 4 + ic4
                            i = t.matmul(ps[:, :],
                                         U1[:, ic, tt * 128:(tt + 1) * 128],
                                         wst2[buf][:, ic4, :],
                                         start=(ic == 0), stop=(ic == 15))
                    p.inc(i, "tS", 1)
                wt_cons2[buf] = p.cnt["tS"]
            cM = p.cnt["tS"]

            @blk.scalar
            def _(s, w=w, fcg=fcg, cM=cM):
                s.wait_ge(p.sem["tS"], cM)
                for lt in range(2):
                    ps = (p.sb["PS_C"], p.sb["PS_D"], p.sb["PS_E"],
                          p.sb["PS_F"])[(fcg % 2) * 2 + lt]
                    p.inc(s.activation(hbuf[lt][:, fcg * 512:(fcg + 1) * 512],
                                       ps[:, :], AF.Gelu), "aX", 1)
            pairfree[fcg % 2] = p.cnt["aX"]
            cG = p.cnt["aX"]

            @blk.vector
            def _(v, fcg=fcg, cG=cG):
                v.wait_ge(p.sem["aX"], cG)
                for lt in range(2):
                    base = lt * 64
                    hsl = hbuf[lt][:, fcg * 512:(fcg + 1) * 512]
                    v.reduce_sum(TMPf[:, base + fcg:base + fcg + 1], hsl, AX)
                    v.reduce_max(TMPf[:, base + 32 + fcg:base + 33 + fcg],
                                 hsl, AX)
                    v.tensor_reduce(TMPf[:, base + 48 + fcg:base + 49 + fcg],
                                    hsl, AX, ALU.min)
                    p.inc(v.scalar_tensor_tensor(
                        scr[:, 0:512], hsl, 1.0, hsl, ALU.mult, ALU.mult,
                        accum_out=TMPf[:, base + 16 + fcg:base + 17 + fcg]),
                        "vX", 1)
        c_fc = p.cnt["aX"]
        c_part = p.cnt["vX"]

        if p.taps == 2 and w == 0:
            @blk.sync
            def _(sy, c_fc=c_fc):
                sy.wait_ge(p.sem["aX"], c_fc)
                p.inc(sy.dma_start(out=p.dr["h0_dbg"].ap(), in_=hbuf[0][:, :]),
                      "dS", 16)
                p.wait(sy, "dS")

        for lt in range(2):
            hb = hbuf[lt]
            base = lt * 64

            @blk.vector
            def _(v, hb=hb, base=base, c_part=c_part):
                v.wait_ge(p.sem["vX"], c_part)
                _vchain(p, v, [
                    lambda: v.reduce_sum(sm[:, 0:1], TMPf[:, base:base + 16],
                                         AX),
                    lambda: v.reduce_sum(sm[:, 1:2],
                                         TMPf[:, base + 16:base + 32], AX),
                    lambda: v.reduce_max(sm[:, 2:3],
                                         TMPf[:, base + 32:base + 48], AX),
                    lambda: v.tensor_reduce(sm[:, 3:4],
                                            TMPf[:, base + 48:base + 64], AX,
                                            ALU.min),
                    lambda: v.tensor_scalar_mul(sm[:, 4:5], sm[:, 0:1],
                                                1.0 / 8192.0),
                    lambda: v.tensor_tensor(sm[:, 5:6], sm[:, 4:5], sm[:, 4:5],
                                            ALU.mult),
                    lambda: v.tensor_scalar_mul(sm[:, 9:10], sm[:, 1:2],
                                                1.0 / 8192.0),
                    lambda: v.tensor_tensor(sm[:, 5:6], sm[:, 9:10], sm[:, 5:6],
                                            ALU.subtract),
                ])
            cV = p.cnt["vX"]

            @blk.scalar
            def _(s, cV=cV):
                s.wait_ge(p.sem["vX"], cV)
                p.inc(s.activation(sm[:, 6:7], sm[:, 5:6], AF.Sqrt,
                                   bias=p.sb["cconst"][:, 4:5]), "aX", 1)
            cS = p.cnt["aX"]

            @blk.vector
            def _(v, hb=hb, cS=cS):
                v.wait_ge(p.sem["aX"], cS)
                _vchain(p, v, [
                    lambda: v.reciprocal(sm[:, 6:7], sm[:, 6:7]),
                    lambda: v.tensor_tensor(sm[:, 2:3], sm[:, 2:3], sm[:, 4:5],
                                            ALU.subtract),
                    lambda: v.tensor_tensor(sm[:, 3:4], sm[:, 4:5], sm[:, 3:4],
                                            ALU.subtract),
                    lambda: v.tensor_tensor(sm[:, 7:8], sm[:, 2:3], sm[:, 3:4],
                                            ALU.max),
                    lambda: v.reciprocal(sm[:, 12:13], sm[:, 7:8]),
                    lambda: v.tensor_scalar_mul(sm[:, 12:13], sm[:, 12:13], 128.0),
                    lambda: v.tensor_tensor(sm[:, 13:14], sm[:, 4:5],
                                            sm[:, 12:13], ALU.mult),
                    lambda: v.tensor_scalar_mul(sm[:, 13:14], sm[:, 13:14], -1.0),
                    lambda: v.tensor_tensor(sm[:, 14:15], sm[:, 6:7], sm[:, 7:8],
                                            ALU.mult),
                    lambda: v.tensor_scalar_mul(sm[:, 14:15], sm[:, 14:15],
                                                1.0 / 128.0),
                ])
                v.tensor_scalar(hb[:, :], hb[:, :], sm[:, 12:13], sm[:, 13:14],
                                ALU.mult, ALU.add)
                v.tensor_scalar(hb[:, :], hb[:, :], CLIP, -CLIP, ALU.min, ALU.max)
                p.inc(v.tensor_scalar_mul(hb[:, :], hb[:, :], sm[:, 14:15]),
                      "vX", 1)
            cQ2 = p.cnt["vX"]

            @blk.sync
            def _(sy, lt=lt, hb=hb, cQ2=cQ2):
                sy.wait_ge(p.sem["vX"], cQ2)
                p.inc(sy.dma_start(
                    out=p.dr["xh2_d"].ap()[lt * 128:(lt + 1) * 128, :],
                    in_=hb[:, :]), "dS", 16)
        c_ship = p.cnt["dS"]

        if p.taps == 2 and w == 0:
            @blk.sync
            def _(sy, c_ship=c_ship):
                sy.wait_ge(p.sem["dS"], c_ship)
                p.inc(sy.dma_start(out=p.dr["xh2_dbg"].ap(), in_=p.dr["xh2_d"].ap()),
                      "dS", 16)
                p.wait(sy, "dS")

        x2g = []

        @blk.scalar
        def _(s, w=w, c_ship=c_ship, x2g=x2g):
            s.wait_ge(p.sem["dS"], c_ship)
            if w > 0:
                p.wait(s, "tS")
            for g8 in range(8):
                for ic8 in range(8):
                    ic = g8 * 8 + ic8
                    p.inc(s.dma_start(
                        out=x2fm[:, ic, :],
                        in_=p.dr["xh2_d"].ap()[:, ic * 128:(ic + 1) * 128],
                        transpose=True), "dT", 16)
                s.wait_ge(p.sem["dT"], p.cnt["dT"])
                x2g.append(p.cnt["dT"])
        c_x2 = p.cnt["dT"]

        WTH = [p.sb["WT"][:, 0, :, :].rearrange("p a n -> p (a n)").rearrange(
                   "p (a n) -> p a n", n=256),
               p.sb["WT"][:, 1, :, :].rearrange("p a n -> p (a n)").rearrange(
                   "p (a n) -> p a n", n=256)]
        wt_cons3 = [p.cnt["tS"], p.cnt["tS"]]
        bankfree = {0: p.cnt["vX"], 1: p.cnt["vX"]}
        for ocp in range(8):
            for ih in range(8):
                buf = ih % 2

                @blk.sync
                def _(sy, ocp=ocp, ih=ih, buf=buf):
                    p.wait(sy, "cc", p.ccmark["wprojT"])
                    p.wait(sy, "tS", wt_cons3[buf])
                    src = p.dr["wprojT_g"].ap()[1024 * ih:1024 * (ih + 1),
                                                ocp * 256:(ocp + 1) * 256]
                    src = src.rearrange("(a p) n -> p a n", p=128)
                    p.inc(sy.dma_start(out=WTH[buf][:, :, :], in_=src),
                          "dSa" if buf == 0 else "dSb", 16)
                cW = p.cnt["dSa" if buf == 0 else "dSb"]

                if p.taps == 2 and w == 0 and ocp == 0 and ih == 1:
                    @blk.sync
                    def _(sy, cW=cW):
                        sy.wait_ge(p.sem["dT"], c_x2)
                        sy.wait_ge(p.sem["dSa"], p.cnt["dSa"])
                        sy.wait_ge(p.sem["dSb"], p.cnt["dSb"])
                        p.inc(sy.dma_start(
                            out=p.dr["x2fm_dbg"].ap(),
                            in_=QB[:, :, :].rearrange("p a n -> p (a n)")),
                            "dS", 16)
                        p.inc(sy.dma_start(
                            out=p.dr["wth_dbg"].ap(),
                            in_=p.sb["WT"][:, :, :, :].rearrange(
                                "p b a n -> p (b a n)")), "dS", 16)
                        p.wait(sy, "dS")

                @blk.tensor
                def _(t, ocp=ocp, ih=ih, buf=buf, cW=cW, c_ih=x2g[ih],
                      bf0=bankfree[0], bf1=bankfree[1]):
                    t.wait_ge(p.sem["dSa" if buf == 0 else "dSb"], cW)
                    t.wait_ge(p.sem["dT"], c_ih)
                    if p.taps == 2 and w == 0 and ocp == 0 and ih <= 1:
                        p.wait(t, "dS")
                    for sub in range(2):
                        if ih == 0:
                            p.wait(t, "vX", bf0 if sub == 0 else bf1)
                        ps = p.sb["PS_C"] if sub == 0 else p.sb["PS_D"]
                        i = None
                        for ic8 in range(8):
                            ic = ih * 8 + ic8
                            i = t.matmul(ps[:, 0:256],
                                         WTH[buf][:, ic8,
                                                  sub * 128:sub * 128 + 128],
                                         x2fm[:, ic, :], start=(ic == 0),
                                         stop=(ic == 63))
                        p.inc(i, "tS", 1)
                wt_cons3[buf] = p.cnt["tS"]
            cM = p.cnt["tS"]

            @blk.vector
            def _(v, w=w, ocp=ocp, cM=cM):
                for sub in range(2):
                    oc = ocp * 2 + sub
                    v.wait_ge(p.sem["tS"], cM - 1 + sub)
                    ps = p.sb["PS_C"] if sub == 0 else p.sb["PS_D"]
                    xo = x[:, oc, 2 * w * 128:(2 * w + 2) * 128]
                    p.inc(v.tensor_tensor(xo, ps[:, 0:256], xo, ALU.add), "vX", 1)
                bankfree[0] = p.cnt["vX"] - 1
                bankfree[1] = p.cnt["vX"]



MODES = ["full"]


def kernel(**inputs):
    per_core = [make_core_inputs(c, inputs) for c in range(CORES)]
    cur = [pc.pop("x_in") for pc in per_core]
    for mode in MODES:
        nc = build_program(mode)
        in_maps = []
        for c in range(CORES):
            m = dict(per_core[c])
            m["x_in"] = cur[c]
            in_maps.append(m)
        res = run_bass_kernel_spmd(nc, in_maps, core_ids=list(range(CORES)))
        cur = [np.asarray(res.results[c]["x_out"]) for c in range(CORES)]
    return assemble_output(cur)


# version 09 (final)
# speedup vs baseline: 1.1672x; 1.1505x over previous
import math
import numpy as np
import ml_dtypes
from contextlib import ExitStack

import concourse.bass as bass
import concourse.mybir as mybir
from concourse.bass_utils import run_bass_kernel_spmd

F32 = mybir.dt.float32
BF16 = mybir.dt.bfloat16
AF = mybir.ActivationFunctionType
ALU = mybir.AluOpType
AX = mybir.AxisListType.X

CORES = 8
T, C = 8192, 2048
HD = 128
B0 = 14.0
B1 = 14.0
SCALE = 1.0 / math.sqrt(HD)
CLIP = 128.0 - 1e-6
BF = ml_dtypes.bfloat16
RG = [list(range(CORES))]


def tiles_of_core(c):
    return [8 * j + 7 - c for j in range(8)]



def _rope_tables(positions):
    inv = 1.0 / 10000.0 ** (np.arange(0, HD, 2, dtype=np.float64) / HD)
    fr = positions.astype(np.float64)[None, :] * np.concatenate([inv, inv])[:, None]
    cos = np.cos(fr)
    sin = np.sin(fr)
    sin[:64] = -sin[:64]
    return cos.astype(np.float32), sin.astype(np.float32)


def make_core_inputs(c, z):
    x = z["L0_attn_x"][0]
    cols = np.concatenate([np.arange(128 * g, 128 * g + 128)
                           for g in tiles_of_core(c)])
    d = {}
    d["x_in"] = np.ascontiguousarray(np.asarray(x, np.float32)[cols, :].T)
    cos, sin = _rope_tables(cols)
    d["rope_cosq"] = np.ascontiguousarray(cos * SCALE)
    d["rope_sinq"] = np.ascontiguousarray(sin * SCALE)
    d["rope_cosk"] = cos
    d["rope_sink"] = sin
    m = np.zeros((128, 8, 8, 128), np.float32)
    tri = (np.arange(128)[:, None] <= np.arange(128)[None, :]).astype(np.float32)
    for j in range(8):
        Dj = 8 * (j + 1)
        dj = Dj - c
        for idx in range(8):
            pp = Dj - 8 + idx
            if pp < dj - 1:
                m[:, j, idx, :] = 1.0
            elif pp == dj - 1:
                m[:, j, idx, :] = tri
    d["mask0"] = np.ascontiguousarray(m.reshape(128, 8 * 8 * 128)).astype(BF)
    psw = np.zeros((128, 128), np.float32)
    psw[(np.arange(128) + 64) % 128, np.arange(128)] = 1.0
    d["pswap"] = psw.astype(BF)
    cc = np.zeros((128, 8), np.float32)
    cc[:, 0] = -B0
    cc[:, 1] = -B1
    cc[:, 2] = 1e-5
    cc[:, 3] = 1e-11
    cc[:, 4] = 1e-6
    d["cconst"] = cc
    d["ones_bf"] = np.ones((128, 1), np.float32).astype(BF)
    d["ones_row"] = np.ones((1, 128), np.float32)
    r0, r1 = 256 * c, 256 * (c + 1)
    d["wq_s"] = np.ascontiguousarray(z["L0_attn_Wq"][r0:r1])
    d["wk_s"] = np.ascontiguousarray(z["L0_attn_Wk"][r0:r1])
    d["wv_s"] = np.ascontiguousarray(z["L0_attn_Wv"][r0:r1])
    d["wproj_s"] = np.ascontiguousarray(z["L0_attn_Wproj"][r0:r1])
    d["gup_s"] = np.ascontiguousarray(
        z["L0_ffn_gate_up_proj"][:, r0:r1].reshape(4 * 256, 4096))
    d["down_s"] = np.ascontiguousarray(
        z["L0_ffn_down_proj"][:, r0:r1].reshape(4 * 256, 2048))
    d["wqkv_s"] = np.ascontiguousarray(z["L1_attn_w_qkv"][:, r0:r1])
    d["wout_s"] = np.ascontiguousarray(z["L1_attn_w_out"][:, r0:r1])
    d["wfc_s"] = np.ascontiguousarray(z["L1_ffn_w_fc"][:, r0:r1])
    d["wproj2_s"] = np.ascontiguousarray(
        z["L1_ffn_w_proj"][:, 1024 * c:1024 * (c + 1)])
    return {k: np.asarray(v) for k, v in d.items()}


def assemble_output(xs):
    out = np.empty((1, T, C), np.float32)
    for c in range(CORES):
        for j, g in enumerate(tiles_of_core(c)):
            out[0, 128 * g:128 * g + 128, :] = xs[c][:, 128 * j:128 * j + 128].T
    return out



class P:
    def __init__(self, nc, es, mode):
        self.nc, self.es, self.mode = nc, es, mode
        self.cnt, self.sem, self.sb, self.dr, self.ccmark = {}, {}, {}, {}, {}

    def mksem(self, *names):
        for n in names:
            self.sem[n] = self.es.enter_context(self.nc.semaphore(n))
            self.cnt[n] = 0

    def inc(self, instr, name, n=1):
        instr.then_inc(self.sem[name], n)
        self.cnt[name] += n
        return self.cnt[name]

    def wait(self, eng, name, val=None):
        v = self.cnt[name] if val is None else val
        if v > 0:
            eng.wait_ge(self.sem[name], v)

    def sbuf(self, name, shape, dtype):
        h = self.es.enter_context(self.nc.sbuf_tensor("sb_" + name, shape, dtype))
        self.sb[name] = h
        return h

    def psum(self, name, shape, dtype=F32):
        h = self.es.enter_context(self.nc.psum_tensor(name, shape, dtype))
        self.sb[name] = h
        return h

    def dram(self, name, shape, dtype, kind=None, shared=False):
        kw = {}
        if kind:
            kw["kind"] = kind
        if shared:
            kw["addr_space"] = "Shared"
        h = self.nc.dram_tensor(name, shape, dtype, **kw)
        self.dr[name] = h
        return h

    def joinall(self, eng,
                names=("dS", "dG", "dT", "tMM", "tS", "aE", "aX", "vM", "vX")):
        for n in names:
            self.wait(eng, n)


ALLSEMS = ["dS", "dSa", "dSb", "dG", "dT", "dTa", "dTb", "cc", "tMM", "tS", "aE", "aX", "vM", "vX", "st"]


def build_program(mode, taps=False):
    nc = bass.Bass("TRN2", target_bir_lowering=False)
    es = ExitStack()
    p = P(nc, es, mode)
    p.taps = taps
    with es:
        _declare_io(p)
        _declare_sbuf(p)
        p.mksem(*ALLSEMS)
        p.psum("PS_A", [128, 1024])
        p.psum("PS_B", [128, 1024])
        p.psum("PS_C", [128, 512])
        p.psum("PS_D", [128, 512])
        p.psum("PS_E", [128, 512])
        p.psum("PS_F", [128, 512])
        p.blk = es.enter_context(nc.Block())

        _emit_loads(p)
        _emit_prep1(p)
        if mode in ("l0a", "full"):
            _emit_l0_qkv(p)
            _emit_l0_attn(p)
            _emit_proj(p, layer=0)
        _emit_prep2a(p)
        if mode in ("l0f", "full"):
            _emit_l0_ffn(p)
        _emit_prep2b(p)
        if mode in ("l1a", "full"):
            _emit_l1_qkv(p)
        _emit_prep2c(p)
        if mode in ("l1a", "full"):
            _emit_l1_attn(p)
            _emit_proj(p, layer=1)
        if mode in ("l1f", "full"):
            _emit_l1_ffn(p)
        _emit_store(p)
    return nc


def _declare_io(p):
    p.dram("x_in", [C, 1024], F32, kind="ExternalInput")
    p.dram("x_out", [C, 1024], F32, kind="ExternalOutput")
    for nm, sh in [("rope_cosq", [128, 1024]), ("rope_sinq", [128, 1024]),
                   ("rope_cosk", [128, 1024]), ("rope_sink", [128, 1024]),
                   ("cconst", [128, 8]), ("ones_row", [1, 128])]:
        p.dram(nm, sh, F32, kind="ExternalInput")
    for nm, sh in [("mask0", [128, 8 * 8 * 128]), ("pswap", [128, 128]),
                   ("ones_bf", [128, 1])]:
        p.dram(nm, sh, BF16, kind="ExternalInput")
    for nm, sh in [("wq_s", [256, 2048]), ("wk_s", [256, 512]), ("wv_s", [256, 512]),
                   ("wproj_s", [256, 2048]), ("gup_s", [1024, 4096]),
                   ("down_s", [1024, 2048]), ("wqkv_s", [6144, 256]),
                   ("wout_s", [2048, 256]), ("wfc_s", [8192, 256]),
                   ("wproj2_s", [2048, 1024])]:
        p.dram(nm, sh, F32, kind="ExternalInput")
    for nm, csh, gsh in [
        ("wq", [256, 2048], [2048, 2048]), ("wk", [256, 512], [2048, 512]),
        ("wv", [256, 512], [2048, 512]), ("wproj", [256, 2048], [2048, 2048]),
        ("gup", [1024, 4096], [8192, 4096]), ("down", [1024, 2048], [8192, 2048]),
        ("wqkvT", [256, 6144], [2048, 6144]), ("woutT", [256, 2048], [2048, 2048]),
        ("wfcT", [256, 8192], [2048, 8192]), ("wprojT", [1024, 2048], [8192, 2048]),
        ("k0", [512, 1024], [4096, 1024]), ("v0", [1024, 512], [8192, 512]),
    ]:
        p.dram(nm + "_c", csh, BF16)
        p.dram(nm + "_g", gsh, BF16, shared=True)
    for hg in range(4):
        p.dram(f"k1_c{hg}", [512, 1024], BF16)
        p.dram(f"k1_g{hg}", [4096, 1024], BF16, shared=True)
        p.dram(f"v1_c{hg}", [1024, 512], BF16)
        p.dram(f"v1_g{hg}", [8192, 512], BF16, shared=True)
    p.dram("wqkv_bf", [6144, 256], BF16)
    p.dram("wout_bf", [2048, 256], BF16)
    p.dram("wfc_bf", [8192, 256], BF16)
    p.dram("wproj2_bf", [2048, 1024], BF16)
    p.dram("gam_c", [128, 2], F32)
    p.dram("gam_g", [128, 2], F32, shared=True)
    if p.taps == 2:
        p.dram("gam_dbg", [128, 2], F32, kind="ExternalOutput")
        p.dram("wfcT_dbg", [2048, 8192], BF16, kind="ExternalOutput")
        p.dram("wprojT_dbg", [8192, 2048], BF16, kind="ExternalOutput")
        p.dram("xhat_dbg", [1024, 2048], BF16, kind="ExternalOutput")
        p.dram("h0_dbg", [128, 8192], BF16, kind="ExternalOutput")
        p.dram("xh2_dbg", [256, 8192], BF16, kind="ExternalOutput")
        p.dram("xbf_dbg", [2048, 1024], BF16, kind="ExternalOutput")
        p.dram("sm_dbg", [128, 16], F32, kind="ExternalOutput")
        p.dram("xt0_dbg", [128, 2048], BF16, kind="ExternalOutput")
        p.dram("x2fm_dbg", [128, 16384], BF16, kind="ExternalOutput")
        p.dram("wth_dbg", [128, 4096], BF16, kind="ExternalOutput")
    if p.taps == 1:
        p.dram("xln_dbg", [128, 16384], BF16, kind="ExternalOutput")
        p.dram("q_dbg", [128, 16384], BF16, kind="ExternalOutput")
        p.dram("k0g_dbg", [4096, 1024], BF16, kind="ExternalOutput")
        p.dram("v0g_dbg", [8192, 512], BF16, kind="ExternalOutput")
        p.dram("ob_dbg", [128, 16384], BF16, kind="ExternalOutput")
    p.dram("xbf_d", [2048, 1024], BF16)
    p.dram("xhat_d", [1024, 2048], BF16)
    p.dram("xh2_d", [256, 8192], BF16)


def _declare_sbuf(p):
    p.sbuf("x", [128, 16, 1024], F32)
    p.sbuf("QB", [128, 16, 1024], BF16)
    p.sbuf("OB", [128, 16, 1024], BF16)
    p.sbuf("U1", [128, 16, 1024], BF16)
    p.sbuf("AR1", [128, 16384], BF16)
    p.sbuf("WT", [128, 2, 16, 128], BF16)
    p.sbuf("TMP", [128, 1024], F32)
    p.sbuf("statm", [128, 16], F32)
    p.sbuf("cconst", [128, 8], F32)
    p.sbuf("pswap", [128, 128], BF16)
    p.sbuf("ones_bf", [128, 1], BF16)
    p.sbuf("ones_row", [1, 128], F32)
    p.sbuf("grow", [1, 16], F32)



def V_ROPE(p):
    return p.sb["AR1"][:, 0:8192].bitcast(F32).rearrange("p (a n) -> p a n", n=1024)

def V_RMAT(p):
    return p.sb["AR1"][:, 8192:10240].bitcast(F32)

def V_RTMP(p):
    return p.sb["AR1"][:, 10240:12288].bitcast(F32)

def V_KS(p):
    return p.sb["AR1"][:, 0:8192].rearrange("p (t n) -> p t n", n=128)

def V_VS(p):
    return p.sb["AR1"][:, 8192:16384].rearrange("p (t n) -> p t n", n=128)

def V_MSK(p):
    return p.sb["U1"][:, 0:8, :].rearrange("p a n -> p (a n)").rearrange(
        "p (t n) -> p t n", n=128)

def V_PT(p, i):
    return p.sb["U1"][:, 8 + i, :]

def V_XLN(p):
    return p.sb["U1"]

def V_KC(p):
    return p.sb["OB"][:, 0:4, :]

def V_VC(p):
    return p.sb["OB"][:, 4:8, :].rearrange("p a n -> p (a n)").rearrange(
        "p (a n) -> p a n", n=512)

def V_QL0(p):
    return p.sb["QB"][:, :, :].rearrange("p a n -> p (a n)").rearrange(
        "p (k s h n) -> p k s h n", k=4, s=8, h=4)



def _emit_loads(p):
    blk = p.blk

    @blk.sync
    def _(sy):
        p.inc(sy.dma_start(
            out=p.sb["x"][:, :, :],
            in_=p.dr["x_in"].ap().rearrange("(a p) n -> p a n", p=128)), "dS", 16)
        for nm in ("cconst", "ones_row", "pswap", "ones_bf"):
            p.inc(sy.dma_start(out=p.sb[nm][:, :], in_=p.dr[nm].ap()), "dS", 16)
        rope = V_ROPE(p)
        if p.mode in ("l0a", "full"):
            for i, nm in enumerate(["rope_cosq", "rope_sinq",
                                    "rope_cosk", "rope_sink"]):
                p.inc(sy.dma_start(out=rope[:, i, :], in_=p.dr[nm].ap()), "dS", 16)
    p.c_xload = p.cnt["dS"]


def _emit_store(p):
    blk = p.blk

    @blk.sync
    def _(sy):
        p.joinall(sy)
        p.inc(sy.dma_start(
            out=p.dr["x_out"].ap().rearrange("(a p) n -> p a n", p=128),
            in_=p.sb["x"][:, :, :]), "st", 16)
        p.wait(sy, "st")



def _emit_prep1(p):
    blk, m = p.blk, p.mode
    names = []
    if m in ("l0a", "full"):
        names += ["wk", "wv", "wq", "wproj"]
    if m in ("l0f", "full"):
        names += ["gup", "down"]
    need_l1a = m in ("l1a", "full")

    early = [nm for nm in names if nm in ("wk", "wv", "wq")] \
        if p.mode == "full" else names
    p.deferred_ags = [nm for nm in names if nm not in early]

    @blk.gpsimd
    def _(g):
        for nm in names:
            i = p.inc(g.dma_start(out=p.dr[nm + "_c"].ap(),
                                  in_=p.dr[nm + "_s"].ap()), "dG", 16)
        g.wait_ge(p.sem["dG"], p.cnt["dG"])
        for nm in early:
            p.inc(g.collective_compute(
                "AllGather", ALU.bypass, replica_groups=RG,
                ins=[p.dr[nm + "_c"].ap()], outs=[p.dr[nm + "_g"].ap()]), "cc", 1)
            p.ccmark[nm] = p.cnt["cc"]
        if need_l1a:
            for src, bnc in [("wqkv_s", "wqkv_bf"), ("wout_s", "wout_bf")]:
                p.inc(g.dma_start(out=p.dr[bnc].ap(), in_=p.dr[src].ap()), "dG", 16)
    c_cast = p.cnt["dG"]

    if need_l1a:
        qbf = p.sb["QB"][:, :, :].rearrange("p a n -> p (a n)")

        @blk.scalar
        def _(s):
            s.wait_ge(p.sem["dG"], c_cast)
            for bnc, contrib, nrows in [("wqkv_bf", "wqkvT_c", 6144),
                                        ("wout_bf", "woutT_c", 2048)]:
                for ci in range(2):
                    p.inc(s.dma_start(out=qbf[:, 0:nrows],
                                      in_=p.dr[bnc].ap()[:, ci * 128:(ci + 1) * 128],
                                      transpose=True), "dT", 16)
                    s.wait_ge(p.sem["dT"], p.cnt["dT"])
                    p.inc(s.dma_start(
                        out=p.dr[contrib].ap()[ci * 128:(ci + 1) * 128, :],
                        in_=qbf[:, 0:nrows]), "dT", 16)
                    s.wait_ge(p.sem["dT"], p.cnt["dT"])

        if p.mode == "full":
            p.deferred_ags += ["wqkvT", "woutT"]
            p.deferred_dt = p.cnt["dT"]
        else:
            @blk.gpsimd
            def _(g):
                g.wait_ge(p.sem["dT"], p.cnt["dT"])
                for nm in ("wqkvT", "woutT"):
                    p.inc(g.collective_compute(
                        "AllGather", ALU.bypass, replica_groups=RG,
                        ins=[p.dr[nm + "_c"].ap()], outs=[p.dr[nm + "_g"].ap()]),
                        "cc", 1)
                    p.ccmark[nm] = p.cnt["cc"]
    p.c_prep1_dT = p.cnt["dT"]


def _emit_prep2a(p):
    if p.mode not in ("l1f", "full"):
        return
    blk = p.blk
    obf = p.sb["OB"][:, :, :].rearrange("p a n -> p (a n)")
    obq = obf[:, 0:8192].bitcast(F32)
    sm = p.sb["statm"]
    quarters = [("wfc_s", q, 2048, 256, 0) for q in range(4)] + \
               [("wproj2_s", q, 512, 1024, 1) for q in range(4)]
    for src_, q, rows, ncol, mi in quarters:
        fcols = rows * ncol // 128

        @blk.sync
        def _(sy, src_=src_, q=q, rows=rows, ncol=ncol, fcols=fcols, mi=mi):
            if mi == 0 and q == 0:
                p.joinall(sy, ("vX", "aE", "aX", "tS"))
            else:
                p.wait(sy, "vX")
            src_ap = p.dr[src_].ap()[q * rows:(q + 1) * rows, :]
            u3 = obq[:, 0:fcols].rearrange("p (a n) -> p a n", n=ncol)
            p.inc(sy.dma_start(out=u3,
                               in_=src_ap.rearrange("(a p) n -> p a n", p=128)),
                  "dS", 16)
        cL = p.cnt["dS"]

        @blk.vector
        def _(v, cL=cL, fcols=fcols, q=q, mi=mi):
            v.wait_ge(p.sem["dS"], cL)
            p.inc(v.reduce_sum(sm[:, q:q + 1], obq[:, 0:fcols], AX,
                               apply_absolute_value=True), "vX", 1)
            if q == 3:
                v.wait_ge(p.sem["vX"], p.cnt["vX"])
                p.inc(v.tensor_tensor(sm[:, 6:7], sm[:, 0:1], sm[:, 1:2],
                                      ALU.add), "vX", 1)
                v.wait_ge(p.sem["vX"], p.cnt["vX"])
                p.inc(v.tensor_tensor(sm[:, 7:8], sm[:, 2:3], sm[:, 3:4],
                                      ALU.add), "vX", 1)
                v.wait_ge(p.sem["vX"], p.cnt["vX"])
                p.inc(v.tensor_tensor(sm[:, 4 + mi:5 + mi], sm[:, 6:7],
                                      sm[:, 7:8], ALU.add), "vX", 1)
    @blk.vector
    def _(v):
        p.inc(v.memset(sm[:, 15:16], 1.0), "vX", 1)
    c_comb = p.cnt["vX"]

    @blk.gpsimd
    def _(g):
        g.wait_ge(p.sem["vX"], c_comb)
        i = p.inc(g.dma_start(out=p.dr["gam_c"].ap(), in_=sm[:, 4:6]),
                  "dG", 16)
        g.wait_ge(p.sem["dG"], i)
        p.inc(g.collective_compute(
            "AllReduce", ALU.add, replica_groups=RG,
            ins=[p.dr["gam_c"].ap()], outs=[p.dr["gam_g"].ap()]), "cc", 1)
        p.ccmark["gam"] = p.cnt["cc"]
    p.c_p2a = p.cnt["vX"]


def _emit_prep2b(p):
    if p.mode not in ("l1f", "full"):
        return
    blk = p.blk
    obf = p.sb["OB"][:, :, :].rearrange("p a n -> p (a n)")
    obq = obf[:, 0:8192].bitcast(F32)
    obo = obf[:, 8192:12288]
    sm = p.sb["statm"]

    @blk.sync
    def _(sy):
        p.wait(sy, "cc", p.ccmark["gam"])
        p.inc(sy.dma_start(out=sm[:, 12:14], in_=p.dr["gam_g"].ap()),
              "dG", 16)
    c_gl = p.cnt["dG"]
    c_vtail = p.cnt["vX"]

    @blk.tensor
    def _(t):
        t.wait_ge(p.sem["dG"], c_gl)
        t.wait_ge(p.sem["vX"], c_vtail)
        p.inc(t.matmul(p.sb["PS_F"][0:1, 0:2], sm[:, 15:16], sm[:, 12:14],
                       start=True, stop=True), "tMM", 1)
    c_gm = p.cnt["tMM"]

    @blk.vector
    def _(v):
        v.wait_ge(p.sem["tMM"], c_gm)
        p.inc(v.tensor_copy(p.sb["grow"][:, 4:6], p.sb["PS_F"][0:1, 0:2]),
              "vX", 1)
        v.wait_ge(p.sem["vX"], p.cnt["vX"])
        p.inc(v.tensor_scalar(p.sb["grow"][:, 8:10], p.sb["grow"][:, 4:6],
                              1.0 / (8192.0 * 2048.0), 1e-6, ALU.mult, ALU.add),
              "vX", 1)
        v.wait_ge(p.sem["vX"], p.cnt["vX"])
        v.tensor_scalar_mul(p.sb["grow"][:, 12:14], p.sb["grow"][:, 4:6],
                            1.0 / (8192.0 * 2048.0))
        p.inc(v.reciprocal(p.sb["grow"][:, 10:12], p.sb["grow"][:, 8:10]), "vX", 1)
    c_gv = p.cnt["vX"]

    @blk.tensor
    def _(t):
        t.wait_ge(p.sem["vX"], c_gv)
        p.inc(t.matmul(p.sb["PS_F"][:, 0:4], p.sb["ones_row"][:, :],
                       p.sb["grow"][0:1, 10:14], start=True, stop=True), "tMM", 1)
    c_gb = p.cnt["tMM"]

    @blk.vector
    def _(v):
        v.wait_ge(p.sem["tMM"], c_gb)
        p.inc(v.tensor_copy(sm[:, 8:12], p.sb["PS_F"][:, 0:4]), "vX", 1)
    p.c_gam = p.cnt["vX"]

    for src_, bnc, q, rows, ncol, icol, gcol in [
        ("wfc_s", "wfc_bf", 0, 2048, 256, 8, 10),
        ("wfc_s", "wfc_bf", 1, 2048, 256, 8, 10),
        ("wfc_s", "wfc_bf", 2, 2048, 256, 8, 10),
        ("wfc_s", "wfc_bf", 3, 2048, 256, 8, 10),
        ("wproj2_s", "wproj2_bf", 0, 512, 1024, 9, 11),
        ("wproj2_s", "wproj2_bf", 1, 512, 1024, 9, 11),
        ("wproj2_s", "wproj2_bf", 2, 512, 1024, 9, 11),
        ("wproj2_s", "wproj2_bf", 3, 512, 1024, 9, 11),
    ]:
        fcols = rows * ncol // 128

        @blk.sync
        def _(sy, src_=src_, q=q, rows=rows, ncol=ncol, fcols=fcols, mi=icol):
            if mi == 8 and q == 0:
                p.wait(sy, "vX", p.c_p2a)
            else:
                p.wait(sy, "vX")
            src_ap = p.dr[src_].ap()[q * rows:(q + 1) * rows, :]
            u3 = obq[:, 0:fcols].rearrange("p (a n) -> p a n", n=ncol)
            p.inc(sy.dma_start(out=u3,
                               in_=src_ap.rearrange("(a p) n -> p a n", p=128)),
                  "dS", 16)
        cL = p.cnt["dS"]

        @blk.vector
        def _(v, cL=cL, fcols=fcols, icol=icol, gcol=gcol):
            v.wait_ge(p.sem["dS"], cL)
            p.wait(v, "vX", p.c_gam)
            tmp = p.sb["TMP"][:, 0:512]
            rm = p.sb["TMP"][:, 512:1024]
            i = None
            for off in range(0, fcols, 512):
                w = min(512, fcols - off)
                sl = obq[:, off:off + w]
                v.tensor_scalar_mul(tmp[:, :w], sl, sm[:, icol:icol + 1])
                v.tensor_scalar(rm[:, :w], tmp[:, :w], 0.5, None, ALU.is_gt)
                v.tensor_scalar(tmp[:, :w], tmp[:, :w], -0.5, None, ALU.is_lt)
                v.tensor_tensor(rm[:, :w], rm[:, :w], tmp[:, :w], ALU.subtract)
                i = v.tensor_scalar_mul(obo[:, off:off + w], rm[:, :w],
                                        sm[:, gcol:gcol + 1])
            p.inc(i, "vX", 1)
        cT = p.cnt["vX"]

        @blk.sync
        def _(sy, bnc=bnc, q=q, rows=rows, ncol=ncol, fcols=fcols, cT=cT):
            sy.wait_ge(p.sem["vX"], cT)
            dst = p.dr[bnc].ap()[q * rows:(q + 1) * rows, :]
            ob3 = obo[:, 0:fcols].rearrange("p (a n) -> p a n", n=ncol)
            p.inc(sy.dma_start(out=dst.rearrange("(a p) n -> p a n", p=128),
                               in_=ob3), "dS", 16)
    c_bf = p.cnt["dS"]

    @blk.scalar
    def _(s):
        s.wait_ge(p.sem["dS"], c_bf)
        for bnc, contrib, nrows, ncols in [("wfc_bf", "wfcT_c", 8192, 256),
                                           ("wproj2_bf", "wprojT_c", 2048, 1024)]:
            for ci in range(ncols // 128):
                p.inc(s.dma_start(out=obf[:, 0:nrows],
                                  in_=p.dr[bnc].ap()[:, ci * 128:(ci + 1) * 128],
                                  transpose=True), "dT", 16)
                s.wait_ge(p.sem["dT"], p.cnt["dT"])
                p.inc(s.dma_start(
                    out=p.dr[contrib].ap()[ci * 128:(ci + 1) * 128, :],
                    in_=obf[:, 0:nrows]), "dT", 16)
                s.wait_ge(p.sem["dT"], p.cnt["dT"])
    p.c_prep2_dT = p.cnt["dT"]


def _emit_prep2c(p):
    if p.mode not in ("l1f", "full"):
        return
    blk = p.blk

    @blk.gpsimd
    def _(g):
        g.wait_ge(p.sem["dT"], p.c_prep2_dT)
        for nm in ("wfcT", "wprojT"):
            p.inc(g.collective_compute(
                "AllGather", ALU.bypass, replica_groups=RG,
                ins=[p.dr[nm + "_c"].ap()], outs=[p.dr[nm + "_g"].ap()]), "cc", 1)
            p.ccmark[nm] = p.cnt["cc"]



def _emit_ln_stream(p, tag):
    blk = p.blk
    x = p.sb["x"]
    XLN = V_XLN(p)
    wtf = p.sb["WT"][:, :, :, :].rearrange("p b a n -> p (b a n)")
    wsets = [(wtf[:, 0:1024], wtf[:, 1024:2048]),
             (wtf[:, 2048:3072], wtf[:, 3072:4096])]
    PS_A, PS_B = p.sb["PS_A"], p.sb["PS_B"]
    TMP = p.sb["TMP"]
    RM = p.sb["U1"][:, :, :].rearrange("p a n -> p (a n)").bitcast(F32)[:, 0:1024]

    mm_cnt = [0, 0]
    for ic in range(16):
        wta, wtb = wsets[ic % 2]

        @blk.vector
        def _(v, ic=ic, wta=wta, wtb=wtb):
            if ic == 0:
                p.joinall(v)
            elif ic == 1:
                pass
            else:
                p.wait(v, "tS", mm_cnt[ic % 2])
            v.tensor_copy(wta[:, :], x[:, ic, :])
            p.inc(v.tensor_tensor(wtb[:, :], x[:, ic, :], x[:, ic, :], ALU.mult),
                  "vX", 1)
        cV = p.cnt["vX"]

        @blk.tensor
        def _(t, ic=ic, cV=cV, wta=wta, wtb=wtb):
            t.wait_ge(p.sem["vX"], cV)
            i = None
            for tc in range(2):
                t.matmul(PS_A[0:1, tc * 512:(tc + 1) * 512], p.sb["ones_bf"][:, :],
                         wta[:, tc * 512:(tc + 1) * 512],
                         start=(ic == 0), stop=(ic == 15))
                i = t.matmul(PS_B[0:1, tc * 512:(tc + 1) * 512],
                             p.sb["ones_bf"][:, :],
                             wtb[:, tc * 512:(tc + 1) * 512],
                             start=(ic == 0), stop=(ic == 15))
            p.inc(i, "tS", 1)
        mm_cnt[ic % 2] = p.cnt["tS"]
    c_mm = p.cnt["tS"]

    @blk.vector
    def _(v):
        v.wait_ge(p.sem["tS"], c_mm)
        p.inc(v.tensor_scalar_mul(TMP[0:1, :], PS_A[0:1, :], 1.0 / 2048.0),
              "vX", 1)
        v.wait_ge(p.sem["vX"], p.cnt["vX"])
        p.inc(v.tensor_tensor(RM[0:1, :], TMP[0:1, :], TMP[0:1, :], ALU.mult),
              "vX", 1)
        v.wait_ge(p.sem["vX"], p.cnt["vX"])
        p.inc(v.scalar_tensor_tensor(RM[0:1, :], PS_B[0:1, :], 1.0 / 2048.0,
                                     RM[0:1, :], ALU.mult, ALU.subtract), "vX", 1)
    c_v1 = p.cnt["vX"]

    @blk.scalar
    def _(s):
        s.wait_ge(p.sem["vX"], c_v1)
        p.inc(s.activation(RM[0:1, :], RM[0:1, :], AF.Sqrt,
                           bias=p.sb["cconst"][0:1, 2:3]), "aX", 1)
    c_sq = p.cnt["aX"]

    @blk.vector
    def _(v):
        v.wait_ge(p.sem["aX"], c_sq)
        p.inc(v.reciprocal(RM[0:1, :], RM[0:1, :]), "vX", 1)
        p.inc(v.tensor_scalar_mul(TMP[0:1, :], TMP[0:1, :], -1.0), "vX", 1)
    c_v2 = p.cnt["vX"]

    @blk.tensor
    def _(t):
        t.wait_ge(p.sem["vX"], c_v2)
        i = None
        for tc in range(2):
            t.matmul(PS_A[:, tc * 512:(tc + 1) * 512], p.sb["ones_row"][:, :],
                     TMP[0:1, tc * 512:(tc + 1) * 512], start=True, stop=True)
            i = t.matmul(PS_B[:, tc * 512:(tc + 1) * 512], p.sb["ones_row"][:, :],
                         RM[0:1, tc * 512:(tc + 1) * 512], start=True, stop=True)
        p.inc(i, "tS", 1)
    c_r1 = p.cnt["tS"]

    ln_cnts = []

    @blk.vector
    def _(v, ln_cnts=ln_cnts):
        v.wait_ge(p.sem["tS"], c_r1)
        for ic in range(16):
            v.tensor_tensor(TMP[:, :], x[:, ic, :], PS_A[:, :], ALU.add)
            p.inc(v.tensor_tensor(XLN[:, ic, :], TMP[:, :], PS_B[:, :],
                                  ALU.mult), "vX", 1)
            ln_cnts.append(p.cnt["vX"])
    setattr(p, f"c_ln_{tag}", p.cnt["vX"])
    setattr(p, f"ln_cnts_{tag}", ln_cnts)



def _wtile_dma(p, sy, wg_name, oc, buf, colw=128, sem=None):
    sem = sem or ("dSa" if buf == 0 else "dSb")
    src = p.dr[wg_name].ap()[:, oc * colw:(oc + 1) * colw]
    src = src.rearrange("(a p) n -> p a n", p=128)
    p.inc(sy.dma_start(out=p.sb["WT"][:, buf, 0:8, 0:colw], in_=src[:, 0:8, :]),
          sem, 16)
    p.inc(sy.dma_start(out=p.sb["WT"][:, buf, 8:16, 0:colw], in_=src[:, 8:16, :]),
          sem, 16)
    return p.cnt[sem]


def _emit_l0_qkv(p):
    blk = p.blk
    _emit_ln_stream(p, "l0a")
    c_ln = p.c_ln_l0a
    XLN = V_XLN(p)
    rope = V_ROPE(p)
    PS_A, PS_B = p.sb["PS_A"], p.sb["PS_B"]
    QL0 = V_QL0(p)
    rope_cnt = []

    chunks = [("wq_g", oc, "q") for oc in range(16)] + \
             [("wk_g", oc, "k") for oc in range(4)]
    WTP = p.sb["WT"][:, :, :, :].rearrange("p b a n -> p (b a n)").rearrange(
        "p (a n) -> p a n", n=256)
    pair_cons = [p.cnt["tS"]]
    for idx, (wg, oc, kind) in enumerate(chunks):
        half = idx % 2
        qraw = p.sb["OB"][:, 8 + half, :]

        if half == 0:
            @blk.sync
            def _(sy, wg=wg, oc=oc):
                p.wait(sy, "cc", p.ccmark["wq" if wg == "wq_g" else "wk"])
                p.wait(sy, "tS", pair_cons[0])
                srcp = p.dr[wg].ap()[:, oc * 128:oc * 128 + 256]
                srcp = srcp.rearrange("(a p) n -> p a n", p=128)
                p.inc(sy.dma_start(out=WTP[:, :, :], in_=srcp), "dSa", 16)
            cW = p.cnt["dSa"]
            p._cW_pair = cW
        else:
            cW = p._cW_pair

        @blk.tensor
        def _(t, half=half, cW=cW, idx=idx):
            t.wait_ge(p.sem["dSa"], cW)
            t.wait_ge(p.sem["vX"], c_ln)
            if idx >= 1:
                p.wait(t, "aE")
            i = None
            for tc in range(2):
                for ic in range(16):
                    i = t.matmul(PS_A[:, tc * 512:(tc + 1) * 512],
                                 WTP[:, ic, half * 128:half * 128 + 128],
                                 XLN[:, ic, tc * 512:(tc + 1) * 512],
                                 start=(ic == 0), stop=(ic == 15))
            p.inc(i, "tS", 1)
        if half == 1:
            pair_cons[0] = p.cnt["tS"]
        cQ = p.cnt["tS"]

        @blk.scalar
        def _(s, cQ=cQ, qraw=qraw, idx=idx):
            s.wait_ge(p.sem["tS"], cQ)
            if idx >= 2:
                p.wait(s, "vX", rope_cnt[idx - 2])
            p.inc(s.copy(qraw[:, :], PS_A[:, :]), "aE", 1)
        cE = p.cnt["aE"]

        @blk.tensor
        def _(t, cE=cE, qraw=qraw, idx=idx):
            t.wait_ge(p.sem["aE"], cE)
            if idx >= 1:
                p.wait(t, "vX", rope_cnt[idx - 1])
            i = None
            for tc in range(2):
                i = t.matmul(PS_B[:, tc * 512:(tc + 1) * 512], p.sb["pswap"][:, :],
                             qraw[:, tc * 512:(tc + 1) * 512],
                             start=True, stop=True)
            p.inc(i, "tS", 1)
        cS = p.cnt["tS"]

        @blk.vector
        def _(v, oc=oc, kind=kind, cS=cS, qraw=qraw):
            v.wait_ge(p.sem["tS"], cS)
            ct = 0 if kind == "q" else 2
            st = 1 if kind == "q" else 3
            TMP, RT = p.sb["TMP"], V_RTMP(p)
            v.tensor_tensor(TMP[:, :], qraw[:, :], rope[:, ct, :], ALU.mult)
            v.tensor_tensor(RT[:, :], PS_B[:, :], rope[:, st, :], ALU.mult)
            if kind == "q":
                out = QL0[:, oc // 4, :, oc % 4, :]
                t3 = TMP[:, :].rearrange("p (s n) -> p s n", n=128)
                r3 = RT[:, :].rearrange("p (s n) -> p s n", n=128)
                p.inc(v.tensor_tensor(out, t3, r3, ALU.add), "vX", 1)
            else:
                p.inc(v.tensor_tensor(V_KC(p)[:, oc, :], TMP[:, :], RT[:, :],
                                      ALU.add), "vX", 1)
        rope_cnt.append(p.cnt["vX"])
    c_qk = p.cnt["vX"]

    v_cons = [p.cnt["tS"]]
    for vh in range(4):
        half = vh % 2

        if half == 0:
            @blk.sync
            def _(sy, vh=vh):
                p.wait(sy, "cc", p.ccmark["wv"])
                p.wait(sy, "tS", v_cons[0])
                srcp = p.dr["wv_g"].ap()[:, vh * 128:vh * 128 + 256]
                srcp = srcp.rearrange("(a p) n -> p a n", p=128)
                p.inc(sy.dma_start(out=WTP[:, :, :], in_=srcp), "dSa", 16)
            cW = p.cnt["dSa"]
            p._cW_pair = cW
        else:
            cW = p._cW_pair
        aE_base = p.cnt["aE"]

        @blk.tensor
        def _(t, vh=vh, half=half, cW=cW, aE_base=aE_base):
            t.wait_ge(p.sem["dSa"], cW)
            t.wait_ge(p.sem["vX"], c_ln)
            for tt in range(8):
                ps = p.sb["PS_C"] if tt % 2 == 0 else p.sb["PS_D"]
                if vh > 0 or tt >= 2:
                    t.wait_ge(p.sem["aE"], aE_base + tt - 2 + 1
                              if tt >= 2 else aE_base)
                i = None
                for ic in range(16):
                    i = t.matmul(ps[:, 0:128], XLN[:, ic, tt * 128:(tt + 1) * 128],
                                 WTP[:, ic, half * 128:half * 128 + 128],
                                 start=(ic == 0), stop=(ic == 15))
                p.inc(i, "tS", 1)
        if half == 1:
            v_cons[0] = p.cnt["tS"]
        cV8 = p.cnt["tS"]

        @blk.scalar
        def _(s, vh=vh, cV8=cV8):
            for tt in range(8):
                ps = p.sb["PS_C"] if tt % 2 == 0 else p.sb["PS_D"]
                s.wait_ge(p.sem["tS"], cV8 - 7 + tt)
                p.inc(s.copy(V_VC(p)[:, tt, vh * 128:(vh + 1) * 128],
                             ps[:, 0:128]), "aE", 1)
    c_v = p.cnt["aE"]

    @blk.sync
    def _(sy):
        p.wait(sy, "vX", c_qk)
        p.inc(sy.dma_start(
            out=p.dr["k0_c"].ap().rearrange("(a p) n -> p a n", p=128),
            in_=V_KC(p)[:, :, :]), "dS", 16)
        p.wait(sy, "aE", c_v)
        p.inc(sy.dma_start(
            out=p.dr["v0_c"].ap().rearrange("(a p) n -> p a n", p=128),
            in_=V_VC(p)[:, :, :]), "dS", 16)
    c_ctr = p.cnt["dS"]

    @blk.gpsimd
    def _(g):
        g.wait_ge(p.sem["dS"], c_ctr)
        for nm in ("k0", "v0"):
            p.inc(g.collective_compute(
                "AllGather", ALU.bypass, replica_groups=RG,
                ins=[p.dr[nm + "_c"].ap()], outs=[p.dr[nm + "_g"].ap()]), "cc", 1)
            p.ccmark[nm] = p.cnt["cc"]
        for nm in getattr(p, "deferred_ags", []):
            if nm in ("wqkvT", "woutT"):
                g.wait_ge(p.sem["dT"], p.deferred_dt)
            p.inc(g.collective_compute(
                "AllGather", ALU.bypass, replica_groups=RG,
                ins=[p.dr[nm + "_c"].ap()], outs=[p.dr[nm + "_g"].ap()]), "cc", 1)
            p.ccmark[nm] = p.cnt["cc"]
        p.deferred_ags = []



def _emit_kv_load(p, kg_name, vg_name, krow_off, vcol_off):
    blk = p.blk
    KS, VS = V_KS(p), V_VS(p)

    @blk.sync
    def _(sy):
        kg = p.dr[kg_name].ap()
        vg = p.dr[vg_name].ap()
        for r in range(8):
            src = kg[512 * r + krow_off:512 * r + krow_off + 128, :]
            src = src.rearrange("p (j n) -> p j n", n=128)
            p.inc(sy.dma_start(out=KS[:, 7 - r:64:8, :], in_=src), "dS", 16)
            vsrc = vg[1024 * r:1024 * (r + 1), vcol_off:vcol_off + 128]
            vsrc = vsrc.rearrange("(j p) n -> p j n", p=128)
            p.inc(sy.dma_start(out=VS[:, 7 - r:64:8, :], in_=vsrc), "dS", 16)
    return p.cnt["dS"]


def _emit_attn_group(p, rhs_q, out_ap, Dj, mask_slot, bias_col, scale):
    blk = p.blk
    KS, VS = V_KS(p), V_VS(p)
    PS_A, PS_B = p.sb["PS_A"], p.sb["PS_B"]
    psO, psD, psR = p.sb["PS_C"], p.sb["PS_E"], p.sb["PS_F"]
    npair = Dj // 2
    tS0, aE0, vM0, vX0 = (p.cnt["tS"], p.cnt["aE"], p.cnt["vM"], p.cnt["vX"])

    def pt(pr):
        return V_PT(p, pr % 2)

    @blk.tensor
    def _(t):

        def emit_S(pp):
            pr = pp // 2
            reg = PS_A if pr % 2 == 0 else PS_B
            t.wait_ge(p.sem["aE"], aE0 + pr - 1 if pr >= 2 else aE0)
            t.matmul(reg[:, (pp % 2) * 512:(pp % 2) * 512 + 512], KS[:, pp, :],
                     rhs_q, start=True, stop=True).then_inc(p.sem["tS"], 1)

        for pp0 in range(min(4, Dj)):
            emit_S(pp0)
        i = None
        for pp in range(Dj):
            pr = pp // 2
            t.wait_ge(p.sem["aE"], aE0 + pr + 1)
            if mask_slot is not None and pp >= Dj - 8:
                mp = (pp - (Dj - 8)) // 2
                t.wait_ge(p.sem["vM"], vM0 + mp + 1)
            if pp == 0:
                p.wait(t, "vX", vX0)
            src = pt(pr)[:, (pp % 2) * 512:(pp % 2) * 512 + 512]
            t.matmul(psO[:, :], VS[:, pp, :], src,
                     start=(pp == 0), stop=(pp == Dj - 1))
            i = t.matmul(psD[0:1, :], p.sb["ones_bf"][:, :], src,
                         start=(pp == 0), stop=(pp == Dj - 1))
            if pp % 2 == 1 and pp + 4 < Dj:
                emit_S(pp + 3)
                emit_S(pp + 4)
        i.then_inc(p.sem["tS"], 1)
        t.wait_ge(p.sem["vX"], vX0 + 1)
        t.matmul(psR[:, :], p.sb["ones_row"][:, :], p.sb["TMP"][0:1, 512:1024],
                 start=True, stop=True).then_inc(p.sem["tMM"], 1)
    p.cnt["tS"] = tS0 + Dj + 1
    p.cnt["tMM"] += 1
    c_av = p.cnt["tS"]
    c_r1 = p.cnt["tMM"]

    @blk.scalar
    def _(s):
        for pr in range(npair):
            s.wait_ge(p.sem["tS"], tS0 + 2 * pr + 2)
            reg = PS_A if pr % 2 == 0 else PS_B
            s.activation(pt(pr)[:, :], reg[:, :], AF.Exp,
                         bias=p.sb["cconst"][:, bias_col:bias_col + 1],
                         scale=scale).then_inc(p.sem["aE"], 1)
    p.cnt["aE"] = aE0 + npair

    @blk.vector
    def _(v):
        if mask_slot is not None:
            MSK = V_MSK(p)
            for mp in range(4):
                pr = npair - 4 + mp
                v.wait_ge(p.sem["aE"], aE0 + pr + 1)
                buf = pt(pr).rearrange("p (a h n) -> p a h n", a=2, h=4)
                mblk = MSK[:, mask_slot * 8 + 2 * mp:mask_slot * 8 + 2 * mp + 2, :]
                mblk = mblk.unsqueeze(2).broadcast_to([128, 2, 4, 128])
                v.tensor_tensor(buf, buf, mblk, ALU.mult).then_inc(p.sem["vM"], 1)
            p.cnt["vM"] = vM0 + 4
        v.wait_ge(p.sem["tS"], c_av)
        p.inc(v.reciprocal(p.sb["TMP"][0:1, 512:1024], psD[0:1, :]), "vX", 1)
        v.wait_ge(p.sem["tMM"], c_r1)
        v.tensor_copy(p.sb["TMP"][:, 0:512], psR[:, :])
        if out_ap.ndim == 3:
            p.inc(v.tensor_tensor(
                out_ap, psO[:, :].rearrange("p (h n) -> p h n", n=128),
                p.sb["TMP"][:, 0:512].rearrange("p (h n) -> p h n", n=128),
                ALU.mult), "vX", 1)
        else:
            p.inc(v.tensor_tensor(out_ap, psO[:, :], p.sb["TMP"][:, 0:512],
                                  ALU.mult), "vX", 1)


def _emit_l0_attn(p):
    blk = p.blk
    QL0 = V_QL0(p)

    @blk.sync
    def _(sy):
        p.joinall(sy)
        if p.taps:
            p.inc(sy.dma_start(out=p.dr["xln_dbg"].ap(),
                               in_=p.sb["U1"][:, :, :].rearrange("p a n -> p (a n)")),
                  "dS", 16)
            p.inc(sy.dma_start(out=p.dr["q_dbg"].ap(),
                               in_=p.sb["QB"][:, :, :].rearrange("p a n -> p (a n)")),
                  "dS", 16)
            p.wait(sy, "cc", p.ccmark["v0"])
            p.inc(sy.dma_start(out=p.dr["k0g_dbg"].ap(), in_=p.dr["k0_g"].ap()),
                  "dS", 16)
            p.inc(sy.dma_start(out=p.dr["v0g_dbg"].ap(), in_=p.dr["v0_g"].ap()),
                  "dS", 16)
            p.wait(sy, "dS")
        p.inc(sy.dma_start(
            out=p.sb["U1"][:, 0:8, :],
            in_=p.dr["mask0"].ap().rearrange("p (a n) -> p a n", n=1024)), "dS", 16)
    c_msk = p.cnt["dS"]

    for kv in range(4):
        @blk.sync
        def _(sy, kv=kv):
            p.wait(sy, "cc", p.ccmark["v0"])
            sy.wait_ge(p.sem["dS"], c_msk)
            if kv > 0:
                p.wait(sy, "vX")
        c_kv = _emit_kv_load(p, "k0_g", "v0_g", 128 * kv, 128 * kv)

        for j in range(8):
            @blk.tensor
            def _(t, c_kv=c_kv, j=j):
                if j == 0:
                    t.wait_ge(p.sem["dS"], c_kv)
            rhs = QL0[:, kv, j, :, :].rearrange("p h n -> p (h n)")
            out = p.sb["OB"][:, 4 * kv:4 * kv + 4, 128 * j:128 * j + 128]
            _emit_attn_group(p, rhs, out, 8 * (j + 1), j, 0, 1.0)


def _emit_proj(p, layer):
    blk = p.blk
    wg = "wproj_g" if layer == 0 else "woutT_g"
    mark = "wproj" if layer == 0 else "woutT"
    OB, x = p.sb["OB"], p.sb["x"]

    if p.taps and layer == 0:
        @blk.sync
        def _(sy):
            p.joinall(sy, ("vX",))
            p.inc(sy.dma_start(out=p.dr["ob_dbg"].ap(),
                               in_=OB[:, :, :].rearrange("p a n -> p (a n)")),
                  "dS", 16)

    WTP = p.sb["WT"][:, :, :, :].rearrange("p b a n -> p (b a n)").rearrange(
        "p (a n) -> p a n", n=256)
    wt_cons = [p.cnt["tS"]]
    ev_cnt = {0: p.cnt["vX"], 1: p.cnt["vX"]}
    for oc in range(16):
        half = oc % 2

        if half == 0:
            @blk.sync
            def _(sy, oc=oc):
                p.wait(sy, "cc", p.ccmark[mark])
                if oc == 0:
                    p.joinall(sy, ("vX", "aE", "tS"))
                p.wait(sy, "tS", wt_cons[0])
                srcp = p.dr[wg].ap()[:, oc * 128:oc * 128 + 256]
                srcp = srcp.rearrange("(a p) n -> p a n", p=128)
                p.inc(sy.dma_start(out=WTP[:, :, :], in_=srcp), "dSa", 16)
            cW = p.cnt["dSa"]
            p._cW_pair = cW
        else:
            cW = p._cW_pair

        @blk.tensor
        def _(t, oc=oc, half=half, cW=cW, e0=ev_cnt[0], e1=ev_cnt[1]):
            t.wait_ge(p.sem["dSa"], cW)
            for tc in range(2):
                ps = p.sb["PS_C"] if tc == 0 else p.sb["PS_D"]
                p.wait(t, "vX", e0 if tc == 0 else e1)
                i = None
                for ic in range(16):
                    i = t.matmul(ps[:, :], WTP[:, ic, half * 128:half * 128 + 128],
                                 OB[:, ic, tc * 512:(tc + 1) * 512],
                                 start=(ic == 0), stop=(ic == 15))
                p.inc(i, "tS", 1)
        if half == 1:
            wt_cons[0] = p.cnt["tS"]
        cM = p.cnt["tS"]

        @blk.vector
        def _(v, oc=oc, cM=cM):
            for tc in range(2):
                v.wait_ge(p.sem["tS"], cM - 1 + tc)
                ps = p.sb["PS_C"] if tc == 0 else p.sb["PS_D"]
                p.inc(v.tensor_tensor(x[:, oc, tc * 512:(tc + 1) * 512], ps[:, :],
                                      x[:, oc, tc * 512:(tc + 1) * 512], ALU.add),
                      "vX", 1)
            ev_cnt[0] = p.cnt["vX"] - 1
            ev_cnt[1] = p.cnt["vX"]



def _emit_l0_ffn(p):
    blk = p.blk
    c_bf0 = p.cnt["vX"]
    _emit_ln_stream(p, "l0f")
    c_ln = p.c_ln_l0f
    XLN = V_XLN(p)
    x = p.sb["x"]
    inter = p.sb["AR1"][:, 0:8192].rearrange("p (a n) -> p a n", n=512)
    sg = p.sb["TMP"][:, :].bitcast(BF16)
    banks = [p.sb["PS_C"], p.sb["PS_D"], p.sb["PS_E"], p.sb["PS_F"]]
    sg_free = [p.cnt["vX"], p.cnt["vX"]]

    for e in range(4):
        rhs_all = XLN[:, :, :].rearrange("p a (n4 f) -> p a n4 f", f=4)
        jobs = []
        for o in range(16):
            jobs.append((o, 0))
            jobs.append((o, 1))
        wt_cons = [p.cnt["tS"], p.cnt["tS"]]
        bank_free = [c_bf0 if e == 0 else p.cnt["vX"]] * 4
        gate_mm = {}
        silu_cnt = {}
        mult_cnt = {}
        for ji, (o, kind) in enumerate(jobs):
            buf = ji % 2
            bank = ji % 4

            @blk.sync
            def _(sy, e=e, o=o, kind=kind, buf=buf):
                p.wait(sy, "cc", p.ccmark["gup"])
                p.wait(sy, "tS", wt_cons[buf])
                srcw = p.dr["gup_g"].ap().rearrange(
                    "(r ee i pp) n -> ee pp r i n", ee=4, i=2, pp=128)[e]
                co = kind * 2048 + o * 128
                sem = "dSa" if buf == 0 else "dSb"
                for k in range(2):
                    p.inc(sy.dma_start(
                        out=p.sb["WT"][:, buf, k:16:2, :],
                        in_=srcw[:, :, k, co:co + 128]), sem, 16)
            cW = p.cnt["dSa" if buf == 0 else "dSb"]

            @blk.tensor
            def _(t, e=e, ji=ji, o=o, kind=kind, buf=buf, bank=bank, cW=cW,
                  bf=bank_free[bank]):
                t.wait_ge(p.sem["dSa" if buf == 0 else "dSb"], cW)
                p.wait(t, "vX", bf)
                if kind == 1:
                    p.wait(t, "aX", silu_cnt.get(o, 0))
                i = None
                for ic in range(16):
                    if e == 0 and ji == 0:
                        t.wait_ge(p.sem["vX"], p.ln_cnts_l0f[ic])
                    i = t.matmul(banks[bank][:, 0:256], p.sb["WT"][:, buf, ic, :],
                                 rhs_all[:, ic, :, e], start=(ic == 0),
                                 stop=(ic == 15))
                p.inc(i, "tS", 1)
            wt_cons[buf] = p.cnt["tS"]
            cM = p.cnt["tS"]
            if kind == 0:
                gate_mm[o] = (cM, bank)
            else:
                gcM, gbank = gate_mm[o]

                sl = sg[:, (o % 2) * 256:(o % 2) * 256 + 256]

                @blk.scalar
                def _(s, o=o, gcM=gcM, gbank=gbank, sl=sl, sf=sg_free[o % 2]):
                    s.wait_ge(p.sem["tS"], gcM)
                    s.wait_ge(p.sem["vX"], mult_cnt[o - 2] if o >= 2 else sf)
                    p.inc(s.activation(sl, banks[gbank][:, 0:256], AF.Silu),
                          "aX", 1)
                silu_cnt[o] = p.cnt["aX"]
                cS = p.cnt["aX"]

                @blk.vector
                def _(v, o=o, cS=cS, cM=cM, bank=bank, sl=sl):
                    v.wait_ge(p.sem["aX"], cS)
                    v.wait_ge(p.sem["tS"], cM)
                    p.inc(v.tensor_tensor(inter[:, o, 0:256], sl,
                                          banks[bank][:, 0:256], ALU.mult),
                          "vX", 1)
                mult_cnt[o] = p.cnt["vX"]
                sg_free[o % 2] = p.cnt["vX"]
                gb_ev = p.cnt["vX"]
                bank_free[gbank] = gb_ev
                bank_free[bank] = gb_ev
        c_int = p.cnt["vX"]

        wt_cons = [p.cnt["tS"], p.cnt["tS"]]
        bank_free = [p.cnt["vX"]] * 2
        for oc in range(16):
            buf = oc % 2
            bank = oc % 2

            @blk.sync
            def _(sy, e=e, oc=oc, buf=buf):
                p.wait(sy, "cc", p.ccmark["down"])
                p.wait(sy, "tS", wt_cons[buf])
                srcw = p.dr["down_g"].ap().rearrange(
                    "(r ee i pp) n -> ee pp r i n", ee=4, i=2, pp=128)[e]
                sem = "dSa" if buf == 0 else "dSb"
                for k in range(2):
                    p.inc(sy.dma_start(
                        out=p.sb["WT"][:, buf, k:16:2, :],
                        in_=srcw[:, :, k, oc * 128:(oc + 1) * 128]), sem, 16)
            cW = p.cnt["dSa" if buf == 0 else "dSb"]

            @blk.tensor
            def _(t, oc=oc, buf=buf, bank=bank, cW=cW, c_int=c_int,
                  bf=bank_free[bank]):
                t.wait_ge(p.sem["dSa" if buf == 0 else "dSb"], cW)
                t.wait_ge(p.sem["vX"], c_int)
                p.wait(t, "vX", bf)
                i = None
                for ic in range(16):
                    i = t.matmul(banks[bank][:, 0:256], p.sb["WT"][:, buf, ic, :],
                                 inter[:, ic, 0:256], start=(ic == 0),
                                 stop=(ic == 15))
                p.inc(i, "tS", 1)
            wt_cons[buf] = p.cnt["tS"]
            cM = p.cnt["tS"]

            @blk.vector
            def _(v, e=e, oc=oc, cM=cM, bank=bank):
                v.wait_ge(p.sem["tS"], cM)
                xo = x[:, :, :].rearrange("p a (n4 f) -> p a n4 f", f=4)[:, oc, :, e]
                p.inc(v.tensor_tensor(xo, banks[bank][:, 0:256], xo, ALU.add),
                      "vX", 1)
            bank_free[bank] = p.cnt["vX"]



def _emit_l1_qkv(p):
    blk = p.blk
    _emit_ln_stream(p, "l1a")
    c_ln = p.c_ln_l1a
    XLN = V_XLN(p)
    QB = p.sb["QB"]

    banks = [p.sb["PS_C"], p.sb["PS_D"], p.sb["PS_E"], p.sb["PS_F"]]
    WTP = p.sb["WT"][:, :, :, :].rearrange("p b a n -> p (b a n)").rearrange(
        "p (a n) -> p a n", n=256)
    wt_cons = [p.cnt["tS"]]
    pair_free = [p.cnt["aX"], p.cnt["aX"]]

    def qkv_chunk(oc, dest_fn, idx):
        half = idx % 2
        pr = idx % 2

        if half == 0:
            @blk.sync
            def _(sy, oc=oc):
                p.wait(sy, "cc", p.ccmark["wqkvT"])
                p.wait(sy, "tS", wt_cons[0])
                srcp = p.dr["wqkvT_g"].ap()[:, oc * 128:oc * 128 + 256]
                srcp = srcp.rearrange("(a p) n -> p a n", p=128)
                p.inc(sy.dma_start(out=WTP[:, :, :], in_=srcp), "dSa", 16)
            p._cW_pair = p.cnt["dSa"]
        cW = p._cW_pair

        @blk.tensor
        def _(t, half=half, cW=cW, pr=pr, pf=pair_free[pr]):
            t.wait_ge(p.sem["dSa"], cW)
            p.wait(t, "aX", pf)
            for tc in range(2):
                ps = banks[pr * 2 + tc]
                i = None
                for ic in range(16):
                    if tc == 0:
                        t.wait_ge(p.sem["vX"], p.ln_cnts_l1a[ic])
                    i = t.matmul(ps[:, :], WTP[:, ic, half * 128:half * 128 + 128],
                                 XLN[:, ic, tc * 512:(tc + 1) * 512],
                                 start=(ic == 0), stop=(ic == 15))
                p.inc(i, "tS", 1)
        if half == 1:
            wt_cons[0] = p.cnt["tS"]
        cM = p.cnt["tS"]

        @blk.scalar
        def _(s, oc=oc, cM=cM, pr=pr):
            for tc in range(2):
                s.wait_ge(p.sem["tS"], cM - 1 + tc)
                ps = banks[pr * 2 + tc]
                p.inc(s.copy(dest_fn(oc, tc), ps[:, :]), "aX", 1)
        pair_free[pr] = p.cnt["aX"]

    idx = 0
    for oc in range(16):
        qkv_chunk(oc, lambda oc, tc: QB[:, oc, tc * 512:(tc + 1) * 512], idx)
        idx += 1
        if idx == 2:
            @blk.vector
            def _(v):
                pass

    for hg in range(4):
        for kk in range(4):
            qkv_chunk(16 + hg * 4 + kk,
                      lambda oc, tc, kk=kk: V_KC(p)[:, kk, tc * 512:(tc + 1) * 512],
                      idx)
            idx += 1
        c_k = p.cnt["aX"]

        @blk.sync
        def _(sy, hg=hg, c_k=c_k):
            p.wait(sy, "aX", c_k)
            p.wait(sy, "tS")
            src = p.dr["wqkvT_g"].ap()[:, 4096 + hg * 512:4096 + (hg + 1) * 512]
            src = src.rearrange("(a p) n -> p a n", p=128)
            for hic in range(2):
                p.inc(sy.dma_start(
                    out=p.sb["AR1"][:, hic * 8192:(hic + 1) * 8192].rearrange(
                        "p (a n) -> p a n", n=512)[:, 0:8, :],
                    in_=src[:, hic * 8:(hic + 1) * 8, :]), "dS", 16)
        cVW = p.cnt["dS"]
        aXb = p.cnt["aX"]

        @blk.tensor
        def _(t, cVW=cVW, aXb=aXb, pf0=pair_free[0], pf1=pair_free[1]):
            t.wait_ge(p.sem["dS"], cVW)
            for tt in range(8):
                ps = p.sb["PS_C"] if tt % 2 == 0 else p.sb["PS_D"]
                if tt >= 2:
                    t.wait_ge(p.sem["aX"], aXb + tt - 1)
                else:
                    t.wait_ge(p.sem["aX"], aXb)
                i = None
                for ic in range(16):
                    wv = p.sb["AR1"][:, (ic // 8) * 8192:(ic // 8 + 1) * 8192]
                    wv = wv.rearrange("p (a n) -> p a n", n=512)
                    i = t.matmul(ps[:, :], XLN[:, ic, tt * 128:(tt + 1) * 128],
                                 wv[:, ic % 8, :], start=(ic == 0), stop=(ic == 15))
                p.inc(i, "tS", 1)
        cVM = p.cnt["tS"]

        @blk.scalar
        def _(s, cVM=cVM):
            for tt in range(8):
                ps = p.sb["PS_C"] if tt % 2 == 0 else p.sb["PS_D"]
                s.wait_ge(p.sem["tS"], cVM - 7 + tt)
                p.inc(s.copy(V_VC(p)[:, tt, :], ps[:, :]), "aX", 1)
        c_vg = p.cnt["aX"]

        @blk.sync
        def _(sy, hg=hg, c_vg=c_vg):
            p.wait(sy, "aX", c_vg)
            p.inc(sy.dma_start(
                out=p.dr[f"k1_c{hg}"].ap().rearrange("(a p) n -> p a n", p=128),
                in_=V_KC(p)[:, :, :]), "dS", 16)
            p.inc(sy.dma_start(
                out=p.dr[f"v1_c{hg}"].ap().rearrange("(a p) n -> p a n", p=128),
                in_=V_VC(p)[:, :, :]), "dS", 16)
        c_ctr = p.cnt["dS"]

        @blk.gpsimd
        def _(g, hg=hg, c_ctr=c_ctr):
            g.wait_ge(p.sem["dS"], c_ctr)
            for nm in (f"k1_c{hg}", f"v1_c{hg}"):
                p.inc(g.collective_compute(
                    "AllGather", ALU.bypass, replica_groups=RG,
                    ins=[p.dr[nm].ap()], outs=[p.dr[nm.replace('_c', '_g')].ap()]),
                    "cc", 1)
            p.ccmark[f"kv1_{hg}"] = p.cnt["cc"]


def _emit_l1_attn(p):
    blk = p.blk
    QB = p.sb["QB"]
    first = True
    for hg in range(4):
        for hh in range(4):
            h = hg * 4 + hh

            @blk.sync
            def _(sy, hg=hg, first=first):
                p.wait(sy, "cc", p.ccmark[f"kv1_{hg}"])
                if first:
                    p.joinall(sy)
                else:
                    p.wait(sy, "vX")
            c_kv = _emit_kv_load(p, f"k1_g{hg}", f"v1_g{hg}", 128 * hh, 128 * hh)

            for qc in range(2):
                @blk.tensor
                def _(t, c_kv=c_kv, qc=qc):
                    if qc == 0:
                        t.wait_ge(p.sem["dS"], c_kv)
                _emit_attn_group(p, QB[:, h, qc * 512:(qc + 1) * 512],
                                 p.sb["OB"][:, h, qc * 512:(qc + 1) * 512],
                                 64, None, 1, SCALE)
            first = False




def _vchain(p, v, ops):
    for f in ops:
        p.inc(f(), "vX", 1)
        v.wait_ge(p.sem["vX"], p.cnt["vX"])


def _emit_l1_ffn(p):
    blk = p.blk
    x = p.sb["x"]
    QB, OB, U1 = p.sb["QB"], p.sb["OB"], p.sb["U1"]
    sm = p.sb["statm"]
    xT = [OB[:, 0:2, :].rearrange("p a n -> p (a n)"),
          OB[:, 2:4, :].rearrange("p a n -> p (a n)")]
    xQ = [OB[:, 4:6, :].rearrange("p a n -> p (a n)"),
          OB[:, 6:8, :].rearrange("p a n -> p (a n)")]
    scr = OB[:, 8:10, :].rearrange("p a n -> p (a n)")
    TMPf = p.sb["TMP"]

    @blk.vector
    def _(v):
        p.joinall(v)
        for ic in range(16):
            p.inc(v.tensor_copy(QB[:, ic, :], x[:, ic, :]), "vX", 1)
    c_xbf = p.cnt["vX"]

    @blk.sync
    def _(sy):
        p.wait(sy, "vX", c_xbf)
        p.inc(sy.dma_start(
            out=p.dr["xbf_d"].ap().rearrange("(a p) n -> p a n", p=128),
            in_=QB[:, :, :]), "dS", 16)
    c_xd = p.cnt["dS"]

    quant_cnt = []
    ship_cnt = []
    for tt in range(8):
        b = tt % 2

        @blk.scalar
        def _(s, tt=tt, b=b):
            s.wait_ge(p.sem["dS"], c_xd)
            if tt >= 2:
                p.wait(s, "vX", quant_cnt[tt - 2])
            p.inc(s.dma_start(out=xT[b][:, :],
                              in_=p.dr["xbf_d"].ap()[:, tt * 128:(tt + 1) * 128],
                              transpose=True), "dTa" if b == 0 else "dTb", 16)
        cT = p.cnt["dTa" if b == 0 else "dTb"]

        @blk.vector
        def _(v, tt=tt, b=b, cT=cT):
            v.wait_ge(p.sem["dTa" if b == 0 else "dTb"], cT)
            v.reduce_sum(sm[:, 0:1], xT[b][:, :], AX)
            v.reduce_max(sm[:, 2:3], xT[b][:, :], AX)
            v.tensor_reduce(sm[:, 3:4], xT[b][:, :], AX, ALU.min)
            p.inc(v.scalar_tensor_tensor(scr[:, 0:2048], xT[b][:, :], 1.0,
                                         xT[b][:, :], ALU.mult, ALU.mult,
                                         accum_out=sm[:, 1:2]), "vX", 1)
            v.wait_ge(p.sem["vX"], p.cnt["vX"])
            _vchain(p, v, [
                lambda: v.tensor_scalar_mul(sm[:, 4:5], sm[:, 0:1], 1.0 / 2048.0),
                lambda: v.tensor_tensor(sm[:, 5:6], sm[:, 4:5], sm[:, 4:5],
                                        ALU.mult),
                lambda: v.tensor_scalar_mul(sm[:, 9:10], sm[:, 1:2], 1.0 / 2048.0),
                lambda: v.tensor_tensor(sm[:, 5:6], sm[:, 9:10], sm[:, 5:6],
                                        ALU.subtract),
            ])
        cV = p.cnt["vX"]

        @blk.scalar
        def _(s, cV=cV):
            s.wait_ge(p.sem["vX"], cV)
            p.inc(s.activation(sm[:, 6:7], sm[:, 5:6], AF.Sqrt,
                               bias=p.sb["cconst"][:, 3:4], scale=1.0 + 1e-6),
                  "aX", 1)
        cS = p.cnt["aX"]

        @blk.vector
        def _(v, tt=tt, b=b, cS=cS):
            v.wait_ge(p.sem["aX"], cS)
            if tt >= 2:
                v.wait_ge(p.sem["dS"], ship_cnt[tt - 2])
            _vchain(p, v, [
                lambda: v.reciprocal(sm[:, 6:7], sm[:, 6:7]),
                lambda: v.tensor_tensor(sm[:, 2:3], sm[:, 2:3], sm[:, 4:5],
                                        ALU.subtract),
                lambda: v.tensor_tensor(sm[:, 3:4], sm[:, 4:5], sm[:, 3:4],
                                        ALU.subtract),
                lambda: v.tensor_tensor(sm[:, 7:8], sm[:, 2:3], sm[:, 3:4],
                                        ALU.max),
                lambda: v.reciprocal(sm[:, 12:13], sm[:, 7:8]),
                lambda: v.tensor_scalar_mul(sm[:, 12:13], sm[:, 12:13], 128.0),
                lambda: v.tensor_tensor(sm[:, 13:14], sm[:, 4:5], sm[:, 12:13],
                                        ALU.mult),
                lambda: v.tensor_scalar_mul(sm[:, 13:14], sm[:, 13:14], -1.0),
                lambda: v.tensor_tensor(sm[:, 14:15], sm[:, 6:7], sm[:, 7:8],
                                        ALU.mult),
                lambda: v.tensor_scalar_mul(sm[:, 14:15], sm[:, 14:15],
                                            1.0 / 128.0),
            ])
            v.tensor_scalar(xQ[b][:, :], xT[b][:, :], sm[:, 12:13], sm[:, 13:14],
                            ALU.mult, ALU.add)
            v.tensor_scalar(xQ[b][:, :], xQ[b][:, :], CLIP, -CLIP, ALU.min, ALU.max)
            p.inc(v.tensor_scalar_mul(xQ[b][:, :], xQ[b][:, :], sm[:, 14:15]),
                  "vX", 1)
        quant_cnt.append(p.cnt["vX"])

        @blk.sync
        def _(sy, tt=tt, b=b):
            p.wait(sy, "vX", quant_cnt[tt])
            if p.taps == 2 and tt == 0:
                p.inc(sy.dma_start(out=p.dr["sm_dbg"].ap(), in_=sm[:, :]), "dS", 16)
                p.inc(sy.dma_start(out=p.dr["xt0_dbg"].ap(), in_=xT[b][:, :]),
                      "dS", 16)
                p.inc(sy.dma_start(
                    out=p.dr["xbf_dbg"].ap().rearrange("(a p) n -> p a n", p=128),
                    in_=QB[:, :, :].rearrange("p a n -> p (a n)").rearrange(
                        "p (a n) -> p a n", n=1024)), "dS", 16)
            p.inc(sy.dma_start(out=p.dr["xhat_d"].ap()[tt * 128:(tt + 1) * 128, :],
                               in_=xQ[b][:, :]), "dS", 16)
        ship_cnt.append(p.cnt["dS"])
    c_xq = p.cnt["dS"]

    @blk.scalar
    def _(s):
        s.wait_ge(p.sem["dS"], c_xq)
        for ic in range(16):
            p.inc(s.dma_start(out=U1[:, ic, :],
                              in_=p.dr["xhat_d"].ap()[:, ic * 128:(ic + 1) * 128],
                              transpose=True), "dT", 16)
    c_xh = p.cnt["dT"]

    if p.taps == 2:
        @blk.sync
        def _(sy):
            p.wait(sy, "cc", p.ccmark["wprojT"])
            p.wait(sy, "dT", c_xh)
            p.inc(sy.dma_start(out=p.dr["gam_dbg"].ap(), in_=p.dr["gam_g"].ap()),
                  "dS", 16)
            p.inc(sy.dma_start(out=p.dr["wfcT_dbg"].ap(), in_=p.dr["wfcT_g"].ap()),
                  "dS", 16)
            p.inc(sy.dma_start(out=p.dr["wprojT_dbg"].ap(),
                               in_=p.dr["wprojT_g"].ap()), "dS", 16)
            p.inc(sy.dma_start(out=p.dr["xhat_dbg"].ap(), in_=p.dr["xhat_d"].ap()),
                  "dS", 16)
            p.wait(sy, "dS")

    hbuf = [p.sb["AR1"][:, 0:8192], p.sb["AR1"][:, 8192:16384]]
    x2fm = QB[:, :, :].rearrange("p a n -> p (a n)").rearrange(
        "p (a n) -> p a n", n=256)
    wst = p.sb["WT"][:, :, :, :].rearrange("p b a n -> p (b a n)").rearrange(
        "p (a n) -> p a n", n=512)
    wtile32 = p.sb["WT"][:, :, :, :].rearrange("p b a n -> p (b a) n")

    for w in range(4):
        wave_v0 = p.cnt["vX"]
        wst2 = [p.sb["WT"][:, 0, :, :].rearrange("p a n -> p (a n)").rearrange(
                    "p (a n) -> p a n", n=512),
                p.sb["WT"][:, 1, :, :].rearrange("p a n -> p (a n)").rearrange(
                    "p (a n) -> p a n", n=512)]
        wt_cons2 = [p.cnt["tS"], p.cnt["tS"]]
        pairfree = {0: p.cnt["aX"], 1: p.cnt["aX"]}
        for fcg in range(16):
            for ich in range(4):
                buf = ich % 2

                @blk.sync
                def _(sy, w=w, fcg=fcg, ich=ich, buf=buf):
                    p.wait(sy, "cc", p.ccmark["wfcT"])
                    p.wait(sy, "tS", wt_cons2[buf])
                    src = p.dr["wfcT_g"].ap()[512 * ich:512 * (ich + 1),
                                              fcg * 512:(fcg + 1) * 512]
                    src = src.rearrange("(a p) n -> p a n", p=128)
                    p.inc(sy.dma_start(out=wst2[buf][:, :, :], in_=src),
                          "dSa" if buf == 0 else "dSb", 16)
                cW = p.cnt["dSa" if buf == 0 else "dSb"]

                @blk.tensor
                def _(t, w=w, fcg=fcg, ich=ich, buf=buf, cW=cW,
                      pf=pairfree[fcg % 2]):
                    t.wait_ge(p.sem["dSa" if buf == 0 else "dSb"], cW)
                    t.wait_ge(p.sem["dT"], c_xh)
                    if ich == 0:
                        p.wait(t, "aX", pf)
                        if fcg < 2:
                            p.wait(t, "vX", wave_v0)
                    i = None
                    for lt in range(2):
                        tt = 2 * w + lt
                        ps = (p.sb["PS_C"], p.sb["PS_D"], p.sb["PS_E"],
                              p.sb["PS_F"])[(fcg % 2) * 2 + lt]
                        for ic4 in range(4):
                            ic = ich * 4 + ic4
                            i = t.matmul(ps[:, :],
                                         U1[:, ic, tt * 128:(tt + 1) * 128],
                                         wst2[buf][:, ic4, :],
                                         start=(ic == 0), stop=(ic == 15))
                    p.inc(i, "tS", 1)
                wt_cons2[buf] = p.cnt["tS"]
            cM = p.cnt["tS"]

            @blk.scalar
            def _(s, w=w, fcg=fcg, cM=cM):
                s.wait_ge(p.sem["tS"], cM)
                for lt in range(2):
                    ps = (p.sb["PS_C"], p.sb["PS_D"], p.sb["PS_E"],
                          p.sb["PS_F"])[(fcg % 2) * 2 + lt]
                    p.inc(s.activation(hbuf[lt][:, fcg * 512:(fcg + 1) * 512],
                                       ps[:, :], AF.Gelu), "aX", 1)
            pairfree[fcg % 2] = p.cnt["aX"]
            cG = p.cnt["aX"]

            @blk.vector
            def _(v, fcg=fcg, cG=cG):
                v.wait_ge(p.sem["aX"], cG)
                for lt in range(2):
                    base = lt * 64
                    hsl = hbuf[lt][:, fcg * 512:(fcg + 1) * 512]
                    v.reduce_sum(TMPf[:, base + fcg:base + fcg + 1], hsl, AX)
                    v.reduce_max(TMPf[:, base + 32 + fcg:base + 33 + fcg],
                                 hsl, AX)
                    v.tensor_reduce(TMPf[:, base + 48 + fcg:base + 49 + fcg],
                                    hsl, AX, ALU.min)
                    p.inc(v.scalar_tensor_tensor(
                        scr[:, 0:512], hsl, 1.0, hsl, ALU.mult, ALU.mult,
                        accum_out=TMPf[:, base + 16 + fcg:base + 17 + fcg]),
                        "vX", 1)
        c_fc = p.cnt["aX"]
        c_part = p.cnt["vX"]

        if p.taps == 2 and w == 0:
            @blk.sync
            def _(sy, c_fc=c_fc):
                sy.wait_ge(p.sem["aX"], c_fc)
                p.inc(sy.dma_start(out=p.dr["h0_dbg"].ap(), in_=hbuf[0][:, :]),
                      "dS", 16)
                p.wait(sy, "dS")

        for lt in range(2):
            hb = hbuf[lt]
            base = lt * 64

            @blk.vector
            def _(v, hb=hb, base=base, c_part=c_part):
                v.wait_ge(p.sem["vX"], c_part)
                _vchain(p, v, [
                    lambda: v.reduce_sum(sm[:, 0:1], TMPf[:, base:base + 16],
                                         AX),
                    lambda: v.reduce_sum(sm[:, 1:2],
                                         TMPf[:, base + 16:base + 32], AX),
                    lambda: v.reduce_max(sm[:, 2:3],
                                         TMPf[:, base + 32:base + 48], AX),
                    lambda: v.tensor_reduce(sm[:, 3:4],
                                            TMPf[:, base + 48:base + 64], AX,
                                            ALU.min),
                    lambda: v.tensor_scalar_mul(sm[:, 4:5], sm[:, 0:1],
                                                1.0 / 8192.0),
                    lambda: v.tensor_tensor(sm[:, 5:6], sm[:, 4:5], sm[:, 4:5],
                                            ALU.mult),
                    lambda: v.tensor_scalar_mul(sm[:, 9:10], sm[:, 1:2],
                                                1.0 / 8192.0),
                    lambda: v.tensor_tensor(sm[:, 5:6], sm[:, 9:10], sm[:, 5:6],
                                            ALU.subtract),
                ])
            cV = p.cnt["vX"]

            @blk.scalar
            def _(s, cV=cV):
                s.wait_ge(p.sem["vX"], cV)
                p.inc(s.activation(sm[:, 6:7], sm[:, 5:6], AF.Sqrt,
                                   bias=p.sb["cconst"][:, 4:5]), "aX", 1)
            cS = p.cnt["aX"]

            @blk.vector
            def _(v, hb=hb, cS=cS):
                v.wait_ge(p.sem["aX"], cS)
                _vchain(p, v, [
                    lambda: v.reciprocal(sm[:, 6:7], sm[:, 6:7]),
                    lambda: v.tensor_tensor(sm[:, 2:3], sm[:, 2:3], sm[:, 4:5],
                                            ALU.subtract),
                    lambda: v.tensor_tensor(sm[:, 3:4], sm[:, 4:5], sm[:, 3:4],
                                            ALU.subtract),
                    lambda: v.tensor_tensor(sm[:, 7:8], sm[:, 2:3], sm[:, 3:4],
                                            ALU.max),
                    lambda: v.reciprocal(sm[:, 12:13], sm[:, 7:8]),
                    lambda: v.tensor_scalar_mul(sm[:, 12:13], sm[:, 12:13], 128.0),
                    lambda: v.tensor_tensor(sm[:, 13:14], sm[:, 4:5],
                                            sm[:, 12:13], ALU.mult),
                    lambda: v.tensor_scalar_mul(sm[:, 13:14], sm[:, 13:14], -1.0),
                    lambda: v.tensor_tensor(sm[:, 14:15], sm[:, 6:7], sm[:, 7:8],
                                            ALU.mult),
                    lambda: v.tensor_scalar_mul(sm[:, 14:15], sm[:, 14:15],
                                                1.0 / 128.0),
                ])
                v.tensor_scalar(hb[:, :], hb[:, :], sm[:, 12:13], sm[:, 13:14],
                                ALU.mult, ALU.add)
                v.tensor_scalar(hb[:, :], hb[:, :], CLIP, -CLIP, ALU.min, ALU.max)
                p.inc(v.tensor_scalar_mul(hb[:, :], hb[:, :], sm[:, 14:15]),
                      "vX", 1)
            cQ2 = p.cnt["vX"]

            @blk.sync
            def _(sy, lt=lt, hb=hb, cQ2=cQ2):
                sy.wait_ge(p.sem["vX"], cQ2)
                p.inc(sy.dma_start(
                    out=p.dr["xh2_d"].ap()[lt * 128:(lt + 1) * 128, :],
                    in_=hb[:, :]), "dS", 16)
        c_ship = p.cnt["dS"]

        if p.taps == 2 and w == 0:
            @blk.sync
            def _(sy, c_ship=c_ship):
                sy.wait_ge(p.sem["dS"], c_ship)
                p.inc(sy.dma_start(out=p.dr["xh2_dbg"].ap(), in_=p.dr["xh2_d"].ap()),
                      "dS", 16)
                p.wait(sy, "dS")

        x2g = []

        @blk.scalar
        def _(s, w=w, c_ship=c_ship, x2g=x2g):
            s.wait_ge(p.sem["dS"], c_ship)
            if w > 0:
                p.wait(s, "tS")
            for g8 in range(8):
                for ic8 in range(8):
                    ic = g8 * 8 + ic8
                    p.inc(s.dma_start(
                        out=x2fm[:, ic, :],
                        in_=p.dr["xh2_d"].ap()[:, ic * 128:(ic + 1) * 128],
                        transpose=True), "dT", 16)
                s.wait_ge(p.sem["dT"], p.cnt["dT"])
                x2g.append(p.cnt["dT"])
        c_x2 = p.cnt["dT"]

        WTH = [p.sb["WT"][:, 0, :, :].rearrange("p a n -> p (a n)").rearrange(
                   "p (a n) -> p a n", n=256),
               p.sb["WT"][:, 1, :, :].rearrange("p a n -> p (a n)").rearrange(
                   "p (a n) -> p a n", n=256)]
        wt_cons3 = [p.cnt["tS"], p.cnt["tS"]]
        bankfree = {0: p.cnt["vX"], 1: p.cnt["vX"]}
        for ocp in range(8):
            for ih in range(8):
                buf = ih % 2

                @blk.sync
                def _(sy, ocp=ocp, ih=ih, buf=buf):
                    p.wait(sy, "cc", p.ccmark["wprojT"])
                    p.wait(sy, "tS", wt_cons3[buf])
                    src = p.dr["wprojT_g"].ap()[1024 * ih:1024 * (ih + 1),
                                                ocp * 256:(ocp + 1) * 256]
                    src = src.rearrange("(a p) n -> p a n", p=128)
                    p.inc(sy.dma_start(out=WTH[buf][:, :, :], in_=src),
                          "dSa" if buf == 0 else "dSb", 16)
                cW = p.cnt["dSa" if buf == 0 else "dSb"]

                if p.taps == 2 and w == 0 and ocp == 0 and ih == 1:
                    @blk.sync
                    def _(sy, cW=cW):
                        sy.wait_ge(p.sem["dT"], c_x2)
                        sy.wait_ge(p.sem["dSa"], p.cnt["dSa"])
                        sy.wait_ge(p.sem["dSb"], p.cnt["dSb"])
                        p.inc(sy.dma_start(
                            out=p.dr["x2fm_dbg"].ap(),
                            in_=QB[:, :, :].rearrange("p a n -> p (a n)")),
                            "dS", 16)
                        p.inc(sy.dma_start(
                            out=p.dr["wth_dbg"].ap(),
                            in_=p.sb["WT"][:, :, :, :].rearrange(
                                "p b a n -> p (b a n)")), "dS", 16)
                        p.wait(sy, "dS")

                @blk.tensor
                def _(t, ocp=ocp, ih=ih, buf=buf, cW=cW, c_ih=x2g[ih],
                      bf0=bankfree[0], bf1=bankfree[1]):
                    t.wait_ge(p.sem["dSa" if buf == 0 else "dSb"], cW)
                    t.wait_ge(p.sem["dT"], c_ih)
                    if p.taps == 2 and w == 0 and ocp == 0 and ih <= 1:
                        p.wait(t, "dS")
                    for sub in range(2):
                        if ih == 0:
                            p.wait(t, "vX", bf0 if sub == 0 else bf1)
                        ps = p.sb["PS_C"] if sub == 0 else p.sb["PS_D"]
                        i = None
                        for ic8 in range(8):
                            ic = ih * 8 + ic8
                            i = t.matmul(ps[:, 0:256],
                                         WTH[buf][:, ic8,
                                                  sub * 128:sub * 128 + 128],
                                         x2fm[:, ic, :], start=(ic == 0),
                                         stop=(ic == 63))
                        p.inc(i, "tS", 1)
                wt_cons3[buf] = p.cnt["tS"]
            cM = p.cnt["tS"]

            @blk.vector
            def _(v, w=w, ocp=ocp, cM=cM):
                for sub in range(2):
                    oc = ocp * 2 + sub
                    v.wait_ge(p.sem["tS"], cM - 1 + sub)
                    ps = p.sb["PS_C"] if sub == 0 else p.sb["PS_D"]
                    xo = x[:, oc, 2 * w * 128:(2 * w + 2) * 128]
                    p.inc(v.tensor_tensor(xo, ps[:, 0:256], xo, ALU.add), "vX", 1)
                bankfree[0] = p.cnt["vX"] - 1
                bankfree[1] = p.cnt["vX"]



MODES = ["full"]


def kernel(**inputs):
    per_core = [make_core_inputs(c, inputs) for c in range(CORES)]
    cur = [pc.pop("x_in") for pc in per_core]
    for mode in MODES:
        nc = build_program(mode)
        in_maps = []
        for c in range(CORES):
            m = dict(per_core[c])
            m["x_in"] = cur[c]
            in_maps.append(m)
        res = run_bass_kernel_spmd(nc, in_maps, core_ids=list(range(CORES)))
        cur = [np.asarray(res.results[c]["x_out"]) for c in range(CORES)]
    return assemble_output(cur)
